# version 03
# baseline (speedup 1.0000x reference)
import math
import numpy as np
import ml_dtypes

import concourse.bass as bass
import concourse.mybir as mybir
from concourse.bass import ts
from concourse.bass_types import DynSlice
from concourse.bass_utils import run_bass_kernel_spmd

F32 = mybir.dt.float32
BF16 = mybir.dt.bfloat16
AF = mybir.ActivationFunctionType
ALU = mybir.AluOpType
AX = mybir.AxisListType
BF16NP = ml_dtypes.bfloat16

NCORES = 8
T = 4096
D = 1024
NTT = T // 128
NDT = D // 128

def _patch_fast_compile():
    try:
        import concourse.bass_utils as bu
        from pathlib import Path
        if getattr(bu, "_fast_compile_patched", False):
            return
        def patched(tmpdir, inp="bir.json", outp="file.neff", arch=None, *, dve_root=None):
            from concourse.bass_utils import (get_walrus_driver, get_walrus_args,
                                              get_bir_arch, run_command, aot_getenv)
            cmd = [
                get_walrus_driver(),
                "--pass",
                ",".join(["birverifier", "runtime_memory_reservation", "lower_act",
                          "lower_dve", "lower_ap_offset", "codegen", "neff_packager"]),
                "-i", inp,
                "--neff-output-filename", outp,
                "--enable-birsim=false",
                "--mem-mode=physical",
                "--policy=0",
                "--enable-ldw-opt=false",
                "--assign-static-dmas-to-sp=false",
                f"--dram-page-size={aot_getenv('NEURON_SCRATCHPAD_PAGE_SIZE', '256')}",
                "--enable-neff-debug-info=true",
                "--jobs", "16",
                *get_walrus_args(get_bir_arch(tmpdir, inp) if arch is None else arch,
                                 tmpdir, dve_root=dve_root),
            ]
            result = run_command(cmd, cwd=tmpdir)
            if result is not None:
                (Path(tmpdir) / "log.txt").write_text(result.stdout)
            return f"{tmpdir}/{outp}"
        bu.bir_verify_and_optimise = patched
        bu._fast_compile_patched = True
    except Exception:
        pass

_patch_fast_compile()


class K:
    ENGINES = ("sync", "gpsimd", "tensor", "vector", "scalar")

    def __init__(self, nc, sems, bufs):
        self.nc = nc
        self.sems = sems
        self.b = bufs
        self.cnt = {s: 0 for s in sems}
        self.eng_name = None
        self.eng = None
        self.dry = False
        self.dry_sems = set()
        self.loop_stack = []
        self.actrs = {}

    def run(self, eng_name, eng, emit_fns):
        self.eng_name = eng_name
        self.eng = eng
        self.cnt = {s: 0 for s in self.sems}
        self.dry = False
        self.dry_sems = set()
        self.loop_stack = []
        self.actrs = {}
        for fn in emit_fns:
            fn(self)

    def op(self, eng_name, fn, wait=(), inc=None, inc_n=1):
        inst = None
        if self.eng_name == eng_name:
            if self.dry:
                for sm, v in wait:
                    self.dry_sems.add(sm)
            else:
                for sm, v in wait:
                    if callable(v):
                        v = v()
                    self.autowait(sm, v)
                inst = fn(self.eng)
                if inc is not None and inst is not None:
                    inst.then_inc(self.sems[inc], inc_n)
        if inc is not None:
            self.cnt[inc] += inc_n
        return inst

    def wait(self, eng_name, sem, value):
        if self.eng_name == eng_name:
            if self.dry:
                self.dry_sems.add(sem)
            else:
                self.autowait(sem, value)

    def autowait(self, sem, value):
        ctr = self.actrs.get(sem)
        if ctr is None:
            self.eng.wait_ge(self.sems[sem], value)
        else:
            ctr.wait(value)

    def autoloop(self, trip, body, engines=("tensor", "vector", "scalar", "gpsimd", "sync")):
        if trip == 0:
            return
        base = dict(self.cnt)
        if self.dry or trip == 1:
            body(0)
            if self.dry:
                for sm in self.cnt:
                    self.cnt[sm] = base[sm] + (self.cnt[sm] - base[sm]) * trip
            return
        top_level = not self.loop_stack
        self.dry = True
        dry_sems_save = self.dry_sems
        self.dry_sems = set()
        body(0)
        delta = {sm: self.cnt[sm] - base[sm] for sm in self.cnt}
        waited = self.dry_sems
        self.dry_sems = dry_sems_save | waited
        self.dry = False
        self.cnt = dict(base)
        created = []
        if top_level:
            for sm in sorted(waited):
                if sm not in self.actrs:
                    c = _AutoCtr(self, sm, base[sm])
                    self.actrs[sm] = c
                    created.append(sm)
        entry_moved = {sm: c._moved for sm, c in self.actrs.items()}
        entry_cur = {sm: c._cur for sm, c in self.actrs.items()}
        cm = None
        it = 0
        if self.eng_name in engines:
            cm = self.eng.Fori(0, trip)
            it = cm.__enter__()
        self.loop_stack.append(trip)
        body(it)
        self.loop_stack.pop()
        loops_here = self.eng_name in engines
        for sm, c in self.actrs.items():
            moved = c._moved - entry_moved[sm]
            want = delta.get(sm, 0) if loops_here else trip * delta.get(sm, 0)
            corr = want - moved
            if corr and self.eng_name == c.eng_name:
                self.eng.reg_add(c.reg, c.reg, corr)
            c._moved = entry_moved[sm] + trip * delta.get(sm, 0)
            c._cur = entry_cur[sm] + trip * delta.get(sm, 0)
        if cm is not None:
            cm.__exit__(None, None, None)
        for sm in self.cnt:
            self.cnt[sm] = base[sm] + (self.cnt[sm] - base[sm]) * trip
        if top_level:
            for sm in created:
                self.actrs[sm]._free()
                del self.actrs[sm]

    def fori(self, trip, engines=None):
        return _Fori(self, trip, engines or ("tensor", "vector", "scalar", "gpsimd", "sync"))

    def iterctr(self, fori_obj, eng_name, sem_name, base, per_iter):
        c = _IterCtr(self, eng_name, sem_name, per_iter, base)
        fori_obj.ctrs.append(c)
        return c


class _AutoCtr:
    def __init__(self, k, sem, base):
        self.k = k
        self.eng_name = k.eng_name
        self.sem = sem
        self._cur = base
        self._moved = 0
        self.reg = k.eng.alloc_register()
        k.eng.reg_mov(self.reg, base)

    def wait(self, value):
        d = value - self._cur
        if d < 0:
            import traceback
            raise AssertionError(
                f"autoctr {self.sem} eng={self.eng_name}: wait target decreased "
                f"({value} < {self._cur})\n" + "".join(traceback.format_stack()[-6:-1]))
        if d:
            self.k.eng.reg_add(self.reg, self.reg, d)
            self._cur = value
            self._moved += d
        self.k.eng.wait_ge(self.k.sems[self.sem], self.reg)

    def _free(self):
        self.k.eng.free_register(self.reg)


class _IterCtr:
    def __init__(self, k, eng_name, sem_name, per_iter, base):
        self.k = k
        self.eng_name = eng_name
        self.sem = sem_name
        self.per_iter = per_iter
        self.base = base
        self.reg = None
        self.first_off = None
        self.last_off = None

    def wait(self, off):
        if self.k.eng_name != self.eng_name:
            return
        eng = self.k.eng
        if self.reg is None:
            raise RuntimeError("iterctr used before loop entry init")
        if self.first_off is None:
            self.first_off = off
            if off != 0:
                eng.reg_add(self.reg, self.reg, off)
        else:
            d = off - self.last_off
            assert d >= 0, "iterctr offsets must be nondecreasing"
            if d:
                eng.reg_add(self.reg, self.reg, d)
        self.last_off = off
        eng.wait_ge(self.k.sems[self.sem], self.reg)

    def _init(self):
        if self.k.eng_name == self.eng_name:
            self.reg = self.k.eng.alloc_register()
            self.k.eng.reg_mov(self.reg, self.base)

    def _advance(self):
        if self.k.eng_name == self.eng_name and self.last_off is not None:
            d = self.per_iter - self.last_off
            if d:
                self.k.eng.reg_add(self.reg, self.reg, d)

    def _free(self):
        if self.reg is not None:
            self.k.eng.free_register(self.reg)
            self.reg = None


class _Fori:
    def __init__(self, k, trip, engines):
        self.k = k
        self.trip = trip
        self.engines = engines
        self.cm = None
        self.ctrs = []
        self.entered = False

    def __enter__(self):
        self.cnt0 = dict(self.k.cnt)
        return self

    def begin(self):
        for c in self.ctrs:
            c._init()
        self.entered = True
        if self.k.eng_name in self.engines and self.trip > 1:
            self.cm = self.k.eng.Fori(0, self.trip)
            return self.cm.__enter__()
        return 0

    def __exit__(self, *a):
        if self.entered:
            for c in self.ctrs:
                c._advance()
        if self.cm is not None:
            self.cm.__exit__(*a)
        for c in self.ctrs:
            c._free()
        if self.trip > 1:
            for s in self.k.cnt:
                d = self.k.cnt[s] - self.cnt0[s]
                self.k.cnt[s] = self.cnt0[s] + d * self.trip
        return False


def ceil_div(a, b):
    return (a + b - 1) // b


class Mamba:
    C = 512

    def __init__(self, nc, E, pfx, T):
        self.T = T
        self.NCH = T // self.C
        C = self.C
        s = self.sb = {}
        s["Wst"] = E(nc.sbuf_tensor(pfx + "Wst", [128, 8 * 1024], BF16))
        s["Wxp"] = E(nc.sbuf_tensor(pfx + "Wxp", [128, 8 * 32], BF16))
        s["A"] = E(nc.sbuf_tensor(pfx + "A", [128, 128], F32))
        s["wconv"] = E(nc.sbuf_tensor(pfx + "wconv", [128, 32], F32))
        s["bconv"] = E(nc.sbuf_tensor(pfx + "bconv", [128, 8], F32))
        s["bdt"] = E(nc.sbuf_tensor(pfx + "bdt", [128, 8], F32))
        s["gam"] = E(nc.sbuf_tensor(pfx + "gam", [128, 8], F32))
        s["beta"] = E(nc.sbuf_tensor(pfx + "beta", [128, 8], F32))
        s["beta_bf"] = E(nc.sbuf_tensor(pfx + "beta_bf", [128, 8], BF16))
        s["poolD"] = E(nc.sbuf_tensor(pfx + "poolD", [128, 2048], BF16))
        s["states"] = E(nc.sbuf_tensor(pfx + "states", [128, 128], F32))
        s["xT"] = E(nc.sbuf_tensor(pfx + "xT", [128, 8 * C], BF16))
        s["xstg"] = E(nc.sbuf_tensor(pfx + "xstg", [128, 4 * 1024], BF16))
        s["xin"] = E(nc.sbuf_tensor(pfx + "xin", [128, 8 * (C + 3)], BF16))
        s["zg"] = E(nc.sbuf_tensor(pfx + "zg", [128, 8 * C], BF16))
        s["delta"] = E(nc.sbuf_tensor(pfx + "delta", [128, 8 * C], F32))
        s["dg"] = E(nc.sbuf_tensor(pfx + "dg", [128, 8 * C], BF16))
        s["BC"] = E(nc.sbuf_tensor(pfx + "BC", [32, C], BF16))
        s["Bbc"] = E(nc.sbuf_tensor(pfx + "Bbc", [128, 16 * C], BF16))
        s["Cbc"] = E(nc.sbuf_tensor(pfx + "Cbc", [128, 16 * C], BF16))
        s["yT"] = E(nc.sbuf_tensor(pfx + "yT", [128, 8 * C], BF16))
        s["da"] = E(nc.sbuf_tensor(pfx + "da", [128, C], F32))
        s["A16"] = E(nc.sbuf_tensor(pfx + "A16", [128, 16], F32))
        s["epsv"] = E(nc.sbuf_tensor(pfx + "epsv", [128, 1], F32))
        s["st16"] = E(nc.sbuf_tensor(pfx + "st16", [128, 16], F32))
        s["u"] = E(nc.sbuf_tensor(pfx + "u", [128, C], F32))
        s["h"] = E(nc.sbuf_tensor(pfx + "h", [128, C], F32))
        s["hc"] = E(nc.sbuf_tensor(pfx + "hc", [128, C], BF16))
        s["poolA"] = E(nc.sbuf_tensor(pfx + "poolA", [128, 4 * C], F32))
        s["poolC"] = E(nc.sbuf_tensor(pfx + "poolC", [128, 2 * C], BF16))
        s["pstat"] = E(nc.sbuf_tensor(pfx + "pstat", [128, 8], F32))
        s["s1r"] = s["poolA"][0:1, 0:C]
        s["s2r"] = s["poolA"][0:1, C:2 * C]
        s["mur"] = s["poolA"][0:1, 2 * C:3 * C]
        s["varr"] = s["poolA"][0:1, 3 * C:4 * C]
        s["rbf"] = s["poolC"][0:1, 0:C]
        s["mrbf"] = s["poolC"][0:1, C:2 * C]
        s["bprime"] = s["poolD"][0:1, 0:1024]
        s["bout_bf"] = s["poolD"][0:1, 1024:2048]
        s["scr2"] = s["da"][:]
        self.ps = [E(nc.psum_tensor(pfx + f"ps{i}", [128, 512], F32)) for i in range(3)]
        self.ps_tr = E(nc.psum_tensor(pfx + "ps_tr", [128, 512], BF16))
        self.ps_y = E(nc.psum_tensor(pfx + "ps_y", [128, 512], F32))
        self.ps_bc = E(nc.psum_tensor(pfx + "ps_bc", [32, 512], F32))
        self.ps_r1 = E(nc.psum_tensor(pfx + "ps_r1", [1, 512], F32))
        self.ps_r2 = E(nc.psum_tensor(pfx + "ps_r2", [1, 512], F32))

    def emit_prologue(self, k, io, cst):
        s = self.sb
        k.op("gpsimd", lambda g: g.dma_start(
            out=s["bout_bf"], in_=io["bout"]), inc="g_ld", inc_n=16)
        k.op("gpsimd", lambda g: g.dma_start(
            out=s["Wxp"][:].rearrange("p (kt e) -> p kt e", kt=8),
            in_=io["Wxp"].rearrange("(kt p) e -> p kt e", p=128)), inc="g_ld", inc_n=16)
        for snm, inm in (("wconv", "wconv"), ("bconv", "bconv"), ("bdt", "bdt"),
                         ("gam", "gamma"), ("beta", "beta")):
            k.op("sync", lambda sy, snm=snm, inm=inm: sy.dma_start(out=s[snm][:], in_=io[inm]),
                 inc="m_ld", inc_n=16)
        k.op("sync", lambda sy: sy.dma_start(
            out=s["A"][:].rearrange("p (kt n) -> p kt n", kt=8),
            in_=io["A_log"].rearrange("(kt p) n -> p kt n", p=128)), inc="m_ld", inc_n=16)
        nld = k.cnt["m_ld"]
        k.op("scalar", lambda sc: sc.activation(s["A"][:], s["A"][:], AF.Exp),
             wait=[("m_ld", nld)], inc="m_s")
        k.op("vector", lambda v: v.tensor_scalar_mul(s["A"][:], s["A"][:], -1.0),
             wait=[("m_s", k.cnt["m_s"])], inc="m_v")
        k.op("vector", lambda v: v.tensor_copy(s["beta_bf"][:], s["beta"][:]),
             wait=[("m_ld", nld)], inc="m_v")
        k.op("vector", lambda v: v.memset(s["xin"][:], 0.0), inc="m_v")
        k.op("vector", lambda v: v.memset(s["epsv"][:], 1e-5), inc="m_v")
        k.op("vector", lambda v: v.memset(s["states"][:], 0.0), inc="m_v")
        self.prologue_v = k.cnt["m_v"]
        self.prologue_s = k.cnt["m_s"]
        self.lds = nld
        self.glds = k.cnt["g_ld"]

    def emit(self, k, io, cst, x_tm):
        self.emit_prologue(k, io, cst)
        for e in ("tensor", "vector", "scalar", "gpsimd"):
            k.wait(e, "m_ld", self.lds)
            k.wait(e, "g_ld", self.glds)
        self.emit_chunk(k, io, cst, x_tm, 0, first=True)
        if self.NCH > 1:
            k.autoloop(self.NCH - 1,
                       lambda ch: self.emit_chunk(k, io, cst, x_tm, ch + 1, first=False))

    def emit_chunk(self, k, io, cst, x_tm, ch, first=False):
        import os
        stop = os.environ.get("MAMBA_STOP", "Z")
        s = self.sb
        C = self.C
        ident = cst["ident_bf"]
        xoff = ch * (4 * 1024)

        k.op("gpsimd", lambda g: g.dma_start(
            out=s["Wst"][:].rearrange("p (kt e) -> p kt e", kt=8),
            in_=io["Win"][:, 0:1024].rearrange("(kt p) e -> p kt e", p=128)),
            wait=[("m_t", k.cnt["m_t"])], inc="m_wl", inc_n=16)
        wina_ready = k.cnt["m_wl"]

        k.op("vector", lambda v: v.tensor_copy(
            s["xstg"][:], x_tm[:, DynSlice(xoff, 4 * 1024)]), inc="m_v")
        stg_v = k.cnt["m_v"]
        for dt in range(8):
            w = [("m_v", k.cnt["m_v"] if dt > 0 else stg_v)]
            for tt4 in range(4):
                inst = k.op("tensor", lambda t, dt=dt, tt4=tt4: t.transpose(
                    self.ps_tr[:, tt4 * 128:(tt4 + 1) * 128],
                    s["xstg"][:, tt4 * 1024 + dt * 128: tt4 * 1024 + (dt + 1) * 128],
                    ident[:]),
                    wait=(w if tt4 == 0 else []),
                    inc=("m_t" if tt4 == 3 else None))
            k.op("vector", lambda v, dt=dt: v.tensor_copy(
                s["xT"][:, dt * C:(dt + 1) * C], self.ps_tr[:, 0:C]),
                wait=[("m_t", k.cnt["m_t"])], inc="m_v")
        if stop <= "A":
            return
        evb = k.cnt["m_v"]
        for half in range(2):
            if half == 1:
                k.op("gpsimd", lambda g: g.dma_start(
                    out=s["Wst"][:].rearrange("p (kt e) -> p kt e", kt=8),
                    in_=io["Win"][:, 1024:2048].rearrange("(kt p) e -> p kt e", p=128)),
                    wait=[("m_t", k.cnt["m_t"])], inc="m_wl", inc_n=16)
            half_ready = k.cnt["m_wl"]
            for et8 in range(8):
                et = half * 8 + et8
                bank = self.ps[et % 3]
                w = [("m_v", evb + et - 2)] if et >= 3 else []
                if et == 0:
                    w = [("m_v", evb), ("m_wl", wina_ready), ("m_s", k.cnt["m_s"])]
                if et8 == 0 and half == 1:
                    w = w + [("m_wl", half_ready)]
                for kt in range(8):
                    k.op("tensor", lambda t, et8=et8, kt=kt, bank=bank: t.matmul(
                        bank[:, 0:512],
                        s["Wst"][:, kt * 1024 + et8 * 128: kt * 1024 + (et8 + 1) * 128],
                        s["xT"][:, kt * C:(kt + 1) * C],
                        start=(kt == 0), stop=(kt == 7)),
                        wait=(w if kt == 0 else []),
                        inc=("m_t" if kt == 7 else None))
                mmv = k.cnt["m_t"]
                if half == 0:
                    k.op("vector", lambda v, et8=et8, bank=bank: v.tensor_copy(
                        s["xin"][:, et8 * (C + 3) + 3: et8 * (C + 3) + 3 + C], bank[:, 0:512]),
                        wait=[("m_t", mmv)], inc="m_v")
                else:
                    k.op("vector", lambda v, et8=et8, bank=bank: v.tensor_copy(
                        s["zg"][:, et8 * C:(et8 + 1) * C], bank[:, 0:512]),
                        wait=[("m_t", mmv)], inc="m_v")
        k.op("gpsimd", lambda g: g.dma_start(
            out=s["Wst"][:].rearrange("p (kt e) -> p kt e", kt=8),
            in_=io["Wdt"].rearrange("(kt p) e -> p kt e", p=128)),
            wait=[("m_t", k.cnt["m_t"])], inc="m_wl", inc_n=16)
        wdt_ready = k.cnt["m_wl"]
        if stop <= "B":
            return
        for dt in range(8):
            base = dt * (C + 3)
            k.op("vector", lambda v, dt=dt, base=base: v.tensor_scalar(
                s["delta"][:, dt * C:(dt + 1) * C], s["xin"][:, base: base + C],
                s["wconv"][:, dt * 4: dt * 4 + 1], None, op0=ALU.mult), inc="m_v")
            for j in range(1, 4):
                k.op("vector", lambda v, dt=dt, base=base, j=j: v.scalar_tensor_tensor(
                    s["delta"][:, dt * C:(dt + 1) * C], s["xin"][:, base + j: base + j + C],
                    s["wconv"][:, dt * 4 + j: dt * 4 + j + 1],
                    s["delta"][:, dt * C:(dt + 1) * C],
                    op0=ALU.mult, op1=ALU.add), inc="m_v")
            k.op("vector", lambda v, dt=dt, base=base: v.tensor_copy(
                s["xin"][:, base: base + 3], s["xin"][:, base + C: base + C + 3]), inc="m_v")
        conv_done_v = k.cnt["m_v"]
        if stop <= "C":
            return
        for dt in range(8):
            w = [("m_v", conv_done_v if dt == 0 else k.cnt["m_v"])]
            k.op("scalar", lambda sc, dt=dt: sc.activation(
                s["scr2"], s["delta"][:, dt * C:(dt + 1) * C], AF.Silu,
                bias=s["bconv"][:, dt: dt + 1]), wait=w, inc="m_s")
            k.op("vector", lambda v, dt=dt: v.tensor_tensor(
                s["zg"][:, dt * C:(dt + 1) * C], s["scr2"],
                s["zg"][:, dt * C:(dt + 1) * C], op=ALU.mult),
                wait=[("m_s", k.cnt["m_s"])], inc="m_v")
        if stop <= "D":
            return
        gate_v = k.cnt["m_v"]
        for mt in range(8):
            bank = self.ps[mt % 2]
            w = ([("m_s", k.cnt["m_s"] - 1)] if mt >= 2
                 else [("m_wl", wdt_ready), ("m_v", gate_v)])
            for kt in range(8):
                k.op("tensor", lambda t, mt=mt, kt=kt, bank=bank: t.matmul(
                    bank[:, 0:512],
                    s["Wst"][:, kt * 1024 + mt * 128: kt * 1024 + (mt + 1) * 128],
                    s["zg"][:, kt * C:(kt + 1) * C],
                    start=(kt == 0), stop=(kt == 7)),
                    wait=(w if kt == 0 else []),
                    inc=("m_t" if kt == 7 else None))
            k.op("scalar", lambda sc, mt=mt, bank=bank: sc.activation(
                s["delta"][:, mt * C:(mt + 1) * C], bank[:, 0:512], AF.Identity,
                bias=s["bdt"][:, mt: mt + 1]),
                wait=[("m_t", k.cnt["m_t"])], inc="m_s")
        sp_s0 = k.cnt["m_s"]
        for mt in range(8):
            dslice = s["delta"][:, mt * C:(mt + 1) * C]
            k.op("vector", lambda v, dslice=dslice: v.scalar_tensor_tensor(
                s["scr2"], dslice, -1.0, dslice, op0=ALU.mult, op1=ALU.max),
                wait=[("m_s", k.cnt["m_s"])], inc="m_v")
            k.op("scalar", lambda sc: sc.activation(s["scr2"], s["scr2"], AF.Exp,
                                                    scale=-1.0),
                 wait=[("m_v", k.cnt["m_v"])], inc="m_s")
            k.op("scalar", lambda sc: sc.activation(s["scr2"], s["scr2"], AF.Ln,
                                                    bias=1.0), inc="m_s")
            k.op("vector", lambda v, dslice=dslice: v.scalar_tensor_tensor(
                dslice, dslice, 0.0, s["scr2"], op0=ALU.max, op1=ALU.add),
                wait=[("m_s", k.cnt["m_s"])], inc="m_v")
        if stop <= "E":
            return
        k.op("sync", lambda sy: sy.dma_start(
            out=s["Wst"][0:32, 0:4096], in_=cst["sel32_hbm"]),
            wait=[("m_t", k.cnt["m_t"])], inc="m_sl", inc_n=16)
        sel_ready = k.cnt["m_sl"]
        for kt in range(8):
            k.op("tensor", lambda t, kt=kt: t.matmul(
                self.ps_bc[:, 0:512],
                s["Wxp"][:, kt * 32:(kt + 1) * 32],
                s["zg"][:, kt * C:(kt + 1) * C],
                start=(kt == 0), stop=(kt == 7)),
                wait=([("m_v", k.cnt["m_v"])] if kt == 0 else []),
                inc=("m_t" if kt == 7 else None))
        k.op("vector", lambda v: v.tensor_copy(s["BC"][:], self.ps_bc[:, 0:C]),
             wait=[("m_t", k.cnt["m_t"])], inc="m_v")
        for dt in range(8):
            k.op("vector", lambda v, dt=dt: v.tensor_tensor(
                s["dg"][:, dt * C:(dt + 1) * C], s["delta"][:, dt * C:(dt + 1) * C],
                s["zg"][:, dt * C:(dt + 1) * C], op=ALU.mult), inc="m_v")
        if stop <= "F":
            return
        bc_v = k.cnt["m_v"]
        for n in range(32):
            bank = self.ps[n % 2]
            tgt = s["Bbc"] if n < 16 else s["Cbc"]
            col = (n % 16) * C
            w = [("m_v", bc_v + n - 1)] if n >= 2 else [("m_v", bc_v)]
            if n < 2:
                w = w + [("m_sl", sel_ready)]
            k.op("tensor", lambda t, n=n, bank=bank, w=w: t.matmul(
                bank[:, 0:512], s["Wst"][0:32, n * 128:(n + 1) * 128],
                s["BC"][:, 0:C], start=True, stop=True),
                wait=w, inc="m_t")
            k.op("vector", lambda v, bank=bank, tgt=tgt, col=col: v.tensor_copy(
                tgt[:, col: col + C], bank[:, 0:512]),
                wait=[("m_t", k.cnt["m_t"])], inc="m_v")
        if stop <= "G":
            return
        k.op("gpsimd", lambda g: g.dma_start(
            out=s["Wst"][:].rearrange("p (kt e) -> p kt e", kt=8),
            in_=io["Wout"].rearrange("(kt p) e -> p kt e", p=128)),
            wait=[("m_s", k.cnt["m_s"]), ("m_t", k.cnt["m_t"])],
            inc="m_wl", inc_n=16)
        wout_ready = k.cnt["m_wl"]
        k.wait("vector", "m_v", k.cnt["m_v"])

        def scan_body(dt):
            dcol = dt * C
            d16 = dt * 16
            sl_c = DynSlice(dcol, C)
            sl_16 = DynSlice(d16, 16)
            ev0 = k.cnt["m_e"]
            k.op("vector", lambda v: v.tensor_copy(
                s["A16"][:], s["A"][:, sl_16]), inc="m_v")
            k.op("vector", lambda v: v.tensor_copy(
                s["st16"][:], s["states"][:, sl_16]), inc="m_v")
            prev_scan_v = k.cnt["m_v"]
            for n in range(16):
                k.op("scalar", lambda sc, n=n: sc.activation(
                    s["da"][:], s["delta"][:, sl_c], AF.Exp,
                    scale=s["A16"][:, n:n + 1]),
                    wait=[("m_v", prev_scan_v)], inc="m_s")
                k.op("vector", lambda v, n=n: v.tensor_tensor(
                    s["u"][:], s["dg"][:, sl_c], s["Bbc"][:, n * C:(n + 1) * C],
                    op=ALU.mult), inc="m_v")
                k.op("vector", lambda v, n=n: v.tensor_tensor_scan(
                    s["h"][:], s["da"][:], s["u"][:],
                    s["st16"][:, n:n + 1], op0=ALU.mult, op1=ALU.add),
                    wait=[("m_s", k.cnt["m_s"])], inc="m_v")
                prev_scan_v = k.cnt["m_v"]
                k.op("vector", lambda v, n=n: v.tensor_copy(
                    s["st16"][:, n:n + 1], s["h"][:, C - 1:C]), inc="m_v")
                k.op("vector", lambda v, n=n: v.tensor_tensor(
                    s["hc"][:], s["h"][:], s["Cbc"][:, n * C:(n + 1) * C],
                    op=ALU.mult),
                    wait=[("m_t", k.cnt["m_t"])], inc="m_v")
                k.op("tensor", lambda t, n=n: t.matmul(
                    self.ps_y[:, 0:512], ident[:], s["hc"][:],
                    start=(n == 0), stop=(n == 15)),
                    wait=[("m_v", k.cnt["m_v"])] + ([("m_e", ev0)] if n == 0 else []),
                    inc="m_t")
            k.op("vector", lambda v: v.tensor_copy(
                s["states"][:, sl_16], s["st16"][:]), inc="m_v")
            k.op("vector", lambda v: v.tensor_tensor(
                s["yT"][:, sl_c], self.ps_y[:, 0:512], s["xT"][:, sl_c],
                op=ALU.add),
                wait=[("m_t", k.cnt["m_t"])], inc="m_e")

        k.autoloop(8, scan_body, engines=("tensor", "vector", "scalar"))
        k.wait("vector", "m_wl", wout_ready)
        if first:
            k.wait("tensor", "m_wl", wout_ready)
            for dt in range(8):
                k.op("tensor", lambda t, dt=dt: t.matmul(
                    self.ps_r1[:, 0:512], s["beta_bf"][:, dt: dt + 1],
                    s["Wst"][:, dt * 1024: dt * 1024 + 512],
                    start=(dt == 0), stop=False),
                    wait=([("m_v", k.cnt["m_v"]), ("m_e", k.cnt["m_e"])] if dt == 0 else []),
                    inc=None)
                k.op("tensor", lambda t, dt=dt: t.matmul(
                    self.ps_r2[:, 0:512], s["beta_bf"][:, dt: dt + 1],
                    s["Wst"][:, dt * 1024 + 512: dt * 1024 + 1024],
                    start=(dt == 0), stop=False), inc=None)
            k.op("tensor", lambda t: t.matmul(
                self.ps_r1[:, 0:512], cst["ones1x128"][0:1, 0:1],
                s["bout_bf"][:, 0:512], start=False, stop=True), inc="m_t")
            k.op("tensor", lambda t: t.matmul(
                self.ps_r2[:, 0:512], cst["ones1x128"][0:1, 0:1],
                s["bout_bf"][:, 512:1024], start=False, stop=True), inc="m_t")
            k.op("vector", lambda v: v.tensor_copy(s["bprime"][:, 0:512], self.ps_r1[0:1, 0:512]),
                 wait=[("m_t", k.cnt["m_t"])], inc="m_v")
            k.op("vector", lambda v: v.tensor_copy(s["bprime"][:, 512:1024], self.ps_r2[0:1, 0:512]),
                 inc="m_v")
        bp_t = k.cnt["m_t"]
        for dt in range(8):
            k.op("vector", lambda v, dt=dt: v.tensor_scalar(
                s["Wst"][:, dt * 1024:(dt + 1) * 1024], s["Wst"][:, dt * 1024:(dt + 1) * 1024],
                s["gam"][:, dt: dt + 1], None, op0=ALU.mult),
                wait=([("m_t", bp_t)] if dt == 0 else []), inc="m_v")
        iln_v0 = k.cnt["m_v"]
        iln_s0 = k.cnt["m_s"]
        for dt in range(8):
            sqbuf = (s["xin"][:, 3:3 + C] if dt % 2 == 0
                     else s["xin"][:, (C + 3) + 3:(C + 3) + 3 + C])
            w = [("m_e", k.cnt["m_e"])] if dt == 0 else []
            if dt >= 2:
                w = [("m_t", k.cnt["m_t"] - 1)]
            k.op("scalar", lambda sc, dt=dt, sqbuf=sqbuf: sc.activation(
                sqbuf, s["yT"][:, dt * C:(dt + 1) * C], AF.Square),
                wait=w, inc="m_s")
            k.op("tensor", lambda t, dt=dt: t.matmul(
                self.ps_r1[:, 0:512], cst["ones128col_bf"][:, 0:1],
                s["yT"][:, dt * C:(dt + 1) * C],
                start=(dt == 0), stop=(dt == 7)),
                wait=([("m_e", k.cnt["m_e"]), ("m_v", k.cnt["m_v"])] if dt == 0 else []),
                inc=None)
            k.op("tensor", lambda t, dt=dt, sqbuf=sqbuf: t.matmul(
                self.ps_r2[:, 0:512], cst["ones128col_bf"][:, 0:1],
                sqbuf,
                start=(dt == 0), stop=(dt == 7)),
                wait=[("m_s", k.cnt["m_s"])],
                inc="m_t")
        sums_t = k.cnt["m_t"]
        k.op("vector", lambda v: v.tensor_copy(s["s1r"], self.ps_r1[0:1, 0:C]),
             wait=[("m_t", sums_t)], inc="m_v")
        k.op("vector", lambda v: v.tensor_copy(s["s2r"], self.ps_r2[0:1, 0:C]),
             inc="m_v")
        k.op("vector", lambda v: v.tensor_scalar_mul(s["mur"], s["s1r"], 1.0 / 1024.0),
             wait=[("m_v", k.cnt["m_v"])], inc="m_v")
        k.op("vector", lambda v: v.tensor_tensor(s["varr"], s["mur"],
                                                 s["mur"], op=ALU.mult),
             wait=[("m_v", k.cnt["m_v"])], inc="m_v")
        k.op("vector", lambda v: v.scalar_tensor_tensor(
            s["varr"], s["s2r"], 1.0 / 1024.0, s["varr"],
            op0=ALU.mult, op1=ALU.subtract), inc="m_v")
        k.op("scalar", lambda sc: sc.activation(s["varr"], s["varr"],
                                                AF.Ln, bias=s["epsv"][0:1, 0:1]),
             wait=[("m_v", k.cnt["m_v"])], inc="m_s")
        k.op("scalar", lambda sc: sc.activation(s["varr"], s["varr"],
                                                AF.Exp, scale=-0.5),
             wait=[("m_s", k.cnt["m_s"])], inc="m_s")
        k.op("vector", lambda v: v.tensor_copy(s["rbf"], s["varr"]),
             wait=[("m_s", k.cnt["m_s"])], inc="m_v")
        k.op("vector", lambda v: v.tensor_tensor(s["mur"], s["mur"],
                                                 s["varr"], op=ALU.mult), inc="m_v")
        k.op("vector", lambda v: v.tensor_copy(s["mrbf"], s["mur"]),
             wait=[("m_v", k.cnt["m_v"])], inc="m_v")
        k.op("tensor", lambda t: t.matmul(
            self.ps[0][:, 0:512], cst["ones1x128"][0:1, :], s["rbf"],
            start=True, stop=True), wait=[("m_v", k.cnt["m_v"])], inc="m_t")
        k.op("tensor", lambda t: t.matmul(
            self.ps[1][:, 0:512], cst["ones1x128"][0:1, :], s["mrbf"],
            start=True, stop=True), inc="m_t")
        k.op("vector", lambda v: v.tensor_copy(s["Bbc"][:, 0:C], self.ps[0][:, 0:512]),
             wait=[("m_t", k.cnt["m_t"])], inc="m_v")
        k.op("vector", lambda v: v.tensor_copy(s["Bbc"][:, C:2 * C], self.ps[1][:, 0:512]),
             inc="m_v")
        for dt in range(8):
            k.op("vector", lambda v, dt=dt: v.tensor_tensor(
                s["u"][:], s["yT"][:, dt * C:(dt + 1) * C], s["Bbc"][:, 0:C], op=ALU.mult),
                inc="m_v")
            k.op("vector", lambda v, dt=dt: v.tensor_tensor(
                s["xT"][:, dt * C:(dt + 1) * C], s["u"][:], s["Bbc"][:, C:2 * C],
                op=ALU.subtract), inc="m_v")
        apply_v = k.cnt["m_v"]
        if stop <= "I":
            return
        for tt4 in range(4):
            pa, pb = self.ps[0], self.ps[1]
            w0 = [("m_v", apply_v)] if tt4 == 0 else [("m_s", k.cnt["m_s"])]
            for nch in range(2):
                bank = pa if nch == 0 else pb
                for dt in range(8):
                    k.op("tensor", lambda t, tt4=tt4, nch=nch, dt=dt, bank=bank: t.matmul(
                        bank[:, 0:512],
                        s["xT"][:, dt * C + tt4 * 128: dt * C + (tt4 + 1) * 128],
                        s["Wst"][:, dt * 1024 + nch * 512: dt * 1024 + (nch + 1) * 512],
                        start=(dt == 0), stop=False),
                        wait=(w0 if (nch == 0 and dt == 0) else []), inc=None)
                k.op("tensor", lambda t, tt4=tt4, nch=nch, bank=bank: t.matmul(
                    bank[:, 0:512], cst["ones1x128"][0:1, :],
                    s["bprime"][0:1, nch * 512:(nch + 1) * 512],
                    start=False, stop=False), inc=None)
                k.op("tensor", lambda t, tt4=tt4, nch=nch, bank=bank: t.matmul(
                    bank[:, 0:512], ident[:],
                    x_tm[:, DynSlice(xoff + (tt4 * 1024 + nch * 512), 512)],
                    start=False, stop=True),
                    inc=("m_t" if nch == 1 else None))
            self.emit_postln_psum(k, cst, [pa, pb], x_tm, xoff, tt4 * 1024, "m_t", k.cnt["m_t"])

    def emit_postln_psum(self, k, cst, banks, x_tm, xoff, xcol, tsem, tval):
        s = self.sb
        st = s["pstat"]
        k.op("vector", lambda v: v.tensor_reduce(st[:, 0:1], banks[0][:, 0:512],
                                                 axis=AX.X, op=ALU.add),
             wait=[(tsem, tval)], inc="m_v")
        v_r0 = k.cnt["m_v"]
        k.op("vector", lambda v: v.tensor_reduce(st[:, 1:2], banks[1][:, 0:512],
                                                 axis=AX.X, op=ALU.add), inc="m_v")
        v_r1 = k.cnt["m_v"]
        k.op("scalar", lambda sc: sc.activation(s["da"][:], banks[0][:, 0:512],
                                                AF.Square, accum_out=st[:, 2:3]),
             wait=[(tsem, tval), ("m_v", v_r0)], inc="m_s")
        k.op("scalar", lambda sc: sc.activation(s["u"][:], banks[1][:, 0:512],
                                                AF.Square, accum_out=st[:, 3:4]),
             wait=[("m_v", v_r1)], inc="m_s")
        k.op("vector", lambda v: v.tensor_tensor(st[:, 0:1], st[:, 0:1], st[:, 1:2], op=ALU.add),
             wait=[("m_s", k.cnt["m_s"])], inc="m_v")
        k.op("vector", lambda v: v.tensor_scalar_mul(st[:, 4:5], st[:, 0:1], 1.0 / 1024.0),
             wait=[("m_v", k.cnt["m_v"])], inc="m_v")
        k.op("vector", lambda v: v.tensor_tensor(st[:, 2:3], st[:, 2:3], st[:, 3:4], op=ALU.add),
             inc="m_v")
        k.op("vector", lambda v: v.tensor_tensor(st[:, 5:6], st[:, 4:5], st[:, 4:5], op=ALU.mult),
             wait=[("m_v", k.cnt["m_v"])], inc="m_v")
        k.op("vector", lambda v: v.scalar_tensor_tensor(
            st[:, 5:6], st[:, 2:3], 1.0 / 1024.0, st[:, 5:6], op0=ALU.mult, op1=ALU.subtract),
            wait=[("m_v", k.cnt["m_v"])], inc="m_v")
        vv = k.cnt["m_v"]
        k.op("scalar", lambda sc: sc.activation(st[:, 5:6], st[:, 5:6], AF.Ln,
                                                bias=s["epsv"][:, 0:1]),
             wait=[("m_v", vv)], inc="m_s")
        k.op("scalar", lambda sc: sc.activation(st[:, 5:6], st[:, 5:6], AF.Exp, scale=-0.5),
             wait=[("m_s", k.cnt["m_s"])], inc="m_s")
        k.op("vector", lambda v: v.tensor_tensor(st[:, 6:7], st[:, 4:5], st[:, 5:6], op=ALU.mult),
             wait=[("m_s", k.cnt["m_s"])], inc="m_v")
        k.op("vector", lambda v: v.tensor_scalar_mul(st[:, 6:7], st[:, 6:7], -1.0),
             wait=[("m_v", k.cnt["m_v"])], inc="m_v")
        vv2 = k.cnt["m_v"]
        for h in range(2):
            k.op("scalar", lambda sc, h=h: sc.activation(
                x_tm[:, DynSlice(xoff + (xcol + h * 512), 512)], banks[h][:, 0:512],
                AF.Identity, bias=st[:, 6:7], scale=st[:, 5:6]),
                wait=[("m_v", vv2)] if h == 0 else [], inc="m_s")


def make_consts():
    ident = np.eye(128, dtype=BF16NP)
    ones1x128 = np.ones((1, 128), dtype=BF16NP)
    ones1x512 = np.ones((1, 512), dtype=BF16NP)
    zeros1x512 = np.zeros((1, 512), dtype=BF16NP)
    ident_f32 = np.eye(128, dtype=np.float32)
    ones128col = np.ones((128, 1), dtype=BF16NP)
    sel = np.zeros((32, 32 * 128), dtype=BF16NP)
    for n in range(32):
        sel[n, n * 128:(n + 1) * 128] = 1
    return {"ident_bf": ident, "ones1x128": ones1x128, "ones1x512": ones1x512,
            "zeros1x512": zeros1x512, "ones128col_bf": ones128col, "sel32": sel,
            "ident_f32": ident_f32,
            "ones1x128f": np.ones((1, 128), dtype=np.float32),
            "ones128colf": np.ones((128, 1), dtype=np.float32),
            "zeros1x512f": np.zeros((1, 512), dtype=np.float32)}


def const_tensors(nc, E, k_unused=None, only=None):
    cst_np = make_consts()
    aps = {}
    sb = {}
    for nm, arr in cst_np.items():
        dt = F32 if arr.dtype == np.float32 else BF16
        aps[nm] = nc.dram_tensor("c_" + nm, list(arr.shape), dt, kind="ExternalInput").ap()
        if nm != "sel32" and (only is None or nm in only):
            sb[nm] = E(nc.sbuf_tensor("cs_" + nm, list(arr.shape), dt))
    return cst_np, aps, sb


MAMBA_W_NAMES = ("Win", "Wdt", "Wxp", "Wout", "A_log", "wconv", "bconv", "bdt",
                 "gamma", "beta", "bout")


def mamba_host_prep(inp):
    f = np.ascontiguousarray
    def tile8(v):
        return f(v.reshape(8, 128).T)
    return {
        "Win": f(inp["L0_attn_W_in"].T),
        "Wdt": f(inp["L0_attn_W_dt"].T),
        "Wxp": f(inp["L0_attn_W_xproj"].T),
        "Wout": f(inp["L0_attn_W_out"].T),
        "A_log": f(inp["L0_attn_A_log"]),
        "wconv": f(inp["L0_attn_W_conv"][:, 0, :].reshape(8, 128, 4).transpose(1, 0, 2).reshape(128, 32)),
        "bconv": tile8(inp["L0_attn_b_conv"]),
        "bdt": tile8(inp["L0_attn_b_dt"]),
        "gamma": tile8(inp["L0_attn_ln_gamma"]),
        "beta": tile8(inp["L0_attn_ln_beta"]),
        "bout": f(inp["L0_attn_b_out"].reshape(1, 1024)),
    }


def build_mamba_program(T):
    from contextlib import ExitStack
    nc = bass.Bass("TRN2", target_bir_lowering=False, detect_race_conditions=False)
    ntt = T // 128
    x_in = nc.dram_tensor("x", [T, 1024], F32, kind="ExternalInput").ap()
    y_out = nc.dram_tensor("y", [T, 1024], BF16, kind="ExternalOutput").ap()
    dbg_out = nc.dram_tensor("dbg", [128, 8192], F32, kind="ExternalOutput").ap()
    io = {nm: nc.dram_tensor("w_" + nm, None, F32, kind="ExternalInput") for nm in ()}
    io = {}
    shp = {"Win": [1024, 2048], "Wdt": [1024, 1024], "Wxp": [1024, 32],
           "Wout": [1024, 1024], "A_log": [1024, 16], "wconv": [128, 32],
           "bconv": [128, 8], "bdt": [128, 8], "gamma": [128, 8], "beta": [128, 8],
           "bout": [1, 1024]}
    for nm in MAMBA_W_NAMES:
        io[nm] = nc.dram_tensor("w_" + nm, shp[nm], F32, kind="ExternalInput").ap()
    ctx = ExitStack()
    with ctx:
        E = ctx.enter_context
        cst_np, cst_aps, cst_sb = const_tensors(nc, E)
        x_tm = E(nc.sbuf_tensor("x_tm", [128, ntt * 1024], BF16))
        mam = Mamba(nc, E, "m_", T)
        sems = {}
        for nm in ("m_ld", "g_ld", "m_wl", "m_t", "m_v", "m_s", "m_e", "st", "g_st", "m_sl"):
            sems[nm] = E(nc.semaphore(nm))
        blk = E(nc.Block())
        k = K(nc, sems, {})

        def emit_all(k):
            for nm in cst_sb:
                k.op("sync", lambda sy, nm=nm: sy.dma_start(out=cst_sb[nm][:], in_=cst_aps[nm]),
                     inc="m_ld", inc_n=16)
            k.op("gpsimd", lambda g: g.dma_start(
                out=x_tm[:].rearrange("p (t d) -> p t d", t=ntt),
                in_=x_in.rearrange("(t p) d -> p t d", p=128)), inc="g_ld", inc_n=16)
            cst = {nm: cst_sb[nm][:] for nm in cst_sb}
            cst["sel32_hbm"] = cst_aps["sel32"]
            mam.emit(k, io, cst, x_tm[:])
            k.op("sync", lambda sy: sy.dma_start(
                out=y_out.rearrange("(t p) d -> p t d", p=128),
                in_=x_tm[:].rearrange("p (t d) -> p t d", t=ntt)),
                wait=[("m_s", k.cnt["m_s"]), ("m_v", k.cnt["m_v"])],
                inc="st", inc_n=16)
            import os
            dmp = os.environ.get("MAMBA_DUMP")
            if dmp:
                src = mam.sb[dmp]
                ncol = min(src.shape[1], 8192, int(os.environ.get('MAMBA_DUMP_NCOL', '8192')))
                k.op("gpsimd", lambda g: g.dma_start(
                    out=dbg_out[0:src.shape[0], 0:ncol], in_=src[:, 0:ncol]),
                    wait=[("m_s", k.cnt["m_s"]), ("m_v", k.cnt["m_v"]),
                          ("m_t", k.cnt["m_t"]), ("m_e", k.cnt["m_e"])],
                    inc="g_st", inc_n=16)
                k.wait("sync", "st", 16)
                k.wait("sync", "g_st", 16)
            else:
                k.wait("sync", "st", 16)

        def section(name):
            def f(eng):
                k.run(name, eng, [emit_all])
            return f
        blk.sync(section("sync"))
        blk.gpsimd(section("gpsimd"))
        blk.tensor(section("tensor"))
        blk.vector(section("vector"))
        blk.scalar(section("scalar"))
    return nc, cst_np


def run_mamba_test(x, weights, T, sim=False):
    nc, cst_np = build_mamba_program(T)
    in_map = {"x": np.ascontiguousarray(x[:T]).astype(np.float32)}
    for nm in MAMBA_W_NAMES:
        in_map["w_" + nm] = weights[nm]
    for nm, arr in cst_np.items():
        in_map["c_" + nm] = arr
    if sim:
        from concourse.bass_interp import MultiCoreSim
        simu = MultiCoreSim(nc, num_cores=1, num_workers=1, trace=False)
        for kk, v in in_map.items():
            simu.cores[0].tensor(kk)[:] = v
        simu.simulate(check_with_hw=False)
        return np.asarray(simu.cores[0].tensor("y"))
    res = run_bass_kernel_spmd(nc, [in_map], core_ids=[0])
    return np.asarray(res.results[0]["y"])


def emit_postln(k, sb, banks_or_sbuf, x_out_slices, tsem, tval, from_psum=True):
    st = sb["pstat"]
    halves = banks_or_sbuf
    k.op("vector", lambda v: v.tensor_reduce(st[:, 0:1], halves[0], axis=AX.X, op=ALU.add),
         wait=[(tsem, tval)], inc="m_v")
    v_r0 = k.cnt["m_v"]
    k.op("vector", lambda v: v.tensor_reduce(st[:, 1:2], halves[1], axis=AX.X, op=ALU.add),
         inc="m_v")
    v_r1 = k.cnt["m_v"]
    k.op("scalar", lambda sc: sc.activation(sb["sq_a"][:], halves[0], AF.Square,
                                            accum_out=st[:, 2:3]),
         wait=[(tsem, tval), ("m_v", v_r0)], inc="m_s")
    k.op("scalar", lambda sc: sc.activation(sb["sq_b"][:], halves[1], AF.Square,
                                            accum_out=st[:, 3:4]),
         wait=[("m_v", v_r1)], inc="m_s")
    k.op("vector", lambda v: v.tensor_tensor(st[:, 0:1], st[:, 0:1], st[:, 1:2], op=ALU.add),
         wait=[("m_s", k.cnt["m_s"])], inc="m_v")
    k.op("vector", lambda v: v.tensor_scalar_mul(st[:, 4:5], st[:, 0:1], 1.0 / 1024.0),
         wait=[("m_v", k.cnt["m_v"])], inc="m_v")
    k.op("vector", lambda v: v.tensor_tensor(st[:, 2:3], st[:, 2:3], st[:, 3:4], op=ALU.add),
         inc="m_v")
    k.op("vector", lambda v: v.tensor_tensor(st[:, 5:6], st[:, 4:5], st[:, 4:5], op=ALU.mult),
         wait=[("m_v", k.cnt["m_v"])], inc="m_v")
    k.op("vector", lambda v: v.scalar_tensor_tensor(
        st[:, 5:6], st[:, 2:3], 1.0 / 1024.0, st[:, 5:6], op0=ALU.mult, op1=ALU.subtract),
        wait=[("m_v", k.cnt["m_v"])], inc="m_v")
    vv = k.cnt["m_v"]
    k.op("scalar", lambda sc: sc.activation(st[:, 5:6], st[:, 5:6], AF.Ln,
                                            bias=sb["epsv"][:, 0:1]),
         wait=[("m_v", vv)], inc="m_s")
    k.op("scalar", lambda sc: sc.activation(st[:, 5:6], st[:, 5:6], AF.Exp, scale=-0.5),
         wait=[("m_s", k.cnt["m_s"])], inc="m_s")
    k.op("vector", lambda v: v.tensor_tensor(st[:, 6:7], st[:, 4:5], st[:, 5:6], op=ALU.mult),
         wait=[("m_s", k.cnt["m_s"])], inc="m_v")
    k.op("vector", lambda v: v.tensor_scalar_mul(st[:, 6:7], st[:, 6:7], -1.0),
         wait=[("m_v", k.cnt["m_v"])], inc="m_v")
    vv2 = k.cnt["m_v"]
    for h in range(2):
        k.op("scalar", lambda sc, h=h: sc.activation(
            x_out_slices[h], halves[h], AF.Identity, bias=st[:, 6:7], scale=st[:, 5:6]),
            wait=[("m_v", vv2)] if h == 0 else [], inc="m_s")


class Attn:

    def __init__(self, nc, E, pfx, mode):
        self.mode = mode
        self.pfx = pfx
        s = self.sb = {}
        self.scr = nc.dram_tensor(pfx + "qkv_scr", [3072, T], BF16, kind="Internal").ap()
        if mode != "L2":
            self.oscr = nc.dram_tensor(pfx + "o_scr", [1024, T], BF16, kind="Internal").ap()
        s["W"] = E(nc.sbuf_tensor(pfx + "W", [128, 8 * 3072], BF16))
        s["Wo"] = E(nc.sbuf_tensor(pfx + "Wo", [128, 8 * 1024], BF16)) if mode != "L2" else None
        s["bias_qkv"] = E(nc.sbuf_tensor(pfx + "bq", [1, 3072], BF16))
        s["bias_o"] = E(nc.sbuf_tensor(pfx + "bo", [1, 1024], BF16))
        s["xstg"] = E(nc.sbuf_tensor(pfx + "xstg", [128, 4 * 1024], BF16))
        s["xTc"] = E(nc.sbuf_tensor(pfx + "xTc", [128, 8 * 512], BF16))
        s["qkvstg"] = E(nc.sbuf_tensor(pfx + "qkvstg", [128, 24 * 512], BF16))
        s["ostg"] = E(nc.sbuf_tensor(pfx + "ostg", [128, 4 * 1024], BF16))
        if mode == "L2":
            s["kblk"] = E(nc.sbuf_tensor(pfx + "kblk", [128, 2 * 8 * 512], BF16))
            s["vblk"] = E(nc.sbuf_tensor(pfx + "vblk", [128, 2 * 8 * 512], BF16))
            s["vtm"] = E(nc.sbuf_tensor(pfx + "vtm", [128, 2 * 4 * 1024], BF16))
            s["qg"] = E(nc.sbuf_tensor(pfx + "qg", [128, 8 * 512], BF16))
            s["oacc"] = E(nc.sbuf_tensor(pfx + "oacc", [128, 4 * 1024], F32))
        else:
            s["qh"] = E(nc.sbuf_tensor(pfx + "qh", [64, T], BF16))
            s["kh"] = E(nc.sbuf_tensor(pfx + "kh", [64, T], BF16))
            s["vh"] = E(nc.sbuf_tensor(pfx + "vh", [64, T], BF16))
            s["vtm"] = E(nc.sbuf_tensor(pfx + "vtm", [128, 32 * 64], BF16))
            s["oh"] = E(nc.sbuf_tensor(pfx + "oh", [128, 32 * 64], BF16))
            s["oacc"] = E(nc.sbuf_tensor(pfx + "oacc", [128, 4 * 64], F32))
            s["oT"] = E(nc.sbuf_tensor(pfx + "oT", [128, 8 * 512], BF16))
            s["oTstg"] = E(nc.sbuf_tensor(pfx + "oTstg", [64, T], BF16))
        s["P"] = E(nc.sbuf_tensor(pfx + "P", [128, 512], BF16))
        s["pT"] = E(nc.sbuf_tensor(pfx + "pT", [128, 512], BF16))
        s["negm"] = E(nc.sbuf_tensor(pfx + "negm", [128, 32], F32))
        s["masks"] = E(nc.sbuf_tensor(pfx + "masks", [128, 4 * 512], F32))
        s["pstat"] = E(nc.sbuf_tensor(pfx + "pstat", [128, 8], F32))
        s["sq_a"] = E(nc.sbuf_tensor(pfx + "sq_a", [128, 512], BF16))
        s["sq_b"] = E(nc.sbuf_tensor(pfx + "sq_b", [128, 512], BF16))
        s["epsv"] = E(nc.sbuf_tensor(pfx + "epsv", [128, 1], F32))
        if mode == "L3":
            s["slopes"] = E(nc.sbuf_tensor(pfx + "slopes", [128, 16], F32))
            s["kaug"] = E(nc.sbuf_tensor(pfx + "kaug", [2, T], BF16))
            s["qaugc"] = E(nc.sbuf_tensor(pfx + "qaugc", [2, 128], F32))
            s["qaug"] = E(nc.sbuf_tensor(pfx + "qaug", [2, 128], BF16))
        self.ps = [E(nc.psum_tensor(pfx + f"ps{i}", [128, 512], F32)) for i in range(4)]
        self.ps_tr = E(nc.psum_tensor(pfx + "ps_tr", [128, 512], BF16))
        self.ps_av = E(nc.psum_tensor(pfx + "ps_av", [128, 512], F32))
        self.ps_av2 = E(nc.psum_tensor(pfx + "ps_av2", [128, 512], F32))

    def emit_phaseQ(self, k, io, cst):
        s = self.sb
        mode = self.mode
        if mode == "L1":
            k.op("gpsimd", lambda g: g.dma_start(
                out=s["W"][:].rearrange("p (kt e) -> p kt e", kt=8),
                in_=io["w_qkv"].rearrange("(kt p) e -> p kt e", p=128)), inc="g_ld", inc_n=16)
        elif mode == "L2":
            for i, nm in enumerate(("Wq", "Wk", "Wv")):
                k.op("gpsimd", lambda g, i=i, nm=nm: g.dma_start(
                    out=s["W"][:].rearrange("p (kt e) -> p kt e", kt=8)[:, :, i * 1024:(i + 1) * 1024],
                    in_=io[nm].rearrange("(kt p) e -> p kt e", p=128)), inc="g_ld", inc_n=16)
        else:
            k.op("gpsimd", lambda g: g.dma_start(
                out=s["W"][:].rearrange("p (kt e) -> p kt e", kt=8)[:, :, 0:1024],
                in_=io["q_w"].rearrange("(kt p) e -> p kt e", p=128)), inc="g_ld", inc_n=16)
            k.op("gpsimd", lambda g: g.dma_start(
                out=s["W"][:].rearrange("p (kt e) -> p kt e", kt=8)[:, :, 1024:3072],
                in_=io["kv_w"].rearrange("(kt p) e -> p kt e", p=128)), inc="g_ld", inc_n=16)
            k.op("gpsimd", lambda g: g.dma_start(out=s["bias_qkv"][0:1, 0:1024], in_=io["q_b"]),
                 inc="g_ld", inc_n=16)
            k.op("gpsimd", lambda g: g.dma_start(out=s["bias_qkv"][0:1, 1024:3072], in_=io["kv_b"]),
                 inc="g_ld", inc_n=16)
        if mode != "L2":
            k.op("gpsimd", lambda g: g.dma_start(
                out=s["Wo"][:].rearrange("p (kt e) -> p kt e", kt=8),
                in_=io["w_out"].rearrange("(kt p) e -> p kt e", p=128)), inc="g_ld", inc_n=16)
        if mode == "L3":
            k.op("gpsimd", lambda g: g.dma_start(out=s["bias_o"][:], in_=io["o_b"]),
                 inc="g_ld", inc_n=16)
            k.op("sync", lambda sy: sy.dma_start(out=s["slopes"][:], in_=io["slopes"]),
                 inc="m_ld", inc_n=16)
            k.op("sync", lambda sy: sy.dma_start(out=s["kaug"][:], in_=io["kaug"]),
                 inc="m_ld", inc_n=16)
            k.op("sync", lambda sy: sy.dma_start(out=s["qaugc"][:], in_=io["qaugc"]),
                 inc="m_ld", inc_n=16)
        k.op("sync", lambda sy: sy.dma_start(out=s["masks"][:], in_=io["masks"]),
             inc="m_ld", inc_n=16)
        for e in ("tensor", "vector", "scalar", "gpsimd", "sync"):
            k.wait(e, "m_ld", k.cnt["m_ld"])
            k.wait(e, "g_ld", k.cnt["g_ld"])
        k.op("vector", lambda v: v.memset(s["epsv"][:], 1e-5), inc="m_v")

        ident = cst["ident_bf"]

        def body(ch):
            xoff = ch * 512
            k.op("gpsimd", lambda g: g.dma_start(
                out=s["xstg"][:].rearrange("p (a d) -> p a d", a=4),
                in_=io["x_in"].rearrange("(t p) d -> p t d", p=128)[:, DynSlice(ch * 4, 4), :]),
                wait=[("m_t", k.cnt["m_t"])], inc="g_x", inc_n=16)
            gx = k.cnt["g_x"]
            for dt in range(8):
                w = [("g_x", gx)] if dt == 0 else []
                w.append(("m_v", k.cnt["m_v"]))
                for tt4 in range(4):
                    k.op("tensor", lambda t, dt=dt, tt4=tt4: t.transpose(
                        self.ps_tr[:, tt4 * 128:(tt4 + 1) * 128],
                        s["xstg"][:, tt4 * 1024 + dt * 128: tt4 * 1024 + (dt + 1) * 128],
                        ident[:]),
                        wait=(w if tt4 == 0 else []),
                        inc=("m_t" if tt4 == 3 else None))
                k.op("vector", lambda v, dt=dt: v.tensor_copy(
                    s["xTc"][:, dt * 512:(dt + 1) * 512], self.ps_tr[:, 0:512]),
                    wait=[("m_t", k.cnt["m_t"])], inc="m_v")
            xt_v = k.cnt["m_v"]
            evq = k.cnt["m_v"]
            for mt in range(24):
                bank = self.ps[mt % 3]
                w = [("m_v", evq + mt - 2)] if mt >= 3 else [("m_v", xt_v)]
                for kt in range(8):
                    k.op("tensor", lambda t, mt=mt, kt=kt, bank=bank: t.matmul(
                        bank[:, 0:512],
                        s["W"][:, kt * 3072 + mt * 128: kt * 3072 + (mt + 1) * 128],
                        s["xTc"][:, kt * 512:(kt + 1) * 512],
                        start=(kt == 0), stop=(kt == 7 and self.mode != "L3")),
                        wait=(w if kt == 0 else []),
                        inc=(None if self.mode == "L3" else ("m_t" if kt == 7 else None)))
                if self.mode == "L3":
                    k.op("tensor", lambda t, mt=mt, bank=bank: t.matmul(
                        bank[:, 0:512],
                        s["bias_qkv"][0:1, mt * 128:(mt + 1) * 128],
                        cst["ones1x512"][0:1, :],
                        start=False, stop=True), inc="m_t")
                if self.mode == "L3" and mt < 8:
                    k.op("vector", lambda v, mt=mt, bank=bank: v.tensor_scalar(
                        s["qkvstg"][:, mt * 512:(mt + 1) * 512], bank[:, 0:512],
                        0.125, None, op0=ALU.mult),
                        wait=[("m_t", k.cnt["m_t"])], inc="m_v")
                else:
                    k.op("vector", lambda v, mt=mt, bank=bank: v.tensor_copy(
                        s["qkvstg"][:, mt * 512:(mt + 1) * 512], bank[:, 0:512]),
                        wait=[("m_t", k.cnt["m_t"])], inc="m_v")
            k.op("gpsimd", lambda g: g.dma_start(
                out=self.scr.rearrange("(mt p) t -> p mt t", p=128)[:, :, DynSlice(xoff, 512)],
                in_=s["qkvstg"][:].rearrange("p (mt c) -> p mt c", mt=24)),
                wait=[("m_v", k.cnt["m_v"])], inc="g_st", inc_n=16)

        k.autoloop(8, body)
        for e in ("tensor", "vector", "scalar", "gpsimd", "sync"):
            k.wait(e, "g_st", k.cnt["g_st"])


    def emit_phaseA(self, k, io, cst):
        if self.mode == "L2":
            self.emit_phaseA_L2(k, io, cst)
        else:
            self.emit_phaseA_heads(k, io, cst)

    def emit_phaseA_heads(self, k, io, cst):
        s = self.sb
        mode = self.mode
        scale = 0.125 if mode == "L1" else 1.0
        ident = cst["ident_bf"]

        def head_body(h):
            k.op("sync", lambda sy: sy.dma_start(
                out=s["qh"][:], in_=self.scr[DynSlice(h * 64, 64), :]),
                wait=[("m_t", k.cnt["m_t"])], inc="m_qk", inc_n=16)
            k.op("sync", lambda sy: sy.dma_start(
                out=s["kh"][:], in_=self.scr[DynSlice(1024 + h * 64, 64), :]),
                inc="m_qk", inc_n=16)
            k.op("sync", lambda sy: sy.dma_start(
                out=s["vh"][:], in_=self.scr[DynSlice(2048 + h * 64, 64), :]),
                inc="m_qk", inc_n=16)
            ldv = k.cnt["m_qk"]
            if mode == "L3":
                k.op("vector", lambda v: v.tensor_scalar(
                    s["qaug"][:], s["qaugc"][:], s["slopes"][0:2, DynSlice(h, 1)], None,
                    op0=ALU.mult),
                    wait=[("m_t", k.cnt["m_t"])], inc="m_v")
            for tt in range(32):
                w = [("m_qk", ldv)] if tt == 0 else []
                if tt == 0:
                    w.append(("m_v", k.cnt["m_v"]))
                k.op("tensor", lambda t, tt=tt: t.transpose(
                    self.ps_tr[:, (tt % 8) * 64:(tt % 8) * 64 + 64],
                    s["vh"][:, tt * 128:(tt + 1) * 128], ident[0:64, 0:64]),
                    wait=w + ([("m_v", k.cnt["m_v"])] if tt >= 8 and tt % 8 == 0 else []),
                    inc=("m_t" if tt % 8 == 7 else None))
                if tt % 8 == 7:
                    k.op("vector", lambda v, tt=tt: v.tensor_copy(
                        s["vtm"][:, (tt - 7) * 64:(tt + 1) * 64], self.ps_tr[:, 0:512]),
                        wait=[("m_t", k.cnt["m_t"])], inc="m_v")
            vtm_v = k.cnt["m_v"]
            for g in range(8):
                kbs = list(range(0, g + 1)) if mode == "L1" else list(range(max(g - 1, 0), g + 1))
                for qt in range(4):
                    qcol = g * 512 + qt * 128
                    nm = s["negm"][:, 0:1]
                    fcol = s["negm"][:, 1:2]
                    lcol = s["negm"][:, 2:3]
                    lb = s["negm"][:, 3:4]
                    dm = s["negm"][:, 4:5]
                    k.op("vector", lambda v: v.memset(s["negm"][:, 0:8], 0.0),
                         wait=[("m_av", k.cnt["m_av"])], inc="m_v")
                    k.op("vector", lambda v, nm=nm: v.memset(nm, 1e30),
                         wait=[("m_v", k.cnt["m_v"])], inc="m_v")
                    k.op("vector", lambda v: v.memset(s["oacc"][:, qt * 64:(qt + 1) * 64], 0.0),
                         inc="m_v")
                    for kb in kbs:
                        bank = self.ps[kb % 2]
                        k.op("tensor", lambda t, qcol=qcol, kb=kb, bank=bank: t.matmul(
                            bank[:, 0:512],
                            s["qh"][:, qcol:qcol + 128],
                            s["kh"][:, kb * 512:(kb + 1) * 512],
                            start=True, stop=(mode == "L1")),
                            wait=[("m_v", k.cnt["m_v"]), ("m_s", k.cnt["m_s"])],
                            inc=("m_t" if mode == "L1" else None))
                        if mode == "L3":
                            k.op("tensor", lambda t, qt=qt, kb=kb, bank=bank: t.matmul(
                                bank[:, 0:512], s["qaug"][:, 0:128],
                                s["kaug"][:, kb * 512:(kb + 1) * 512],
                                start=False, stop=True),
                                inc="m_t")
                        mm_t = k.cnt["m_t"]
                        if kb == g:
                            k.op("vector", lambda v, qt=qt, bank=bank: v.tensor_tensor(
                                bank[:, 0:512], bank[:, 0:512],
                                s["masks"][:, qt * 512:(qt + 1) * 512], op=ALU.add),
                                wait=[("m_t", mm_t)], inc="m_v")
                        k.op("vector", lambda v, bank=bank: v.tensor_reduce(
                            lb, bank[:, 0:512], axis=AX.X, op=ALU.max, negate=True),
                            wait=[("m_t", mm_t), ("m_v", k.cnt["m_v"])], inc="m_v")
                        if mode == "L3":
                            K_c = (kb - g) * 512 - qt * 128 + 192
                            k.op("vector", lambda v, K_c=K_c: v.scalar_tensor_tensor(
                                lb, s["slopes"][:, DynSlice(h, 1)], float(-K_c), lb,
                                op0=ALU.mult, op1=ALU.add),
                                wait=[("m_v", k.cnt["m_v"])], inc="m_v")
                        k.op("vector", lambda v: v.tensor_tensor(dm, nm, lb, op=ALU.min),
                             wait=[("m_v", k.cnt["m_v"])], inc="m_v")
                        k.op("vector", lambda v: v.tensor_tensor(fcol, dm, nm, op=ALU.subtract),
                             wait=[("m_v", k.cnt["m_v"])], inc="m_v")
                        k.op("vector", lambda v: v.tensor_copy(nm, dm),
                             wait=[("m_v", k.cnt["m_v"])], inc="m_v")
                        k.op("scalar", lambda sc: sc.activation(fcol, fcol, AF.Exp, scale=scale),
                             wait=[("m_v", k.cnt["m_v"])], inc="m_s")
                        if mode == "L3":
                            K_c = (kb - g) * 512 - qt * 128 + 192
                            k.op("vector", lambda v, K_c=K_c: v.scalar_tensor_tensor(
                                dm, s["slopes"][:, DynSlice(h, 1)], float(K_c), nm,
                                op0=ALU.mult, op1=ALU.add),
                                wait=[("m_s", k.cnt["m_s"])], inc="m_v")
                            bsrc = dm
                        else:
                            bsrc = nm
                        k.op("vector", lambda v, bsrc=bsrc: v.tensor_scalar_mul(
                            s["negm"][:, 5:6], bsrc, scale),
                            wait=[("m_v", k.cnt["m_v"]), ("m_s", k.cnt["m_s"])], inc="m_v")
                        k.op("scalar", lambda sc, bank=bank: sc.activation(
                            s["P"][:], bank[:, 0:512], AF.Exp, bias=s["negm"][:, 5:6],
                            scale=scale, accum_out=s["negm"][:, 6:7]),
                            wait=[("m_v", k.cnt["m_v"])], inc="m_s")
                        k.op("vector", lambda v: v.scalar_tensor_tensor(
                            lcol, lcol, fcol, s["negm"][:, 6:7], op0=ALU.mult, op1=ALU.add),
                            wait=[("m_s", k.cnt["m_s"]), ("m_v", k.cnt["m_v"])], inc="m_v")
                        for c4 in range(4):
                            k.op("tensor", lambda t, c4=c4: t.transpose(
                                self.ps_tr[:, c4 * 128:(c4 + 1) * 128],
                                s["P"][:, c4 * 128:(c4 + 1) * 128], ident[:]),
                                wait=([("m_s", k.cnt["m_s"]), ("m_v", k.cnt["m_v"])] if c4 == 0 else []),
                                inc=("m_t" if c4 == 3 else None))
                        k.op("vector", lambda v: v.tensor_copy(s["pT"][:], self.ps_tr[:, 0:512]),
                             wait=[("m_t", k.cnt["m_t"])], inc="m_v")
                        for c4 in range(4):
                            k.op("tensor", lambda t, kb=kb, c4=c4: t.matmul(
                                self.ps_av[:, 0:64],
                                s["pT"][:, c4 * 128:(c4 + 1) * 128],
                                s["vtm"][:, (kb * 4 + c4) * 64:(kb * 4 + c4 + 1) * 64],
                                start=(c4 == 0), stop=(c4 == 3)),
                                wait=([("m_v", k.cnt["m_v"])] if c4 == 0 else []),
                                inc=("m_av" if c4 == 3 else None))
                        k.op("vector", lambda v, qt=qt: v.scalar_tensor_tensor(
                            s["oacc"][:, qt * 64:(qt + 1) * 64],
                            s["oacc"][:, qt * 64:(qt + 1) * 64], fcol,
                            self.ps_av[:, 0:64], op0=ALU.mult, op1=ALU.add),
                            wait=[("m_av", k.cnt["m_av"])], inc="m_v")
                    k.op("vector", lambda v: v.reciprocal(fcol, lcol),
                         wait=[("m_v", k.cnt["m_v"])], inc="m_v")
                    k.op("vector", lambda v, g=g, qt=qt: v.tensor_scalar(
                        s["oh"][:, (g * 4 + qt) * 64:(g * 4 + qt + 1) * 64],
                        s["oacc"][:, qt * 64:(qt + 1) * 64], fcol, None, op0=ALU.mult),
                        wait=[("m_v", k.cnt["m_v"])], inc="m_v")
            for tt in range(32):
                w = [("m_v", k.cnt["m_v"])] if tt % 4 == 0 else []
                k.op("tensor", lambda t, tt=tt: t.transpose(
                    self.ps_tr[0:64, (tt % 4) * 128:(tt % 4) * 128 + 128],
                    s["oh"][:, tt * 64:(tt + 1) * 64], ident[:]),
                    wait=w, inc=("m_t" if tt % 4 == 3 else None))
                if tt % 4 == 3:
                    k.op("vector", lambda v, tt=tt: v.tensor_copy(
                        s["oTstg"][0:64, (tt - 3) * 128:(tt + 1) * 128],
                        self.ps_tr[0:64, 0:512]),
                        wait=[("m_t", k.cnt["m_t"])], inc="m_v")
            k.op("gpsimd", lambda g: g.dma_start(
                out=self.oscr[DynSlice(h * 64, 64), :], in_=s["oTstg"][0:64, :]),
                wait=[("m_v", k.cnt["m_v"])], inc="g_st", inc_n=16)

        k.autoloop(16, head_body, engines=("tensor", "vector", "scalar", "sync", "gpsimd"))
        for e in ("tensor", "vector", "scalar", "gpsimd", "sync"):
            k.wait(e, "g_st", k.cnt["g_st"])

    def emit_phaseO(self, k, io, cst):
        s = self.sb
        ident = cst["ident_bf"]

        def body(ch):
            k.op("sync", lambda sy: sy.dma_start(
                out=s["oT"][:].rearrange("p (kt c) -> p kt c", kt=8),
                in_=self.oscr.rearrange("(kt p) t -> p kt t", p=128)[:, :, DynSlice(ch * 512, 512)]),
                wait=[("m_t", k.cnt["m_t"])], inc="m_qk", inc_n=16)
            k.op("gpsimd", lambda g: g.dma_start(
                out=s["xstg"][:].rearrange("p (a d) -> p a d", a=4),
                in_=io["x_in"].rearrange("(t p) d -> p t d", p=128)[:, DynSlice(ch * 4, 4), :]),
                wait=[("m_t", k.cnt["m_t"])], inc="g_x", inc_n=16)
            ld = k.cnt["m_qk"]
            gx = k.cnt["g_x"]
            for tt4 in range(4):
                pa, pb = self.ps[0], self.ps[1]
                w0 = [("m_qk", ld), ("g_x", gx), ("m_s", k.cnt["m_s"])]
                for nch in range(2):
                    bank = pa if nch == 0 else pb
                    for kt in range(8):
                        k.op("tensor", lambda t, tt4=tt4, nch=nch, kt=kt, bank=bank: t.matmul(
                            bank[:, 0:512],
                            s["oT"][:, kt * 512 + tt4 * 128: kt * 512 + (tt4 + 1) * 128],
                            s["Wo"][:, kt * 1024 + nch * 512: kt * 1024 + (nch + 1) * 512],
                            start=(kt == 0), stop=False),
                            wait=(w0 if (nch == 0 and kt == 0) else []), inc=None)
                    if self.mode == "L3":
                        k.op("tensor", lambda t, nch=nch, bank=bank: t.matmul(
                            bank[:, 0:512], cst["ones1x128"][0:1, :],
                            s["bias_o"][0:1, nch * 512:(nch + 1) * 512],
                            start=False, stop=False), inc=None)
                    k.op("tensor", lambda t, tt4=tt4, nch=nch, bank=bank: t.matmul(
                        bank[:, 0:512], ident[:],
                        s["xstg"][:, tt4 * 1024 + nch * 512: tt4 * 1024 + (nch + 1) * 512],
                        start=False, stop=True),
                        inc=("m_t" if nch == 1 else None))
                emit_postln(k, s, [pa[:, 0:512], pb[:, 0:512]],
                            [s["ostg"][:, tt4 * 1024: tt4 * 1024 + 512],
                             s["ostg"][:, tt4 * 1024 + 512: tt4 * 1024 + 1024]],
                            "m_t", k.cnt["m_t"])
            k.op("gpsimd", lambda g: g.dma_start(
                out=io["x_out"].rearrange("(t p) d -> p t d", p=128)[:, DynSlice(ch * 4, 4), :],
                in_=s["ostg"][:].rearrange("p (a d) -> p a d", a=4)),
                wait=[("m_s", k.cnt["m_s"])], inc="g_st", inc_n=16)

        k.autoloop(8, body)
        k.wait("sync", "g_st", k.cnt["g_st"])

    def emit_phaseA_L2(self, k, io, cst):
        s = self.sb
        ident = cst["ident_bf"]
        scale = 1.0 / 32.0
        for g in range(8):
            k.op("sync", lambda sy, g=g: sy.dma_start(
                out=s["qg"][:].rearrange("p (kt c) -> p kt c", kt=8),
                in_=self.scr[0:1024, :].rearrange("(kt p) t -> p kt t", p=128)[:, :, g * 512:(g + 1) * 512]),
                wait=[("m_t", k.cnt["m_t"])], inc="m_qk", inc_n=16)
            ldq = k.cnt["m_qk"]
            k.op("gpsimd", lambda g_, g=g: g_.dma_start(
                out=s["xstg"][:].rearrange("p (a d) -> p a d", a=4),
                in_=io["x_in"].rearrange("(t p) d -> p t d", p=128)[:, g * 4:(g + 1) * 4, :]),
                wait=[("m_s", k.cnt["m_s"])], inc="g_x", inc_n=16)
            gx = k.cnt["g_x"]
            for qt in range(4):
                k.op("vector", lambda v, qt=qt: v.memset(s["negm"][:, qt * 8: qt * 8 + 8], 0.0),
                     wait=[("m_av", k.cnt["m_av"])] if qt == 0 else [], inc="m_v")
                k.op("vector", lambda v, qt=qt: v.memset(s["negm"][:, qt * 8: qt * 8 + 1], 1e30),
                     wait=[("m_v", k.cnt["m_v"])], inc="m_v")
                k.op("vector", lambda v, qt=qt: v.memset(s["oacc"][:, qt * 1024:(qt + 1) * 1024], 0.0),
                     inc="m_v")
            init_v = k.cnt["m_v"]
            for kb in range(g + 1):
                par = kb % 2
                k.op("sync", lambda sy, kb=kb, par=par: sy.dma_start(
                    out=s["kblk"][:, par * 4096:(par + 1) * 4096].rearrange("p (kt c) -> p kt c", kt=8),
                    in_=self.scr[1024:2048, :].rearrange("(kt p) t -> p kt t", p=128)[:, :, kb * 512:(kb + 1) * 512]),
                    wait=[("m_t", k.cnt["m_t"])], inc="m_qk", inc_n=16)
                k.op("sync", lambda sy, kb=kb, par=par: sy.dma_start(
                    out=s["vblk"][:, par * 4096:(par + 1) * 4096].rearrange("p (kt c) -> p kt c", kt=8),
                    in_=self.scr[2048:3072, :].rearrange("(kt p) t -> p kt t", p=128)[:, :, kb * 512:(kb + 1) * 512]),
                    inc="m_qk", inc_n=16)
                ldkv = k.cnt["m_qk"]
                for tt4 in range(4):
                    for dt in range(8):
                        k.op("tensor", lambda t, tt4=tt4, dt=dt, par=par: t.transpose(
                            self.ps_tr[:, (dt % 4) * 128:(dt % 4) * 128 + 128],
                            s["vblk"][:, par * 4096 + dt * 512 + tt4 * 128:
                                      par * 4096 + dt * 512 + (tt4 + 1) * 128],
                            ident[:]),
                            wait=([("m_qk", ldkv), ("m_v", k.cnt["m_v"])] if dt % 4 == 0 else []),
                            inc=("m_t" if dt % 4 == 3 else None))
                        if dt % 4 == 3:
                            k.op("vector", lambda v, tt4=tt4, dt=dt, par=par: v.tensor_copy(
                                s["vtm"][:, par * 4096 + tt4 * 1024 + (dt - 3) * 128:
                                         par * 4096 + tt4 * 1024 + (dt + 1) * 128],
                                self.ps_tr[:, 0:512]),
                                wait=[("m_t", k.cnt["m_t"])], inc="m_v")
                vtm_v = k.cnt["m_v"]
                for qt in range(4):
                    nm = s["negm"][:, qt * 8: qt * 8 + 1]
                    fcol = s["negm"][:, qt * 8 + 1: qt * 8 + 2]
                    lcol = s["negm"][:, qt * 8 + 2: qt * 8 + 3]
                    lb = s["negm"][:, qt * 8 + 3: qt * 8 + 4]
                    dm = s["negm"][:, qt * 8 + 4: qt * 8 + 5]
                    bexp = s["negm"][:, qt * 8 + 5: qt * 8 + 6]
                    lsum = s["negm"][:, qt * 8 + 6: qt * 8 + 7]
                    bank = self.ps[2 + qt % 2]
                    for kt in range(8):
                        k.op("tensor", lambda t, qt=qt, kt=kt, par=par, bank=bank: t.matmul(
                            bank[:, 0:512],
                            s["qg"][:, kt * 512 + qt * 128: kt * 512 + (qt + 1) * 128],
                            s["kblk"][:, par * 4096 + kt * 512:par * 4096 + (kt + 1) * 512],
                            start=(kt == 0), stop=(kt == 7)),
                            wait=([("m_v", k.cnt["m_v"]), ("m_s", k.cnt["m_s"]),
                                   ("m_qk", ldkv if qt == 0 else 0),
                                   ("m_qk", ldq)] if kt == 0 else []),
                            inc=("m_t" if kt == 7 else None))
                    mm_t = k.cnt["m_t"]
                    if kb == g:
                        k.op("vector", lambda v, qt=qt, bank=bank: v.tensor_tensor(
                            bank[:, 0:512], bank[:, 0:512],
                            s["masks"][:, qt * 512:(qt + 1) * 512], op=ALU.add),
                            wait=[("m_t", mm_t)], inc="m_v")
                    k.op("vector", lambda v, bank=bank, lb=lb: v.tensor_reduce(
                        lb, bank[:, 0:512], axis=AX.X, op=ALU.max, negate=True),
                        wait=[("m_t", mm_t), ("m_v", k.cnt["m_v"])], inc="m_v")
                    k.op("vector", lambda v, dm=dm, nm=nm, lb=lb: v.tensor_tensor(dm, nm, lb, op=ALU.min),
                         wait=[("m_v", k.cnt["m_v"])], inc="m_v")
                    k.op("vector", lambda v, fcol=fcol, dm=dm, nm=nm: v.tensor_tensor(fcol, dm, nm, op=ALU.subtract),
                         wait=[("m_v", k.cnt["m_v"])], inc="m_v")
                    k.op("vector", lambda v, nm=nm, dm=dm: v.tensor_copy(nm, dm),
                         wait=[("m_v", k.cnt["m_v"])], inc="m_v")
                    k.op("scalar", lambda sc, fcol=fcol: sc.activation(fcol, fcol, AF.Exp, scale=scale),
                         wait=[("m_v", k.cnt["m_v"])], inc="m_s")
                    k.op("vector", lambda v, bexp=bexp, nm=nm: v.tensor_scalar_mul(bexp, nm, scale),
                         wait=[("m_v", k.cnt["m_v"]), ("m_s", k.cnt["m_s"])], inc="m_v")
                    k.op("scalar", lambda sc, bank=bank, bexp=bexp, lsum=lsum: sc.activation(
                        s["P"][:], bank[:, 0:512], AF.Exp, bias=bexp,
                        scale=scale, accum_out=lsum),
                        wait=[("m_v", k.cnt["m_v"])], inc="m_s")
                    k.op("vector", lambda v, lcol=lcol, fcol=fcol, lsum=lsum: v.scalar_tensor_tensor(
                        lcol, lcol, fcol, lsum, op0=ALU.mult, op1=ALU.add),
                        wait=[("m_s", k.cnt["m_s"]), ("m_v", k.cnt["m_v"])], inc="m_v")
                    for c4 in range(4):
                        k.op("tensor", lambda t, c4=c4: t.transpose(
                            self.ps_tr[:, c4 * 128:(c4 + 1) * 128],
                            s["P"][:, c4 * 128:(c4 + 1) * 128], ident[:]),
                            wait=([("m_s", k.cnt["m_s"]), ("m_v", k.cnt["m_v"])] if c4 == 0 else []),
                            inc=("m_t" if c4 == 3 else None))
                    k.op("vector", lambda v: v.tensor_copy(s["pT"][:], self.ps_tr[:, 0:512]),
                         wait=[("m_t", k.cnt["m_t"])], inc="m_v")
                    for half in range(2):
                        pav = self.ps_av if half == 0 else self.ps_av2
                        for c4 in range(4):
                            k.op("tensor", lambda t, c4=c4, half=half, pav=pav, par=par: t.matmul(
                                pav[:, 0:512],
                                s["pT"][:, c4 * 128:(c4 + 1) * 128],
                                s["vtm"][:, par * 4096 + c4 * 1024 + half * 512:
                                         par * 4096 + c4 * 1024 + (half + 1) * 512],
                                start=(c4 == 0), stop=(c4 == 3)),
                                wait=([("m_v", k.cnt["m_v"])] if (c4 == 0 and half == 0) else []),
                                inc=("m_av" if c4 == 3 else None))
                    for half in range(2):
                        pav = self.ps_av if half == 0 else self.ps_av2
                        k.op("vector", lambda v, qt=qt, half=half, pav=pav, fcol=fcol: v.scalar_tensor_tensor(
                            s["oacc"][:, qt * 1024 + half * 512: qt * 1024 + (half + 1) * 512],
                            s["oacc"][:, qt * 1024 + half * 512: qt * 1024 + (half + 1) * 512],
                            fcol, pav[:, 0:512], op0=ALU.mult, op1=ALU.add),
                            wait=[("m_av", k.cnt["m_av"])], inc="m_v")
            for qt in range(4):
                fcol = s["negm"][:, qt * 8 + 1: qt * 8 + 2]
                lcol = s["negm"][:, qt * 8 + 2: qt * 8 + 3]
                k.op("vector", lambda v, fcol=fcol, lcol=lcol: v.reciprocal(fcol, lcol),
                     wait=[("m_v", k.cnt["m_v"])], inc="m_v")
                for half in range(2):
                    k.op("vector", lambda v, qt=qt, half=half, fcol=fcol: v.tensor_scalar(
                        s["oacc"][:, qt * 1024 + half * 512: qt * 1024 + (half + 1) * 512],
                        s["oacc"][:, qt * 1024 + half * 512: qt * 1024 + (half + 1) * 512],
                        fcol, None, op0=ALU.mult),
                        wait=([("m_v", k.cnt["m_v"])] if half == 0 else []), inc="m_v")
                    k.op("vector", lambda v, qt=qt, half=half: v.tensor_tensor(
                        s["oacc"][:, qt * 1024 + half * 512: qt * 1024 + (half + 1) * 512],
                        s["oacc"][:, qt * 1024 + half * 512: qt * 1024 + (half + 1) * 512],
                        s["xstg"][:, qt * 1024 + half * 512: qt * 1024 + (half + 1) * 512],
                        op=ALU.add),
                        wait=([("g_x", gx)] if (qt == 0 and half == 0) else []), inc="m_v")
                emit_postln(k, s, [s["oacc"][:, qt * 1024: qt * 1024 + 512],
                                   s["oacc"][:, qt * 1024 + 512: qt * 1024 + 1024]],
                            [s["ostg"][:, qt * 1024: qt * 1024 + 512],
                             s["ostg"][:, qt * 1024 + 512: qt * 1024 + 1024]],
                            "m_v", k.cnt["m_v"], from_psum=False)
            k.op("gpsimd", lambda g_, g=g: g_.dma_start(
                out=io["x_out"].rearrange("(t p) d -> p t d", p=128)[:, g * 4:(g + 1) * 4, :],
                in_=s["ostg"][:].rearrange("p (a d) -> p a d", a=4)),
                wait=[("m_s", k.cnt["m_s"])], inc="g_st", inc_n=16)
        k.wait("sync", "g_st", k.cnt["g_st"])

    def emit(self, k, io, cst):
        self.emit_phaseQ(k, io, cst)
        self.emit_phaseA(k, io, cst)
        if self.mode != "L2":
            self.emit_phaseO(k, io, cst)



def attn_masks_np():
    m = np.zeros((128, 4 * 512), np.float32)
    for qt in range(4):
        i = np.arange(128)[:, None] + qt * 128
        j = np.arange(512)[None, :]
        m[:, qt * 512:(qt + 1) * 512] = np.where(i >= j, 0.0, -1e9)
    return m


def attn_host_prep(inp, mode):
    f = np.ascontiguousarray
    io = {"masks": attn_masks_np()}
    if mode == "L1":
        io["w_qkv"] = f(inp["L1_attn_w_qkv"])
        io["w_out"] = f(inp["L1_attn_w_out"])
    elif mode == "L2":
        io["Wq"] = f(inp["L2_attn_Wq"])
        io["Wk"] = f(inp["L2_attn_Wk"])
        io["Wv"] = f(inp["L2_attn_Wv"])
    else:
        io["q_w"] = f(inp["L3_attn_q_w"])
        io["kv_w"] = f(inp["L3_attn_kv_w"])
        io["w_out"] = f(inp["L3_attn_o_w"])
        io["q_b"] = f(inp["L3_attn_q_b"].reshape(1, 1024))
        io["kv_b"] = f(inp["L3_attn_kv_b"].reshape(1, 2048))
        io["o_b"] = f(inp["L3_attn_o_b"].reshape(1, 1024))
        slopes = (np.arange(1, 17, dtype=np.float32) / 16.0)
        io["slopes"] = f(np.tile(slopes[None, :], (128, 1)))
        j = np.arange(T)
        kaug = np.stack([(j % 512) - 256.0, np.ones(T)]).astype(BF16NP)
        io["kaug"] = kaug
        qi = np.arange(128)
        io["qaugc"] = np.stack([np.ones(128), -(qi - 64.0)]).astype(np.float32)
    return io


ATTN_W = {
    "L1": [("w_qkv", [1024, 3072], F32), ("w_out", [1024, 1024], F32),
           ("masks", [128, 2048], F32)],
    "L2": [("Wq", [1024, 1024], F32), ("Wk", [1024, 1024], F32),
           ("Wv", [1024, 1024], F32), ("masks", [128, 2048], F32)],
    "L3": [("q_w", [1024, 1024], F32), ("kv_w", [1024, 2048], F32),
           ("w_out", [1024, 1024], F32), ("q_b", [1, 1024], F32),
           ("kv_b", [1, 2048], F32), ("o_b", [1, 1024], F32),
           ("slopes", [128, 16], F32), ("kaug", [2, T], BF16),
           ("qaugc", [2, 128], F32), ("masks", [128, 2048], F32)],
}


def build_attn_program(mode):
    from contextlib import ExitStack
    nc = bass.Bass("TRN2", target_bir_lowering=False, detect_race_conditions=False)
    x_in = nc.dram_tensor("x", [T, 1024], BF16, kind="ExternalInput").ap()
    x_out = nc.dram_tensor("y", [T, 1024], BF16, kind="ExternalOutput").ap()
    io = {"x_in": x_in, "x_out": x_out}
    for nm, shp, dt in ATTN_W[mode]:
        io[nm] = nc.dram_tensor("w_" + nm, shp, dt, kind="ExternalInput").ap()
    ctx = ExitStack()
    with ctx:
        E = ctx.enter_context
        cst_np, cst_aps, cst_sb = const_tensors(nc, E)
        blk_obj = Attn(nc, E, "a_", mode)
        sems = {}
        for nm in ("m_ld", "g_ld", "m_t", "m_v", "m_s", "m_qk", "m_av", "g_x", "g_st", "st"):
            sems[nm] = E(nc.semaphore(nm))
        blk = E(nc.Block())
        k = K(nc, sems, {})

        def emit_all(k):
            for nm in cst_sb:
                k.op("sync", lambda sy, nm=nm: sy.dma_start(out=cst_sb[nm][:], in_=cst_aps[nm]),
                     inc="m_ld", inc_n=16)
            cst = {nm: cst_sb[nm][:] for nm in cst_sb}
            blk_obj.emit(k, io, cst)

        def section(name):
            def f(eng):
                k.run(name, eng, [emit_all])
            return f
        blk.sync(section("sync"))
        blk.gpsimd(section("gpsimd"))
        blk.tensor(section("tensor"))
        blk.vector(section("vector"))
        blk.scalar(section("scalar"))
    return nc, cst_np


def run_attn_test(x_bf16, io_np, mode, sim=False):
    nc, cst_np = build_attn_program(mode)
    in_map = {"x": x_bf16}
    for nm, arr in io_np.items():
        in_map["w_" + nm] = arr
    for nm, arr in cst_np.items():
        in_map["c_" + nm] = arr
    if sim:
        from concourse.bass_interp import MultiCoreSim
        simu = MultiCoreSim(nc, num_cores=1, num_workers=1, trace=False)
        for kk, v in in_map.items():
            simu.cores[0].tensor(kk)[:] = v
        simu.simulate(check_with_hw=False)
        return np.asarray(simu.cores[0].tensor("y"))
    res = run_bass_kernel_spmd(nc, [in_map], core_ids=[0])
    return np.asarray(res.results[0]["y"])


class L1FFN:
    CH = 256

    def __init__(self, nc, E, pfx):
        s = self.sb = {}
        C = self.CH
        s["rw"] = E(nc.sbuf_tensor(pfx + "rw", [128, 8 * 32], BF16))
        s["rb"] = E(nc.sbuf_tensor(pfx + "rb", [1, 32], BF16))
        s["sel"] = E(nc.sbuf_tensor(pfx + "sel", [32, 32 * 128], BF16))
        s["selbuf"] = E(nc.sbuf_tensor(pfx + "selbuf", [32, 128], BF16))
        s["wg"] = E(nc.sbuf_tensor(pfx + "wg", [128, 2 * 1024], BF16))
        s["wu"] = E(nc.sbuf_tensor(pfx + "wu", [128, 2 * 1024], BF16))
        s["wd"] = E(nc.sbuf_tensor(pfx + "wd", [128, 2 * 1024], BF16))
        s["wsg"] = E(nc.sbuf_tensor(pfx + "wsg", [128, 8 * 512], BF16))
        s["wsu"] = E(nc.sbuf_tensor(pfx + "wsu", [128, 8 * 512], BF16))
        s["wsd"] = E(nc.sbuf_tensor(pfx + "wsd", [128, 4 * 1024], BF16))
        s["xstg"] = E(nc.sbuf_tensor(pfx + "xstg", [128, 2 * 1024], BF16))
        s["xTc"] = E(nc.sbuf_tensor(pfx + "xTc", [128, 8 * C], BF16))
        s["ostg"] = E(nc.sbuf_tensor(pfx + "ostg", [128, 2 * 1024], BF16))
        s["aff"] = E(nc.sbuf_tensor(pfx + "aff", [128, 32], F32))
        s["wtok"] = E(nc.sbuf_tensor(pfx + "wtok", [128, 32], BF16))
        s["wT"] = E(nc.sbuf_tensor(pfx + "wT", [32, C], BF16))
        s["wbc"] = E(nc.sbuf_tensor(pfx + "wbc", [128, C], BF16))
        s["top8"] = E(nc.sbuf_tensor(pfx + "top8", [128, 8], F32))
        s["sg"] = E(nc.sbuf_tensor(pfx + "sg", [128, C], BF16))
        s["hp"] = E(nc.sbuf_tensor(pfx + "hp", [128, C], BF16))
        s["hs"] = E(nc.sbuf_tensor(pfx + "hs", [128, 4 * C], BF16))
        s["rows"] = E(nc.sbuf_tensor(pfx + "rows", [128, 8], F32))
        s["pstat"] = E(nc.sbuf_tensor(pfx + "pstat", [128, 8], F32))
        s["sq_a"] = E(nc.sbuf_tensor(pfx + "sq_a", [128, 512], BF16))
        s["sq_b"] = E(nc.sbuf_tensor(pfx + "sq_b", [128, 512], BF16))
        s["epsv"] = E(nc.sbuf_tensor(pfx + "epsv", [128, 1], F32))
        self.ps_tr = E(nc.psum_tensor(pfx + "ps_tr", [128, 512], BF16))
        self.ps_g = E(nc.psum_tensor(pfx + "ps_g", [128, 512], F32))
        self.ps_u = E(nc.psum_tensor(pfx + "ps_u", [128, 512], F32))
        self.ps_r = E(nc.psum_tensor(pfx + "ps_r", [128, 512], F32))
        self.ps_o = [E(nc.psum_tensor(pfx + f"pso{i}", [128, 512], F32)) for i in range(4)]

    def emit(self, k, io, cst):
        s = self.sb
        C = self.CH
        ident = cst["ident_bf"]
        k.op("gpsimd", lambda g: g.dma_start(
            out=s["rw"][:].rearrange("p (kt e) -> p kt e", kt=8),
            in_=io["router_wT"].rearrange("(kt p) e -> p kt e", p=128)), inc="g_ld", inc_n=16)
        k.op("gpsimd", lambda g: g.dma_start(out=s["rb"][:], in_=io["router_bias"]),
             inc="g_ld", inc_n=16)
        k.op("sync", lambda sy: sy.dma_start(out=s["sel"][:], in_=io["sel32d"]),
             inc="m_ld", inc_n=16)
        for nm, src in (("wsg", "sgateT"), ("wsu", "supT")):
            k.op("gpsimd", lambda g, nm=nm, src=src: g.dma_start(
                out=s[nm][:].rearrange("p (kt e) -> p kt e", kt=8),
                in_=io[src].rearrange("(kt p) e -> p kt e", p=128)), inc="g_ld", inc_n=16)
        k.op("gpsimd", lambda g: g.dma_start(
            out=s["wsd"][:].rearrange("p (kt e) -> p kt e", kt=4),
            in_=io["sdownT"].rearrange("(kt p) e -> p kt e", p=128)), inc="g_ld", inc_n=16)
        for e in ("tensor", "vector", "scalar", "gpsimd", "sync"):
            k.wait(e, "g_ld", k.cnt["g_ld"])
            k.wait(e, "m_ld", k.cnt["m_ld"])
        k.op("vector", lambda v: v.memset(s["epsv"][:], 1e-5), inc="m_v")

        def chunk_body(ch):
            k.op("gpsimd", lambda g: g.dma_start(
                out=s["xstg"][:].rearrange("p (a d) -> p a d", a=2),
                in_=io["x_in"].rearrange("(t p) d -> p t d", p=128)[:, DynSlice(ch * 2, 2), :]),
                wait=[("m_t", k.cnt["m_t"]), ("m_s", k.cnt["m_s"])], inc="g_x", inc_n=16)
            gx = k.cnt["g_x"]
            for dt in range(8):
                w = [("g_x", gx)] if dt == 0 else []
                w.append(("m_v", k.cnt["m_v"]))
                for tt in range(2):
                    k.op("tensor", lambda t, dt=dt, tt=tt: t.transpose(
                        self.ps_tr[:, tt * 128:(tt + 1) * 128],
                        s["xstg"][:, tt * 1024 + dt * 128: tt * 1024 + (dt + 1) * 128],
                        ident[:]),
                        wait=(w if tt == 0 else []),
                        inc=("m_t" if tt == 1 else None))
                k.op("vector", lambda v, dt=dt: v.tensor_copy(
                    s["xTc"][:, dt * C:(dt + 1) * C], self.ps_tr[:, 0:C]),
                    wait=[("m_t", k.cnt["m_t"])], inc="m_v")
            for tt in range(2):
                for kt in range(8):
                    k.op("tensor", lambda t, tt=tt, kt=kt: t.matmul(
                        self.ps_r[:, 0:32],
                        s["xTc"][:, kt * C + tt * 128: kt * C + (tt + 1) * 128],
                        s["rw"][:, kt * 32:(kt + 1) * 32],
                        start=(kt == 0), stop=False),
                        wait=([("m_v", k.cnt["m_v"]), ("m_s", k.cnt["m_s"])] if kt == 0 else []),
                        inc=None)
                k.op("tensor", lambda t: t.matmul(
                    self.ps_r[:, 0:32], cst["ones1x128"][0:1, :], s["rb"][0:1, :],
                    start=False, stop=True), inc="m_t")
                k.op("vector", lambda v: v.tensor_reduce(
                    s["rows"][:, 0:1], self.ps_r[:, 0:32], axis=AX.X, op=ALU.max, negate=True),
                    wait=[("m_t", k.cnt["m_t"])], inc="m_v")
                k.op("scalar", lambda sc: sc.activation(
                    s["aff"][:], self.ps_r[:, 0:32], AF.Exp, bias=s["rows"][:, 0:1],
                    accum_out=s["rows"][:, 1:2]),
                    wait=[("m_v", k.cnt["m_v"])], inc="m_s")
                k.op("vector", lambda v: v.reciprocal(s["rows"][:, 2:3], s["rows"][:, 1:2]),
                     wait=[("m_s", k.cnt["m_s"])], inc="m_v")
                k.op("vector", lambda v: v.tensor_scalar(
                    s["aff"][:], s["aff"][:], s["rows"][:, 2:3], None, op0=ALU.mult),
                    wait=[("m_v", k.cnt["m_v"])], inc="m_v")
                k.op("vector", lambda v: v.max(out=s["top8"][:], in_=s["aff"][:]),
                     wait=[("m_v", k.cnt["m_v"])], inc="m_v")
                k.op("vector", lambda v: v.tensor_scalar(
                    s["wtok"][:], s["aff"][:], s["top8"][:, 3:4], None, op0=ALU.is_ge),
                    wait=[("m_v", k.cnt["m_v"])], inc="m_v")
                k.op("vector", lambda v: v.tensor_tensor(
                    s["wtok"][:], s["wtok"][:], s["aff"][:], op=ALU.mult),
                    wait=[("m_v", k.cnt["m_v"])], inc="m_v")
                k.op("vector", lambda v: v.tensor_reduce(
                    s["rows"][:, 3:4], s["wtok"][:], axis=AX.X, op=ALU.add),
                    wait=[("m_v", k.cnt["m_v"])], inc="m_v")
                k.op("vector", lambda v: v.tensor_scalar_max(
                    s["rows"][:, 3:4], s["rows"][:, 3:4], 1e-12),
                    wait=[("m_v", k.cnt["m_v"])], inc="m_v")
                k.op("vector", lambda v: v.reciprocal(s["rows"][:, 4:5], s["rows"][:, 3:4]),
                     wait=[("m_v", k.cnt["m_v"])], inc="m_v")
                k.op("vector", lambda v: v.tensor_scalar(
                    s["wtok"][:], s["wtok"][:], s["rows"][:, 4:5], None, op0=ALU.mult),
                    wait=[("m_v", k.cnt["m_v"])], inc="m_v")
                k.op("tensor", lambda t, tt=tt: t.transpose(
                    self.ps_tr[0:32, tt * 128:(tt + 1) * 128],
                    s["wtok"][:, 0:32], ident[:]),
                    wait=[("m_v", k.cnt["m_v"])], inc="m_t")
            k.op("vector", lambda v: v.tensor_copy(s["wT"][0:32, 0:C], self.ps_tr[0:32, 0:C]),
                 wait=[("m_t", k.cnt["m_t"])], inc="m_v")
            wt_v = k.cnt["m_v"]

            def pair_body(ep):
                for par in range(2):
                    eidx = ep * 2 + par
                    k.op("gpsimd", lambda g, par=par, eidx=eidx: g.dma_start(
                        out=s["wg"][:, par * 1024:(par + 1) * 1024].rearrange("p (o kt c) -> p o kt c", o=1, kt=8),
                        in_=io["gateT"].rearrange("eo (kt p) c -> p eo kt c", p=128)[:, DynSlice(eidx, 1), :, :]),
                        wait=[("m_t", k.cnt["m_t"])], inc="g_w", inc_n=16)
                    k.op("gpsimd", lambda g, par=par, eidx=eidx: g.dma_start(
                        out=s["wu"][:, par * 1024:(par + 1) * 1024].rearrange("p (o kt c) -> p o kt c", o=1, kt=8),
                        in_=io["upT"].rearrange("eo (kt p) c -> p eo kt c", p=128)[:, DynSlice(eidx, 1), :, :]),
                        inc="g_w", inc_n=16)
                    k.op("gpsimd", lambda g, par=par, eidx=eidx: g.dma_start(
                        out=s["wd"][:, par * 1024:(par + 1) * 1024].rearrange("p (o c) -> p o c", o=1),
                        in_=io["downT"].rearrange("eo p c -> p eo c")[:, DynSlice(eidx, 1), :]),
                        inc="g_w", inc_n=16)
                    gw = k.cnt["g_w"]
                    k.op("vector", lambda v, eidx=eidx: v.tensor_copy(
                        s["selbuf"][:], s["sel"][0:32, DynSlice(eidx * 128, 128)]),
                        wait=[("m_t", k.cnt["m_t"])], inc="m_v")
                    k.op("tensor", lambda t: t.matmul(
                        self.ps_r[:, 0:C], s["selbuf"][0:32, 0:128],
                        s["wT"][0:32, 0:C], start=True, stop=True),
                        wait=[("m_v", k.cnt["m_v"])], inc="m_t")
                    k.op("vector", lambda v: v.tensor_copy(s["wbc"][:], self.ps_r[:, 0:C]),
                         wait=[("m_t", k.cnt["m_t"])], inc="m_v")
                    for which, bank in (("wg", self.ps_g), ("wu", self.ps_u)):
                        for kt in range(8):
                            k.op("tensor", lambda t, which=which, par=par, kt=kt, bank=bank: t.matmul(
                                bank[:, 0:C],
                                s[which][:, par * 1024 + kt * 128: par * 1024 + (kt + 1) * 128],
                                s["xTc"][:, kt * C:(kt + 1) * C],
                                start=(kt == 0), stop=(kt == 7)),
                                wait=([("g_w", gw), ("m_v", k.cnt["m_v"]),
                                       ("m_s", k.cnt["m_s"])] if kt == 0 else []),
                                inc=("m_t" if kt == 7 else None))
                    k.op("scalar", lambda sc: sc.activation(
                        s["sg"][:], self.ps_g[:, 0:C], AF.Silu),
                        wait=[("m_t", k.cnt["m_t"])], inc="m_s")
                    k.op("vector", lambda v: v.tensor_tensor(
                        s["hp"][:], s["sg"][:], self.ps_u[:, 0:C], op=ALU.mult),
                        wait=[("m_s", k.cnt["m_s"]), ("m_v", k.cnt["m_v"])], inc="m_v")
                    k.op("vector", lambda v: v.tensor_tensor(
                        s["hp"][:], s["hp"][:], s["wbc"][:], op=ALU.mult),
                        wait=[("m_v", k.cnt["m_v"])], inc="m_v")
                    hp_v = k.cnt["m_v"]
                    for tt in range(2):
                        for half in range(2):
                            k.op("tensor", lambda t, tt=tt, half=half, par=par: t.matmul(
                                self.ps_o[tt * 2 + half][:, 0:512],
                                s["hp"][:, tt * 128:(tt + 1) * 128],
                                s["wd"][:, par * 1024 + half * 512: par * 1024 + (half + 1) * 512],
                                start=False, stop=False),
                                wait=([("m_v", hp_v)] if (tt == 0 and half == 0) else []),
                                inc=("m_t" if (tt == 1 and half == 1) else None))

            z_v = k.cnt["m_v"]
            for i in range(4):
                k.op("tensor", lambda t, i=i: t.matmul(
                    self.ps_o[i][:, 0:512], cst["ones1x128"][0:1, :],
                    cst["zeros1x512"][0:1, :], start=True, stop=False),
                    wait=([("m_v", z_v), ("m_s", k.cnt["m_s"])] if i == 0 else []), inc=None)
            k.autoloop(16, pair_body, engines=("tensor", "vector", "scalar", "gpsimd"))
            for mt in range(4):
                for which, bank in (("wsg", self.ps_g), ("wsu", self.ps_u)):
                    for kt in range(8):
                        k.op("tensor", lambda t, mt=mt, which=which, kt=kt, bank=bank: t.matmul(
                            bank[:, 0:C],
                            s[which][:, kt * 512 + mt * 128: kt * 512 + (mt + 1) * 128],
                            s["xTc"][:, kt * C:(kt + 1) * C],
                            start=(kt == 0), stop=(kt == 7)),
                            wait=([("m_v", k.cnt["m_v"]), ("m_s", k.cnt["m_s"])] if kt == 0 else []),
                            inc=("m_t" if kt == 7 else None))
                k.op("scalar", lambda sc, mt=mt: sc.activation(
                    s["sg"][:], self.ps_g[:, 0:C], AF.Silu),
                    wait=[("m_t", k.cnt["m_t"])], inc="m_s")
                k.op("vector", lambda v, mt=mt: v.tensor_tensor(
                    s["hs"][:, mt * C:(mt + 1) * C], s["sg"][:], self.ps_u[:, 0:C], op=ALU.mult),
                    wait=[("m_s", k.cnt["m_s"]), ("m_v", k.cnt["m_v"])], inc="m_v")
            hs_v = k.cnt["m_v"]
            for tt in range(2):
                for half in range(2):
                    for kt in range(4):
                        k.op("tensor", lambda t, tt=tt, half=half, kt=kt: t.matmul(
                            self.ps_o[tt * 2 + half][:, 0:512],
                            s["hs"][:, kt * C + tt * 128: kt * C + (tt + 1) * 128],
                            s["wsd"][:, kt * 1024 + half * 512: kt * 1024 + (half + 1) * 512],
                            start=False, stop=False),
                            wait=([("m_v", hs_v)] if (tt == 0 and half == 0 and kt == 0) else []),
                            inc=None)
                    k.op("tensor", lambda t, tt=tt, half=half: t.matmul(
                        self.ps_o[tt * 2 + half][:, 0:512], ident[:],
                        s["xstg"][:, tt * 1024 + half * 512: tt * 1024 + (half + 1) * 512],
                        start=False, stop=True),
                        inc="m_t")
            for tt in range(2):
                emit_postln(k, s, [self.ps_o[tt * 2][:, 0:512], self.ps_o[tt * 2 + 1][:, 0:512]],
                            [s["ostg"][:, tt * 1024: tt * 1024 + 512],
                             s["ostg"][:, tt * 1024 + 512: tt * 1024 + 1024]],
                            "m_t", k.cnt["m_t"])
            k.op("gpsimd", lambda g: g.dma_start(
                out=io["x_out"].rearrange("(t p) d -> p t d", p=128)[:, DynSlice(ch * 2, 2), :],
                in_=s["ostg"][:].rearrange("p (a d) -> p a d", a=2)),
                wait=[("m_s", k.cnt["m_s"])], inc="g_st", inc_n=16)

        k.autoloop(16, chunk_body)
        k.wait("sync", "g_st", k.cnt["g_st"])


def l1ffn_host_prep(inp):
    f = np.ascontiguousarray
    return {
        "router_wT": f(inp["L1_ffn_router_w"].T),
        "router_bias": f(inp["L1_ffn_router_bias"].reshape(1, 32)),
        "gateT": f(np.transpose(inp["L1_ffn_gate_w"], (0, 2, 1))),
        "upT": f(np.transpose(inp["L1_ffn_up_w"], (0, 2, 1))),
        "downT": f(np.transpose(inp["L1_ffn_down_w"], (0, 2, 1))),
        "sgateT": f(inp["L1_ffn_sgate_w"].T),
        "supT": f(inp["L1_ffn_sup_w"].T),
        "sdownT": f(inp["L1_ffn_sdown_w"].T),
        "sel32d": make_consts()["sel32"],
    }


L1FFN_W = [("router_wT", [1024, 32], F32), ("router_bias", [1, 32], F32),
           ("gateT", [32, 1024, 128], F32), ("upT", [32, 1024, 128], F32),
           ("downT", [32, 128, 1024], F32), ("sgateT", [1024, 512], F32),
           ("supT", [1024, 512], F32), ("sdownT", [512, 1024], F32),
           ("sel32d", [32, 4096], BF16)]


def build_l1ffn_program():
    from contextlib import ExitStack
    nc = bass.Bass("TRN2", target_bir_lowering=False, detect_race_conditions=False)
    x_in = nc.dram_tensor("x", [T, 1024], BF16, kind="ExternalInput").ap()
    x_out = nc.dram_tensor("y", [T, 1024], BF16, kind="ExternalOutput").ap()
    io = {"x_in": x_in, "x_out": x_out}
    for nm, shp, dt in L1FFN_W:
        io[nm] = nc.dram_tensor("w_" + nm, shp, dt, kind="ExternalInput").ap()
    ctx = ExitStack()
    with ctx:
        E = ctx.enter_context
        cst_np, cst_aps, cst_sb = const_tensors(nc, E)
        blk_obj = L1FFN(nc, E, "f1_")
        sems = {}
        for nm in ("m_ld", "g_ld", "m_t", "m_v", "m_s", "g_x", "g_w", "g_st", "st"):
            sems[nm] = E(nc.semaphore(nm))
        blk = E(nc.Block())
        k = K(nc, sems, {})

        def emit_all(k):
            for nm in cst_sb:
                k.op("sync", lambda sy, nm=nm: sy.dma_start(out=cst_sb[nm][:], in_=cst_aps[nm]),
                     inc="m_ld", inc_n=16)
            cst = {nm: cst_sb[nm][:] for nm in cst_sb}
            blk_obj.emit(k, io, cst)

        def section(name):
            def fny(eng):
                k.run(name, eng, [emit_all])
            return fny
        blk.sync(section("sync"))
        blk.gpsimd(section("gpsimd"))
        blk.tensor(section("tensor"))
        blk.vector(section("vector"))
        blk.scalar(section("scalar"))
    return nc, cst_np


def run_l1ffn_test(x_bf16, io_np, sim=False):
    nc, cst_np = build_l1ffn_program()
    in_map = {"x": x_bf16}
    for nm, arr in io_np.items():
        in_map["w_" + nm] = arr
    for nm, arr in cst_np.items():
        in_map["c_" + nm] = arr
    if sim:
        from concourse.bass_interp import MultiCoreSim
        simu = MultiCoreSim(nc, num_cores=1, num_workers=1, trace=False)
        for kk, v in in_map.items():
            simu.cores[0].tensor(kk)[:] = v
        simu.simulate(check_with_hw=False)
        return np.asarray(simu.cores[0].tensor("y"))
    res = run_bass_kernel_spmd(nc, [in_map], core_ids=[0])
    return np.asarray(res.results[0]["y"])


class L23FFN:

    def __init__(self, nc, E, pfx, mode, final_f32=False):
        self.mode = mode
        self.final_f32 = final_f32
        self.I = 2048 if mode == "L2" else 4096
        s = self.sb = {}
        self.scr = nc.dram_tensor(pfx + "acc_scr", [T, 1024], F32, kind="Internal").ap()
        s["gw"] = E(nc.sbuf_tensor(pfx + "gw", [128, 8 * 8], BF16))
        s["w1"] = E(nc.sbuf_tensor(pfx + "w1", [128, 8 * (self.I)], BF16))
        if mode == "L2":
            s["wu"] = E(nc.sbuf_tensor(pfx + "wu", [128, 8 * self.I], BF16))
        s["w2"] = E(nc.sbuf_tensor(pfx + "w2", [128, (self.I // 128) * 1024], BF16))
        s["xstg"] = E(nc.sbuf_tensor(pfx + "xstg", [128, 2 * 1024], BF16))
        s["xTc"] = E(nc.sbuf_tensor(pfx + "xTc", [128, 8 * 256], BF16))
        s["h"] = E(nc.sbuf_tensor(pfx + "h", [128, (self.I // 128) * 256], BF16))
        s["stage"] = E(nc.sbuf_tensor(pfx + "stage", [128, 2 * 1024], F32))
        s["ostg"] = E(nc.sbuf_tensor(pfx + "ostg", [128, 2 * 1024],
                                     F32 if final_f32 else BF16))
        s["affs"] = E(nc.sbuf_tensor(pfx + "affs", [128, 8], F32))
        s["wgt_tm"] = E(nc.sbuf_tensor(pfx + "wgt_tm", [128, 32 * 8], F32))
        s["wstage"] = E(nc.sbuf_tensor(pfx + "wstage", [128, 2], F32))
        s["affT"] = E(nc.sbuf_tensor(pfx + "affT", [8, T], F32))
        s["cmp"] = E(nc.sbuf_tensor(pfx + "cmp", [8, T], BF16))
        s["thr"] = E(nc.sbuf_tensor(pfx + "thr", [8, 8], F32))
        s["rows"] = E(nc.sbuf_tensor(pfx + "rows", [128, 8], F32))
        s["pstat"] = E(nc.sbuf_tensor(pfx + "pstat", [128, 8], F32))
        s["sq_a"] = E(nc.sbuf_tensor(pfx + "sq_a", [128, 512], BF16))
        s["sq_b"] = E(nc.sbuf_tensor(pfx + "sq_b", [128, 512], BF16))
        s["epsv"] = E(nc.sbuf_tensor(pfx + "epsv", [128, 1], F32))
        self.ps_tr = E(nc.psum_tensor(pfx + "ps_tr", [128, 512], BF16))
        self.ps_trf = E(nc.psum_tensor(pfx + "ps_trf", [128, 128], F32))
        self.ps_g = E(nc.psum_tensor(pfx + "ps_g", [128, 512], F32))
        self.ps_u = E(nc.psum_tensor(pfx + "ps_u", [128, 512], F32))
        self.ps_o = [E(nc.psum_tensor(pfx + f"pso{i}", [128, 512], F32)) for i in range(2)]
        self.ps_r = E(nc.psum_tensor(pfx + "ps_r", [128, 512], F32))

    def _xTc(self, k, io, cst, ch, gate_t=("m_t",)):
        s = self.sb
        ident = cst["ident_bf"]
        k.op("gpsimd", lambda g: g.dma_start(
            out=s["xstg"][:].rearrange("p (a d) -> p a d", a=2),
            in_=io["x_in"].rearrange("(t p) d -> p t d", p=128)[:, DynSlice(ch * 2, 2), :]),
            wait=[("m_t", k.cnt["m_t"]), ("m_s", k.cnt["m_s"])], inc="g_x", inc_n=16)
        gx = k.cnt["g_x"]
        for dt in range(8):
            w = [("g_x", gx)] if dt == 0 else []
            w.append(("m_v", k.cnt["m_v"]))
            for tt4 in range(2):
                k.op("tensor", lambda t, dt=dt, tt4=tt4: t.transpose(
                    self.ps_tr[:, tt4 * 128:(tt4 + 1) * 128],
                    s["xstg"][:, tt4 * 1024 + dt * 128: tt4 * 1024 + (dt + 1) * 128],
                    ident[:]),
                    wait=(w if tt4 == 0 else []),
                    inc=("m_t" if tt4 == 1 else None))
            k.op("vector", lambda v, dt=dt: v.tensor_copy(
                s["xTc"][:, dt * 256:(dt + 1) * 256], self.ps_tr[:, 0:256]),
                wait=[("m_t", k.cnt["m_t"])], inc="m_v")
        return gx

    def emit(self, k, io, cst):
        s = self.sb
        mode = self.mode
        I = self.I
        NI = I // 128
        ident = cst["ident_bf"]
        k.op("gpsimd", lambda g: g.dma_start(
            out=s["gw"][:].rearrange("p (kt e) -> p kt e", kt=8),
            in_=io["gate_w"].rearrange("(kt p) e -> p kt e", p=128)), inc="g_ld", inc_n=16)
        for e in ("tensor", "vector", "scalar", "gpsimd", "sync"):
            k.wait(e, "g_ld", k.cnt["g_ld"])
        k.op("vector", lambda v: v.memset(s["epsv"][:], 1e-5), inc="m_v")

        def rbody(ch):
            self._xTc(k, io, cst, ch)
            for tt4 in range(2):
                for kt in range(8):
                    k.op("tensor", lambda t, tt4=tt4, kt=kt: t.matmul(
                        self.ps_r[:, 0:8],
                        s["xTc"][:, kt * 256 + tt4 * 128: kt * 256 + (tt4 + 1) * 128],
                        s["gw"][:, kt * 8:(kt + 1) * 8],
                        start=(kt == 0), stop=(kt == 7)),
                        wait=([("m_v", k.cnt["m_v"])] if kt == 0 else []),
                        inc=("m_t" if kt == 7 else None))
                k.op("vector", lambda v: v.tensor_reduce(
                    s["rows"][:, 0:1], self.ps_r[:, 0:8], axis=AX.X, op=ALU.max, negate=True),
                    wait=[("m_t", k.cnt["m_t"])], inc="m_v")
                k.op("scalar", lambda sc, tt4=tt4: sc.activation(
                    s["affs"][:], self.ps_r[:, 0:8],
                    AF.Exp, bias=s["rows"][:, 0:1], accum_out=s["rows"][:, 1:2]),
                    wait=[("m_v", k.cnt["m_v"])], inc="m_s")
                k.op("vector", lambda v: v.reciprocal(s["rows"][:, 2:3], s["rows"][:, 1:2]),
                     wait=[("m_s", k.cnt["m_s"])], inc="m_v")
                k.op("vector", lambda v, tt4=tt4: v.tensor_scalar(
                    s["affs"][:], s["affs"][:],
                    s["rows"][:, 2:3], None, op0=ALU.mult),
                    wait=[("m_v", k.cnt["m_v"])], inc="m_v")
                k.op("tensor", lambda t, tt4=tt4: t.transpose(
                    self.ps_trf[0:8, 0:128],
                    s["affs"][:], cst["ident_f32"][:]),
                    wait=[("m_v", k.cnt["m_v"])], inc="m_t")
                k.op("vector", lambda v, tt4=tt4: v.tensor_copy(
                    s["affT"][0:8, DynSlice((ch * 2 + tt4) * 128, 128)],
                    self.ps_trf[0:8, 0:128]),
                    wait=[("m_t", k.cnt["m_t"])], inc="m_v")
        k.autoloop(16, rbody)

        th = s["thr"]
        k.op("vector", lambda v: v.memset(th[0:8, 0:1], 0.0),
             wait=[("m_v", k.cnt["m_v"])], inc="m_v")
        k.op("vector", lambda v: v.memset(th[0:8, 1:2], 1.0),
             wait=[("m_v", k.cnt["m_v"])], inc="m_v")
        for it in range(25):
            k.op("vector", lambda v: v.tensor_tensor(th[0:8, 2:3], th[0:8, 0:1], th[0:8, 1:2], op=ALU.add),
                 wait=[("m_v", k.cnt["m_v"])], inc="m_v")
            k.op("vector", lambda v: v.tensor_scalar_mul(th[0:8, 2:3], th[0:8, 2:3], 0.5),
                 wait=[("m_v", k.cnt["m_v"])], inc="m_v")
            k.op("vector", lambda v: v.tensor_scalar(
                s["cmp"][0:8, :], s["affT"][0:8, :], th[0:8, 2:3], None, op0=ALU.is_ge),
                wait=[("m_v", k.cnt["m_v"])], inc="m_v")
            k.op("vector", lambda v: v.tensor_reduce(
                th[0:8, 3:4], s["cmp"][0:8, :], axis=AX.X, op=ALU.add),
                wait=[("m_v", k.cnt["m_v"])], inc="m_v")
            k.op("vector", lambda v: v.tensor_scalar(
                th[0:8, 4:5], th[0:8, 3:4], 1024.0, None, op0=ALU.is_ge),
                wait=[("m_v", k.cnt["m_v"])], inc="m_v")
            k.op("vector", lambda v: v.tensor_tensor(
                th[0:8, 6:7], th[0:8, 2:3], th[0:8, 0:1], op=ALU.subtract),
                wait=[("m_v", k.cnt["m_v"])], inc="m_v")
            k.op("vector", lambda v: v.scalar_tensor_tensor(
                th[0:8, 0:1], th[0:8, 6:7], th[0:8, 4:5], th[0:8, 0:1],
                op0=ALU.mult, op1=ALU.add),
                wait=[("m_v", k.cnt["m_v"])], inc="m_v")
            k.op("vector", lambda v: v.tensor_tensor(
                th[0:8, 6:7], th[0:8, 1:2], th[0:8, 2:3], op=ALU.subtract),
                wait=[("m_v", k.cnt["m_v"])], inc="m_v")
            k.op("vector", lambda v: v.scalar_tensor_tensor(
                th[0:8, 1:2], th[0:8, 6:7], th[0:8, 4:5], th[0:8, 2:3],
                op0=ALU.mult, op1=ALU.add),
                wait=[("m_v", k.cnt["m_v"])], inc="m_v")
        k.op("vector", lambda v: v.tensor_copy(th[0:8, 5:6], th[0:8, 0:1]),
             wait=[("m_v", k.cnt["m_v"])], inc="m_v")
        k.op("vector", lambda v: v.tensor_scalar(
            s["cmp"][0:8, :], s["affT"][0:8, :], th[0:8, 5:6], None, op0=ALU.is_ge),
            wait=[("m_v", k.cnt["m_v"])], inc="m_v")
        k.op("vector", lambda v: v.tensor_tensor(
            s["affT"][0:8, :], s["affT"][0:8, :], s["cmp"][0:8, :], op=ALU.mult),
            wait=[("m_v", k.cnt["m_v"])], inc="m_v")
        wg_v = k.cnt["m_v"]
        for tt in range(32):
            k.op("tensor", lambda t, tt=tt: t.transpose(
                self.ps_trf[:, 0:8],
                s["affT"][0:8, tt * 128:(tt + 1) * 128], cst["ident_f32"][0:8, 0:8]),
                wait=[("m_v", k.cnt["m_v"])], inc="m_t")
            k.op("vector", lambda v, tt=tt: v.tensor_copy(
                s["wgt_tm"][:, tt * 8:(tt + 1) * 8], self.ps_trf[:, 0:8]),
                wait=[("m_t", k.cnt["m_t"])], inc="m_v")

        def expert_block(e, first):
            k.op("gpsimd", lambda g, e=e: g.dma_start(
                out=s["w1"][:].rearrange("p (o kt c) -> p o kt c", o=1, kt=8),
                in_=io["w1"].rearrange("eo (kt p) c -> p eo kt c", p=128)[:, DynSlice(e, 1), :, :]),
                wait=[("m_t", k.cnt["m_t"])], inc="g_w", inc_n=16)
            if mode == "L2":
                k.op("gpsimd", lambda g, e=e: g.dma_start(
                    out=s["wu"][:].rearrange("p (o kt c) -> p o kt c", o=1, kt=8),
                    in_=io["wu"].rearrange("eo (kt p) c -> p eo kt c", p=128)[:, DynSlice(e, 1), :, :]),
                    inc="g_w", inc_n=16)
            k.op("gpsimd", lambda g, e=e: g.dma_start(
                out=s["w2"][:].rearrange("p (o kt c) -> p o kt c", o=1, kt=NI),
                in_=io["w2"].rearrange("eo (kt p) c -> p eo kt c", p=128)[:, DynSlice(e, 1), :, :]),
                inc="g_w", inc_n=16)
            gw = k.cnt["g_w"]

            def ebody(ch):
                self._xTc(k, io, cst, ch)
                xt_v = k.cnt["m_v"]
                k.op("vector", lambda v: v.tensor_copy(
                    s["wstage"][:],
                    s["wgt_tm"][:].rearrange("p (t e) -> p t e", e=8)[:, DynSlice(ch * 2, 2), DynSlice(e, 1)].rearrange("p t e -> p (t e)")),
                    wait=[("m_v", k.cnt["m_v"])], inc="m_v")
                for mt in range(NI):
                    for kt in range(8):
                        k.op("tensor", lambda t, mt=mt, kt=kt: t.matmul(
                            self.ps_g[:, 0:256],
                            s["w1"][:, kt * I + mt * 128: kt * I + (mt + 1) * 128],
                            s["xTc"][:, kt * 256:(kt + 1) * 256],
                            start=(kt == 0), stop=(kt == 7)),
                            wait=([("g_w", gw), ("m_v", k.cnt["m_v"]),
                                   ("m_s", k.cnt["m_s"])] if kt == 0 else []),
                            inc=("m_t" if kt == 7 else None))
                    if mode == "L2":
                        for kt in range(8):
                            k.op("tensor", lambda t, mt=mt, kt=kt: t.matmul(
                                self.ps_u[:, 0:256],
                                s["wu"][:, kt * I + mt * 128: kt * I + (mt + 1) * 128],
                                s["xTc"][:, kt * 256:(kt + 1) * 256],
                                start=(kt == 0), stop=(kt == 7)),
                                inc=("m_t" if kt == 7 else None))
                        k.op("scalar", lambda sc, mt=mt: sc.activation(
                            s["sq_a"][:, 0:256], self.ps_g[:, 0:256], AF.Silu),
                            wait=[("m_t", k.cnt["m_t"])], inc="m_s")
                        k.op("vector", lambda v, mt=mt: v.tensor_tensor(
                            s["h"][:, mt * 256:(mt + 1) * 256], s["sq_a"][:, 0:256],
                            self.ps_u[:, 0:256], op=ALU.mult),
                            wait=[("m_s", k.cnt["m_s"]), ("m_v", k.cnt["m_v"])], inc="m_v")
                    else:
                        k.op("scalar", lambda sc, mt=mt: sc.activation(
                            s["h"][:, mt * 256:(mt + 1) * 256], self.ps_g[:, 0:256], AF.Gelu),
                            wait=[("m_t", k.cnt["m_t"])], inc="m_s")
                for tt in range(2):
                    for half in range(2):
                        bank = self.ps_o[(tt * 2 + half) % 2]
                        for kt in range(NI):
                            k.op("tensor", lambda t, tt=tt, half=half, kt=kt, bank=bank: t.matmul(
                                bank[:, 0:512],
                                s["h"][:, kt * 256 + tt * 128: kt * 256 + (tt + 1) * 128],
                                s["w2"][:, kt * 1024 + half * 512: kt * 1024 + (half + 1) * 512],
                                start=(kt == 0), stop=(kt == NI - 1)),
                                wait=([("m_v", k.cnt["m_v"]), ("m_s", k.cnt["m_s"])] if kt == 0 else []),
                                inc=("m_t" if kt == NI - 1 else None))
                        k.op("vector", lambda v, tt=tt, half=half, bank=bank: v.tensor_scalar(
                            s["stage"][:, tt * 1024 + half * 512: tt * 1024 + (half + 1) * 512],
                            bank[:, 0:512],
                            s["wstage"][:, tt:tt + 1], None,
                            op0=ALU.mult),
                            wait=[("m_t", k.cnt["m_t"])], inc="m_v")
                k.op("gpsimd", lambda g: g.dma_start(
                    out=self.scr.rearrange("(t p) d -> p t d", p=128)[:, DynSlice(ch * 2, 2), :],
                    in_=s["stage"][:].rearrange("p (a d) -> p a d", a=2),
                    accum_op=(ALU.bypass if first else ALU.add)),
                    wait=[("m_v", k.cnt["m_v"])], inc="g_st", inc_n=16)
            k.autoloop(16, ebody)
            for en in ("tensor", "vector", "scalar", "gpsimd", "sync"):
                k.wait(en, "g_st", k.cnt["g_st"])

        expert_block(0, True)
        k.autoloop(7, lambda e: expert_block(e + 1, False),
                   engines=("tensor", "vector", "scalar", "gpsimd", "sync"))

        def fbody(ch):
            k.op("gpsimd", lambda g: g.dma_start(
                out=s["xstg"][:].rearrange("p (a d) -> p a d", a=2),
                in_=io["x_in"].rearrange("(t p) d -> p t d", p=128)[:, DynSlice(ch * 2, 2), :]),
                wait=[("m_s", k.cnt["m_s"])], inc="g_x", inc_n=16)
            k.op("sync", lambda sy: sy.dma_start(
                out=s["stage"][:].rearrange("p (a d) -> p a d", a=2),
                in_=self.scr.rearrange("(t p) d -> p t d", p=128)[:, DynSlice(ch * 2, 2), :]),
                wait=[("m_s", k.cnt["m_s"])], inc="m_qk", inc_n=16)
            gx = k.cnt["g_x"]
            lq = k.cnt["m_qk"]
            for tt in range(2):
                for half in range(2):
                    k.op("vector", lambda v, tt=tt, half=half: v.tensor_tensor(
                        s["stage"][:, tt * 1024 + half * 512: tt * 1024 + (half + 1) * 512],
                        s["stage"][:, tt * 1024 + half * 512: tt * 1024 + (half + 1) * 512],
                        s["xstg"][:, tt * 1024 + half * 512: tt * 1024 + (half + 1) * 512],
                        op=ALU.add),
                        wait=([("g_x", gx), ("m_qk", lq)] if (tt == 0 and half == 0) else []),
                        inc="m_v")
                emit_postln(k, s, [s["stage"][:, tt * 1024: tt * 1024 + 512],
                                   s["stage"][:, tt * 1024 + 512: tt * 1024 + 1024]],
                            [s["ostg"][:, tt * 1024: tt * 1024 + 512],
                             s["ostg"][:, tt * 1024 + 512: tt * 1024 + 1024]],
                            "m_v", k.cnt["m_v"])
            k.op("gpsimd", lambda g: g.dma_start(
                out=io["x_out"].rearrange("(t p) d -> p t d", p=128)[:, DynSlice(ch * 2, 2), :],
                in_=s["ostg"][:].rearrange("p (a d) -> p a d", a=2)),
                wait=[("m_s", k.cnt["m_s"])], inc="g_st", inc_n=16)
        k.autoloop(16, fbody)
        k.wait("sync", "g_st", k.cnt["g_st"])


def l23ffn_host_prep(inp, mode):
    f = np.ascontiguousarray
    if mode == "L2":
        return {"gate_w": f(inp["L2_ffn_gate_w"]),
                "w1": f(inp["L2_ffn_Wg"]), "wu": f(inp["L2_ffn_Wu"]),
                "w2": f(inp["L2_ffn_Wd"])}
    return {"gate_w": f(inp["L3_ffn_gate_w"]),
            "w1": f(inp["L3_ffn_w1"]), "w2": f(inp["L3_ffn_w2"])}


L23FFN_W = {
    "L2": [("gate_w", [1024, 8], F32), ("w1", [8, 1024, 2048], F32),
           ("wu", [8, 1024, 2048], F32), ("w2", [8, 2048, 1024], F32)],
    "L3": [("gate_w", [1024, 8], F32), ("w1", [8, 1024, 4096], F32),
           ("w2", [8, 4096, 1024], F32)],
}


def build_l23ffn_program(mode, final_f32=False):
    from contextlib import ExitStack
    nc = bass.Bass("TRN2", target_bir_lowering=False, detect_race_conditions=False)
    x_in = nc.dram_tensor("x", [T, 1024], BF16, kind="ExternalInput").ap()
    x_out = nc.dram_tensor("y", [T, 1024], F32 if final_f32 else BF16,
                           kind="ExternalOutput").ap()
    io = {"x_in": x_in, "x_out": x_out}
    for nm, shp, dt in L23FFN_W[mode]:
        io[nm] = nc.dram_tensor("w_" + nm, shp, dt, kind="ExternalInput").ap()
    ctx = ExitStack()
    with ctx:
        E = ctx.enter_context
        cst_np, cst_aps, cst_sb = const_tensors(nc, E)
        blk_obj = L23FFN(nc, E, "f_", mode, final_f32=final_f32)
        sems = {}
        for nm in ("m_ld", "g_ld", "m_t", "m_v", "m_s", "m_qk", "g_x", "g_w", "g_st", "st"):
            sems[nm] = E(nc.semaphore(nm))
        blk = E(nc.Block())
        k = K(nc, sems, {})

        def emit_all(k):
            for nm in cst_sb:
                k.op("sync", lambda sy, nm=nm: sy.dma_start(out=cst_sb[nm][:], in_=cst_aps[nm]),
                     inc="m_ld", inc_n=16)
            for en in ("tensor", "vector", "scalar", "gpsimd", "sync"):
                k.wait(en, "m_ld", k.cnt["m_ld"])
            cst = {nm: cst_sb[nm][:] for nm in cst_sb}
            blk_obj.emit(k, io, cst)

        def section(name):
            def fny(eng):
                k.run(name, eng, [emit_all])
            return fny
        blk.sync(section("sync"))
        blk.gpsimd(section("gpsimd"))
        blk.tensor(section("tensor"))
        blk.vector(section("vector"))
        blk.scalar(section("scalar"))
    return nc, cst_np


def run_l23ffn_test(x_bf16, io_np, mode, sim=False):
    nc, cst_np = build_l23ffn_program(mode)
    in_map = {"x": x_bf16}
    for nm, arr in io_np.items():
        in_map["w_" + nm] = arr
    for nm, arr in cst_np.items():
        in_map["c_" + nm] = arr
    if sim:
        from concourse.bass_interp import MultiCoreSim
        simu = MultiCoreSim(nc, num_cores=1, num_workers=1, trace=False)
        for kk, v in in_map.items():
            simu.cores[0].tensor(kk)[:] = v
        simu.simulate(check_with_hw=False)
        return np.asarray(simu.cores[0].tensor("y"))
    res = run_bass_kernel_spmd(nc, [in_map], core_ids=[0])
    return np.asarray(res.results[0]["y"])


L0H = 2730
L0NP = 22


def l0_tables_np():
    DH, W = 64, 512
    f64 = np.float64
    half = np.arange(0, DH, 2, dtype=f64)
    inv_freq = 1.0 / 10000.0 ** (half / DH)
    t = np.arange(T, dtype=f64)
    freqs = t[:, None] * inv_freq[None, :]
    freqs = np.concatenate([freqs, freqs], axis=-1)
    base = (half + 0.4 * DH) / (1.4 * DH)
    power = (t - T // 2) / float(W // 2)
    scale = base[None, :] ** power[:, None]
    scale = np.concatenate([scale, scale], axis=-1)
    cos, sin = np.cos(freqs), np.sin(freqs)
    sgn = np.ones(DH); sgn[:DH // 2] = -1.0
    Aq = (8.0 * cos * scale).T
    Bq = (8.0 * sin * scale * sgn[None, :]).T
    Ak = (cos / scale).T
    Bk = (sin / scale * sgn[None, :]).T
    tabs = {nm: np.ascontiguousarray(a).astype(BF16NP)
            for nm, a in (("rAq", Aq), ("rBq", Bq), ("rAk", Ak), ("rBk", Bk))}
    m = np.zeros((128, 4 * 1024), np.float32)
    for qt in range(4):
        r = np.arange(128)[:, None] + qt * 128
        c = np.arange(512)[None, :]
        m[:, qt * 1024: qt * 1024 + 512] = np.where(c >= r, 0.0, -1e9)
        m[:, qt * 1024 + 512: (qt + 1) * 1024] = np.where(c <= r, 0.0, -1e9)
    tabs["l0masks"] = m
    return tabs


def l0_host_prep(inp):
    f = np.ascontiguousarray
    io = dict(l0_tables_np())
    io["wqkv"] = f(inp["L0_ffn_wqkv"])
    io["qs"] = f(inp["L0_ffn_q_scale"].reshape(2, 64, 1))
    io["ks"] = f(inp["L0_ffn_k_scale"].reshape(2, 64, 1))
    io["wo"] = f(inp["L0_ffn_wo"])
    io["lng"] = f(inp["L0_ffn_ln_g"])
    io["lnb"] = f(inp["L0_ffn_ln_b"])
    w1 = inp["L0_ffn_w1"]
    b1 = inp["L0_ffn_b1"]
    w1p = np.zeros((2, 1024, L0NP * 2 * 128), np.float32)
    b1p = np.zeros((2, L0NP * 2, 128), np.float32)
    for p in range(L0NP):
        wa = min(128, L0H - p * 128)
        w1p[:, :, (2 * p) * 128:(2 * p) * 128 + wa] = w1[:, :, p * 128:p * 128 + wa]
        w1p[:, :, (2 * p + 1) * 128:(2 * p + 1) * 128 + wa] = w1[:, :, L0H + p * 128:L0H + p * 128 + wa]
        b1p[:, 2 * p, :wa] = b1[:, p * 128:p * 128 + wa]
        b1p[:, 2 * p + 1, :wa] = b1[:, L0H + p * 128:L0H + p * 128 + wa]
    io["w1p"] = w1p
    io["b1p"] = b1p
    w2p = np.zeros((2, L0NP * 128, 1024), np.float32)
    w2p[:, :L0H, :] = inp["L0_ffn_w2"]
    io["w2p"] = w2p
    io["b2"] = f(inp["L0_ffn_b2"])
    return io


L0_W = [("wqkv", [2, 1024, 1536], F32), ("qs", [2, 64, 1], F32),
        ("ks", [2, 64, 1], F32), ("wo", [2, 512, 1024], F32),
        ("lng", [2, 1024], F32), ("lnb", [2, 1024], F32),
        ("w1p", [2, 1024, L0NP * 256], F32), ("b1p", [2, L0NP * 2, 128], F32),
        ("w2p", [2, L0NP * 128, 1024], F32), ("b2", [2, 1024], F32),
        ("rAq", [64, T], BF16), ("rBq", [64, T], BF16),
        ("rAk", [64, T], BF16), ("rBk", [64, T], BF16),
        ("l0masks", [128, 4 * 1024], F32)]


class L0Attn:

    def __init__(self, nc, E, pfx, layer):
        self.layer = layer
        self.pfx = pfx
        s = self.sb = {}
        self.qscr = nc.dram_tensor(pfx + "qkv_scr", [1536, T], BF16, kind="Internal").ap()
        self.oscr = nc.dram_tensor(pfx + "o_scr", [512, T], BF16, kind="Internal").ap()
        s["W"] = E(nc.sbuf_tensor(pfx + "W", [128, 8 * 1536], BF16))
        s["Wo"] = E(nc.sbuf_tensor(pfx + "Wo", [128, 4 * 1024], BF16))
        s["hstg"] = E(nc.sbuf_tensor(pfx + "hstg", [128, 4 * 1024], BF16))
        s["hTc"] = E(nc.sbuf_tensor(pfx + "hTc", [128, 8 * 512], BF16))
        s["qkvstg"] = E(nc.sbuf_tensor(pfx + "qkvstg", [128, 12 * 512], BF16))
        s["xstg"] = E(nc.sbuf_tensor(pfx + "xstg", [128, 4 * 1024], BF16))
        s["ostg"] = E(nc.sbuf_tensor(pfx + "ostg", [128, 4 * 1024], BF16))
        s["otc"] = E(nc.sbuf_tensor(pfx + "otc", [128, 4 * 512], BF16))
        s["qh"] = E(nc.sbuf_tensor(pfx + "qh", [64, T], BF16))
        s["kh"] = E(nc.sbuf_tensor(pfx + "kh", [64, T], BF16))
        s["vh"] = E(nc.sbuf_tensor(pfx + "vh", [64, T], BF16))
        s["vtm"] = E(nc.sbuf_tensor(pfx + "vtm", [128, 32 * 64], BF16))
        s["qf"] = E(nc.sbuf_tensor(pfx + "qf", [64, T], BF16))
        s["kf"] = E(nc.sbuf_tensor(pfx + "kf", [64, T], BF16))
        s["oTb"] = E(nc.sbuf_tensor(pfx + "oTb", [64, T], BF16))
        s["tmpA"] = E(nc.sbuf_tensor(pfx + "tmpA", [64, 512], F32))
        s["tmpB"] = E(nc.sbuf_tensor(pfx + "tmpB", [64, 512], F32))
        s["tmpC"] = E(nc.sbuf_tensor(pfx + "tmpC", [64, 512], F32))
        s["rows"] = E(nc.sbuf_tensor(pfx + "rows", [1, 3 * 512], F32))
        s["P"] = E(nc.sbuf_tensor(pfx + "P", [128, 1024], BF16))
        s["pT"] = E(nc.sbuf_tensor(pfx + "pT", [128, 1024], BF16))
        s["otmp"] = E(nc.sbuf_tensor(pfx + "otmp", [128, 64], BF16))
        s["st"] = E(nc.sbuf_tensor(pfx + "st", [128, 8], F32))
        s["l0masks"] = E(nc.sbuf_tensor(pfx + "l0masks", [128, 4 * 1024], F32))
        s["rAq"] = E(nc.sbuf_tensor(pfx + "rAq", [64, T], BF16))
        s["rBq"] = E(nc.sbuf_tensor(pfx + "rBq", [64, T], BF16))
        s["rAk"] = E(nc.sbuf_tensor(pfx + "rAk", [64, T], BF16))
        s["rBk"] = E(nc.sbuf_tensor(pfx + "rBk", [64, T], BF16))
        s["qscol"] = E(nc.sbuf_tensor(pfx + "qscol", [64, 1], F32))
        s["kscol"] = E(nc.sbuf_tensor(pfx + "kscol", [64, 1], F32))
        self.ps = [E(nc.psum_tensor(pfx + f"ps{i}", [128, 512], F32)) for i in range(2)]
        self.ps_trL = E(nc.psum_tensor(pfx + "ps_trL", [128, 1024], BF16))
        self.ps_s = E(nc.psum_tensor(pfx + "ps_s", [128, 1024], F32))
        self.ps_o = E(nc.psum_tensor(pfx + "ps_o", [128, 64], F32))
        self.ps_n = E(nc.psum_tensor(pfx + "ps_n", [1, 512], F32))
        self.ps_b = E(nc.psum_tensor(pfx + "ps_b", [64, 512], F32))

    def emit(self, k, io, cst):
        s = self.sb
        l = self.layer
        ident = cst["ident_bf"]
        k.op("gpsimd", lambda g: g.dma_start(
            out=s["W"][:].rearrange("p (o kt m) -> p o kt m", o=1, kt=8),
            in_=io["wqkv"].rearrange("l (kt p) m -> p l kt m", p=128)[:, l:l + 1, :, :]),
            inc="g_ld", inc_n=16)
        k.op("gpsimd", lambda g: g.dma_start(
            out=s["Wo"][:].rearrange("p (o kt m) -> p o kt m", o=1, kt=4),
            in_=io["wo"].rearrange("l (kt p) m -> p l kt m", p=128)[:, l:l + 1, :, :]),
            inc="g_ld", inc_n=16)
        for nm in ("rAq", "rBq", "rAk", "rBk", "l0masks"):
            k.op("sync", lambda sy, nm=nm: sy.dma_start(out=s[nm][:], in_=io[nm]),
                 inc="m_ld", inc_n=16)
        k.op("sync", lambda sy: sy.dma_start(out=s["qscol"][:], in_=io["qs"][l:l + 1].rearrange("o p c -> (o p) c")),
             inc="m_ld", inc_n=16)
        k.op("sync", lambda sy: sy.dma_start(out=s["kscol"][:], in_=io["ks"][l:l + 1].rearrange("o p c -> (o p) c")),
             inc="m_ld", inc_n=16)
        for e in ("tensor", "vector", "scalar", "gpsimd", "sync"):
            k.wait(e, "m_ld", k.cnt["m_ld"])
            k.wait(e, "g_ld", k.cnt["g_ld"])
        zf = cst["zeros1x512f"]
        for half in range(2):
            k.op("tensor", lambda t, half=half: t.matmul(
                self.ps_s[:, half * 512:(half + 1) * 512],
                zf[0:1, 0:128], zf[0:1, 0:512], start=True, stop=True), inc="m_t")

        def qbody(ch):
            k.op("gpsimd", lambda g: g.dma_start(
                out=s["hstg"][:].rearrange("p (a d) -> p a d", a=4)[:, :, 0:512],
                in_=io["x_pad"][1:T + 1, 0:512].rearrange("(t p) d -> p t d", p=128)[:, DynSlice(ch * 4, 4), :]),
                wait=[("m_t", k.cnt["m_t"])], inc="g_x", inc_n=16)
            k.op("gpsimd", lambda g: g.dma_start(
                out=s["hstg"][:].rearrange("p (a d) -> p a d", a=4)[:, :, 512:1024],
                in_=io["x_pad"][0:T, 512:1024].rearrange("(t p) d -> p t d", p=128)[:, DynSlice(ch * 4, 4), :]),
                inc="g_x", inc_n=16)
            gx = k.cnt["g_x"]
            for dt in range(8):
                w = [("g_x", gx)] if dt == 0 else []
                w.append(("m_v", k.cnt["m_v"]))
                for tt4 in range(4):
                    k.op("tensor", lambda t, dt=dt, tt4=tt4: t.transpose(
                        self.ps_trL[:, tt4 * 128:(tt4 + 1) * 128],
                        s["hstg"][:, tt4 * 1024 + dt * 128: tt4 * 1024 + (dt + 1) * 128],
                        ident[:]),
                        wait=(w if tt4 == 0 else []),
                        inc=("m_t" if tt4 == 3 else None))
                k.op("vector", lambda v, dt=dt: v.tensor_copy(
                    s["hTc"][:, dt * 512:(dt + 1) * 512], self.ps_trL[:, 0:512]),
                    wait=[("m_t", k.cnt["m_t"])], inc="m_v")
            xt_v = k.cnt["m_v"]
            evq = k.cnt["m_v"]
            for mt in range(12):
                bank = self.ps[mt % 2]
                w = [("m_v", evq + mt - 1)] if mt >= 2 else [("m_v", xt_v)]
                if mt == 0:
                    w.append(("g_st", k.cnt["g_st"]))
                for kt in range(8):
                    k.op("tensor", lambda t, mt=mt, kt=kt, bank=bank: t.matmul(
                        bank[:, 0:512],
                        s["W"][:, kt * 1536 + mt * 128: kt * 1536 + (mt + 1) * 128],
                        s["hTc"][:, kt * 512:(kt + 1) * 512],
                        start=(kt == 0), stop=(kt == 7)),
                        wait=(w if kt == 0 else []),
                        inc=("m_t" if kt == 7 else None))
                k.op("vector", lambda v, mt=mt, bank=bank: v.tensor_copy(
                    s["qkvstg"][:, mt * 512:(mt + 1) * 512], bank[:, 0:512]),
                    wait=[("m_t", k.cnt["m_t"])], inc="m_v")
            k.op("gpsimd", lambda g: g.dma_start(
                out=self.qscr.rearrange("(mt p) t -> p mt t", p=128)[:, :, DynSlice(ch * 512, 512)],
                in_=s["qkvstg"][:].rearrange("p (mt c) -> p mt c", mt=12)),
                wait=[("m_v", k.cnt["m_v"])], inc="g_st", inc_n=16)

        k.autoloop(8, qbody)
        for e in ("tensor", "vector", "scalar", "gpsimd", "sync"):
            k.wait(e, "g_st", k.cnt["g_st"])

        def prep(k, dst, src, scol, Anm, Bnm, g):
            c0, c1 = g * 512, (g + 1) * 512
            k.op("scalar", lambda sc: sc.activation(
                s["tmpA"][0:64, :], src[0:64, c0:c1], AF.Square),
                wait=[("m_s", k.cnt["m_s"]), ("m_v", k.cnt["m_v"])], inc="m_s")
            k.op("tensor", lambda t: t.matmul(
                self.ps_n[0:1, 0:512], cst["ones128colf"][0:64, 0:1],
                s["tmpA"][0:64, :], start=True, stop=True),
                wait=[("m_s", k.cnt["m_s"])], inc="m_t")
            k.op("vector", lambda v: v.tensor_scalar(
                s["rows"][0:1, 0:512], self.ps_n[0:1, 0:512], 1e-24, None, op0=ALU.max),
                wait=[("m_t", k.cnt["m_t"]), ("m_v", k.cnt["m_v"])], inc="m_v")
            k.op("scalar", lambda sc: sc.activation(
                s["rows"][0:1, 512:1024], s["rows"][0:1, 0:512], AF.Ln),
                wait=[("m_v", k.cnt["m_v"]), ("m_s", k.cnt["m_s"])], inc="m_s")
            k.op("scalar", lambda sc: sc.activation(
                s["rows"][0:1, 1024:1536], s["rows"][0:1, 512:1024], AF.Exp, scale=-0.5),
                wait=[("m_s", k.cnt["m_s"])], inc="m_s")
            k.op("tensor", lambda t: t.matmul(
                self.ps_b[0:64, 0:512], cst["ones1x128f"][0:1, 0:64],
                s["rows"][0:1, 1024:1536], start=True, stop=True),
                wait=[("m_s", k.cnt["m_s"])], inc="m_t")
            k.op("vector", lambda v: v.scalar_tensor_tensor(
                s["tmpB"][0:64, :], src[0:64, c0:c1], scol, self.ps_b[0:64, 0:512],
                op0=ALU.mult, op1=ALU.mult),
                wait=[("m_t", k.cnt["m_t"]), ("m_v", k.cnt["m_v"])], inc="m_v")
            k.op("vector", lambda v: v.tensor_copy(s["tmpC"][0:32, :], s["tmpB"][32:64, :]),
                 wait=[("m_v", k.cnt["m_v"])], inc="m_v")
            k.op("vector", lambda v: v.tensor_copy(s["tmpC"][32:64, :], s["tmpB"][0:32, :]),
                 wait=[("m_v", k.cnt["m_v"])], inc="m_v")
            k.op("vector", lambda v: v.tensor_tensor(
                s["tmpB"][0:64, :], s["tmpB"][0:64, :], s[Anm][0:64, c0:c1], op=ALU.mult),
                wait=[("m_v", k.cnt["m_v"])], inc="m_v")
            k.op("vector", lambda v: v.tensor_tensor(
                s["tmpC"][0:64, :], s["tmpC"][0:64, :], s[Bnm][0:64, c0:c1], op=ALU.mult),
                wait=[("m_v", k.cnt["m_v"])], inc="m_v")
            k.op("vector", lambda v: v.tensor_tensor(
                dst[0:64, c0:c1], s["tmpB"][0:64, :], s["tmpC"][0:64, :], op=ALU.add),
                wait=[("m_v", k.cnt["m_v"])], inc="m_v")

        def head_body(h):
            k.op("sync", lambda sy: sy.dma_start(
                out=s["qh"][:], in_=self.qscr[DynSlice(h * 64, 64), :]),
                wait=[("m_t", k.cnt["m_t"])], inc="m_qk", inc_n=16)
            k.op("sync", lambda sy: sy.dma_start(
                out=s["kh"][:], in_=self.qscr[DynSlice(512 + h * 64, 64), :]),
                inc="m_qk", inc_n=16)
            k.op("sync", lambda sy: sy.dma_start(
                out=s["vh"][:], in_=self.qscr[DynSlice(1024 + h * 64, 64), :]),
                inc="m_qk", inc_n=16)
            ldv = k.cnt["m_qk"]
            for tt in range(32):
                w = [("m_qk", ldv)] if tt == 0 else []
                if tt % 8 == 0:
                    w.append(("m_v", k.cnt["m_v"]))
                k.op("tensor", lambda t, tt=tt: t.transpose(
                    self.ps_trL[:, (tt % 8) * 64:(tt % 8) * 64 + 64],
                    s["vh"][:, tt * 128:(tt + 1) * 128], ident[0:64, 0:64]),
                    wait=w, inc=("m_t" if tt % 8 == 7 else None))
                if tt % 8 == 7:
                    k.op("vector", lambda v, tt=tt: v.tensor_copy(
                        s["vtm"][:, (tt - 7) * 64:(tt + 1) * 64], self.ps_trL[:, 0:512]),
                        wait=[("m_t", k.cnt["m_t"])], inc="m_v")
            for g in range(8):
                prep(k, s["kf"][:], s["kh"][:], s["kscol"][0:64, 0:1], "rAk", "rBk", g)
                prep(k, s["qf"][:], s["qh"][:], s["qscol"][0:64, 0:1], "rAq", "rBq", g)
                for qt in range(4):
                    qcol = g * 512 + qt * 128
                    mm_w = [("m_s", k.cnt["m_s"]), ("m_v", k.cnt["m_v"])]
                    if g > 0:
                        k.op("tensor", lambda t, qcol=qcol, g=g: t.matmul(
                            self.ps_s[:, 0:512], s["qf"][0:64, qcol:qcol + 128],
                            s["kf"][0:64, (g - 1) * 512:g * 512], start=True, stop=True),
                            wait=mm_w, inc="m_t")
                    k.op("tensor", lambda t, qcol=qcol, g=g: t.matmul(
                        self.ps_s[:, 512:1024], s["qf"][0:64, qcol:qcol + 128],
                        s["kf"][0:64, g * 512:(g + 1) * 512], start=True, stop=True),
                        wait=([] if g > 0 else mm_w), inc="m_t")
                    k.op("vector", lambda v, qt=qt: v.tensor_tensor(
                        self.ps_s[:, 0:1024], self.ps_s[:, 0:1024],
                        s["l0masks"][:, qt * 1024:(qt + 1) * 1024], op=ALU.add),
                        wait=[("m_t", k.cnt["m_t"]), ("m_v", k.cnt["m_v"])], inc="m_v")
                    if g == 0:
                        k.op("vector", lambda v: v.tensor_scalar(
                            self.ps_s[:, 0:512], self.ps_s[:, 0:512], -1e9, None, op0=ALU.add),
                            wait=[("m_v", k.cnt["m_v"])], inc="m_v")
                    k.op("vector", lambda v: v.tensor_reduce(
                        s["st"][:, 0:1], self.ps_s[:, 0:1024], axis=AX.X, op=ALU.max, negate=True),
                        wait=[("m_v", k.cnt["m_v"])], inc="m_v")
                    k.op("scalar", lambda sc: sc.activation(
                        s["P"][:], self.ps_s[:, 0:1024], AF.Exp, bias=s["st"][:, 0:1],
                        accum_out=s["st"][:, 1:2]),
                        wait=[("m_v", k.cnt["m_v"])], inc="m_s")
                    for j in range(8):
                        w = ([("m_s", k.cnt["m_s"]), ("m_av", k.cnt["m_av"]),
                              ("m_v", k.cnt["m_v"])] if j == 0 else [])
                        k.op("tensor", lambda t, j=j: t.transpose(
                            self.ps_trL[:, j * 128:(j + 1) * 128],
                            s["P"][:, j * 128:(j + 1) * 128], ident[:]),
                            wait=w, inc=("m_t" if j == 7 else None))
                    k.op("vector", lambda v: v.tensor_copy(s["pT"][:], self.ps_trL[:, 0:1024]),
                         wait=[("m_t", k.cnt["m_t"])], inc="m_v")
                    js = list(range(8)) if g > 0 else list(range(4, 8))
                    for idx, j in enumerate(js):
                        wt = (g - 1) * 4 + j
                        k.op("tensor", lambda t, j=j, wt=wt, idx=idx, js=js: t.matmul(
                            self.ps_o[:, 0:64], s["pT"][:, j * 128:(j + 1) * 128],
                            s["vtm"][:, wt * 64:(wt + 1) * 64],
                            start=(idx == 0), stop=(idx == len(js) - 1)),
                            wait=([("m_v", k.cnt["m_v"]), ("m_s", k.cnt["m_s"])] if idx == 0 else []),
                            inc=("m_av" if idx == len(js) - 1 else None))
                    k.op("vector", lambda v: v.reciprocal(s["st"][:, 2:3], s["st"][:, 1:2]),
                         wait=[("m_av", k.cnt["m_av"]), ("m_v", k.cnt["m_v"])], inc="m_v")
                    k.op("scalar", lambda sc: sc.activation(
                        s["otmp"][:], self.ps_o[:, 0:64], AF.Identity, scale=s["st"][:, 2:3]),
                        wait=[("m_v", k.cnt["m_v"]), ("m_s", k.cnt["m_s"])], inc="m_s")
                    k.op("tensor", lambda t: t.transpose(
                        self.ps_trL[0:64, 0:128], s["otmp"][:], ident[:]),
                        wait=[("m_s", k.cnt["m_s"]), ("m_v", k.cnt["m_v"])], inc="m_t")
                    ow = [("m_t", k.cnt["m_t"])]
                    if g == 0 and qt == 0:
                        ow.append(("g_st", k.cnt["g_st"]))
                    k.op("vector", lambda v, qcol=qcol, ow=ow: v.tensor_copy(
                        s["oTb"][0:64, qcol:qcol + 128], self.ps_trL[0:64, 0:128]),
                        wait=ow, inc="m_v")
            k.op("gpsimd", lambda g_: g_.dma_start(
                out=self.oscr[DynSlice(h * 64, 64), :], in_=s["oTb"][0:64, :]),
                wait=[("m_v", k.cnt["m_v"])], inc="g_st", inc_n=16)

        k.autoloop(8, head_body)
        for e in ("tensor", "vector", "scalar", "gpsimd", "sync"):
            k.wait(e, "g_st", k.cnt["g_st"])

        def obody(ch):
            k.op("sync", lambda sy: sy.dma_start(
                out=s["otc"][:].rearrange("p (kt c) -> p kt c", kt=4),
                in_=self.oscr.rearrange("(kt p) t -> p kt t", p=128)[:, :, DynSlice(ch * 512, 512)]),
                wait=[("m_t", k.cnt["m_t"])], inc="m_qk", inc_n=16)
            k.op("gpsimd", lambda g: g.dma_start(
                out=s["xstg"][:].rearrange("p (a d) -> p a d", a=4),
                in_=io["x_pad"][1:T + 1, :].rearrange("(t p) d -> p t d", p=128)[:, DynSlice(ch * 4, 4), :]),
                wait=[("m_t", k.cnt["m_t"])], inc="g_x", inc_n=16)
            ld = k.cnt["m_qk"]
            gx = k.cnt["g_x"]
            for tt4 in range(4):
                for nch in range(2):
                    bank = self.ps[nch]
                    w0 = [("m_qk", ld), ("g_x", gx), ("m_v", k.cnt["m_v"])]
                    for kt in range(4):
                        k.op("tensor", lambda t, tt4=tt4, nch=nch, kt=kt, bank=bank: t.matmul(
                            bank[:, 0:512],
                            s["otc"][:, kt * 512 + tt4 * 128: kt * 512 + (tt4 + 1) * 128],
                            s["Wo"][:, kt * 1024 + nch * 512: kt * 1024 + (nch + 1) * 512],
                            start=(kt == 0), stop=False),
                            wait=(w0 if kt == 0 else []), inc=None)
                    k.op("tensor", lambda t, tt4=tt4, nch=nch, bank=bank: t.matmul(
                        bank[:, 0:512], ident[:],
                        s["xstg"][:, tt4 * 1024 + nch * 512: tt4 * 1024 + (nch + 1) * 512],
                        start=False, stop=True),
                        inc="m_t")
                    ew = [("m_t", k.cnt["m_t"])]
                    if tt4 == 0 and nch == 0:
                        ew.append(("g_st", k.cnt["g_st"]))
                    k.op("vector", lambda v, tt4=tt4, nch=nch, bank=bank, ew=ew: v.tensor_copy(
                        s["ostg"][:, tt4 * 1024 + nch * 512: tt4 * 1024 + (nch + 1) * 512],
                        bank[:, 0:512]),
                        wait=ew, inc="m_v")
            k.op("gpsimd", lambda g: g.dma_start(
                out=io["x_out"].rearrange("(t p) d -> p t d", p=128)[:, DynSlice(ch * 4, 4), :],
                in_=s["ostg"][:].rearrange("p (a d) -> p a d", a=4)),
                wait=[("m_v", k.cnt["m_v"])], inc="g_st", inc_n=16)

        k.autoloop(8, obody)
        k.wait("sync", "g_st", k.cnt["g_st"])


class L0Ffn:

    def __init__(self, nc, E, pfx, layer, postln):
        self.layer = layer
        self.postln = postln
        s = self.sb = {}
        s["w1p"] = E(nc.sbuf_tensor(pfx + "w1p", [128, 8 * L0NP * 256], BF16))
        s["w2p"] = E(nc.sbuf_tensor(pfx + "w2p", [128, L0NP * 1024], BF16))
        s["hstg"] = E(nc.sbuf_tensor(pfx + "hstg", [128, 4 * 1024], BF16))
        s["hln"] = E(nc.sbuf_tensor(pfx + "hln", [128, 4 * 1024], BF16))
        s["hT"] = E(nc.sbuf_tensor(pfx + "hT", [128, 8 * 512], BF16))
        s["hid"] = E(nc.sbuf_tensor(pfx + "hid", [128, L0NP * 512], BF16))
        s["xstg"] = E(nc.sbuf_tensor(pfx + "xstg", [128, 4 * 1024], BF16))
        if postln:
            s["astg"] = E(nc.sbuf_tensor(pfx + "astg", [128, 1024], BF16))
        s["ostg"] = E(nc.sbuf_tensor(pfx + "ostg", [128, 4 * 1024], BF16))
        s["stage"] = E(nc.sbuf_tensor(pfx + "stage", [128, 1024], F32))
        s["sq"] = E(nc.sbuf_tensor(pfx + "sq", [128, 1024], BF16))
        s["st"] = E(nc.sbuf_tensor(pfx + "st", [128, 16], F32))
        s["gcol"] = E(nc.sbuf_tensor(pfx + "gcol", [128, 8], F32))
        s["bcol"] = E(nc.sbuf_tensor(pfx + "bcol", [128, 8], F32))
        s["b1sb"] = E(nc.sbuf_tensor(pfx + "b1sb", [128, 2 * L0NP], F32))
        s["b2bc"] = E(nc.sbuf_tensor(pfx + "b2bc", [128, 1024], BF16))
        s["epsv"] = E(nc.sbuf_tensor(pfx + "epsv", [128, 1], F32))
        self.ps_tr = E(nc.psum_tensor(pfx + "ps_tr", [128, 512], BF16))
        self.ps_a = E(nc.psum_tensor(pfx + "ps_a", [128, 512], F32))
        self.ps_g = E(nc.psum_tensor(pfx + "ps_g", [128, 512], F32))
        self.ps_o = E(nc.psum_tensor(pfx + "ps_o", [128, 1024], F32))

    def emit_stats(self, k, src, dst, sb):
        s = sb
        k.op("vector", lambda v: v.tensor_reduce(
            s["st"][:, 0:1], src, axis=AX.X, op=ALU.add),
            wait=[("m_v", k.cnt["m_v"]), ("m_s", k.cnt["m_s"])], inc="m_v")
        k.op("scalar", lambda sc: sc.activation(
            s["sq"][:], src, AF.Square, accum_out=s["st"][:, 1:2]),
            wait=[("m_v", k.cnt["m_v"]), ("m_s", k.cnt["m_s"])], inc="m_s")
        k.op("vector", lambda v: v.tensor_scalar_mul(
            s["st"][:, 2:3], s["st"][:, 0:1], 1.0 / 1024.0),
            wait=[("m_v", k.cnt["m_v"]), ("m_s", k.cnt["m_s"])], inc="m_v")
        k.op("vector", lambda v: v.tensor_scalar_mul(
            s["st"][:, 3:4], s["st"][:, 1:2], 1.0 / 1024.0),
            wait=[("m_v", k.cnt["m_v"])], inc="m_v")
        k.op("vector", lambda v: v.tensor_tensor(
            s["st"][:, 4:5], s["st"][:, 2:3], s["st"][:, 2:3], op=ALU.mult),
            wait=[("m_v", k.cnt["m_v"])], inc="m_v")
        k.op("vector", lambda v: v.tensor_tensor(
            s["st"][:, 5:6], s["st"][:, 3:4], s["st"][:, 4:5], op=ALU.subtract),
            wait=[("m_v", k.cnt["m_v"])], inc="m_v")
        k.op("scalar", lambda sc: sc.activation(
            s["st"][:, 6:7], s["st"][:, 5:6], AF.Ln, bias=s["epsv"][:, 0:1]),
            wait=[("m_v", k.cnt["m_v"]), ("m_s", k.cnt["m_s"])], inc="m_s")
        k.op("scalar", lambda sc: sc.activation(
            s["st"][:, 7:8], s["st"][:, 6:7], AF.Exp, scale=-0.5),
            wait=[("m_s", k.cnt["m_s"])], inc="m_s")
        k.op("vector", lambda v: v.tensor_tensor(
            s["st"][:, 8:9], s["st"][:, 2:3], s["st"][:, 7:8], op=ALU.mult),
            wait=[("m_v", k.cnt["m_v"]), ("m_s", k.cnt["m_s"])], inc="m_v")
        k.op("vector", lambda v: v.tensor_scalar_mul(
            s["st"][:, 9:10], s["st"][:, 8:9], -1.0),
            wait=[("m_v", k.cnt["m_v"])], inc="m_v")
        k.op("scalar", lambda sc: sc.activation(
            dst, src, AF.Identity, scale=s["st"][:, 7:8], bias=s["st"][:, 9:10]),
            wait=[("m_v", k.cnt["m_v"]), ("m_s", k.cnt["m_s"])], inc="m_s")

    def emit(self, k, io, cst):
        s = self.sb
        l = self.layer
        ident = cst["ident_bf"]
        k.op("gpsimd", lambda g: g.dma_start(
            out=s["w1p"][:].rearrange("p (o kt m) -> p o kt m", o=1, kt=8),
            in_=io["w1p"].rearrange("l (kt p) m -> p l kt m", p=128)[:, l:l + 1, :, :]),
            inc="g_ld", inc_n=16)
        k.op("gpsimd", lambda g: g.dma_start(
            out=s["w2p"][:].rearrange("p (o pr m) -> p o pr m", o=1, pr=L0NP),
            in_=io["w2p"].rearrange("l (pr p) m -> p l pr m", p=128)[:, l:l + 1, :, :]),
            inc="g_ld", inc_n=16)
        k.op("sync", lambda sy: sy.dma_start(
            out=s["gcol"][:], in_=io["lng"][l:l + 1].rearrange("o (kt p) -> p (o kt)", p=128),
            allow_slow_non_contiguous=True),
            inc="m_ld", inc_n=16)
        k.op("sync", lambda sy: sy.dma_start(
            out=s["bcol"][:], in_=io["lnb"][l:l + 1].rearrange("o (kt p) -> p (o kt)", p=128),
            allow_slow_non_contiguous=True),
            inc="m_ld", inc_n=16)
        k.op("sync", lambda sy: sy.dma_start(
            out=s["b1sb"][:], in_=io["b1p"][l].rearrange("pr p -> p pr"),
            allow_slow_non_contiguous=True),
            inc="m_ld", inc_n=16)
        k.op("sync", lambda sy: sy.dma_start(out=s["stage"][0:1, 0:1024], in_=io["b2"][l:l + 1, :]),
             inc="m_ld", inc_n=16)
        for e in ("tensor", "vector", "scalar", "gpsimd", "sync"):
            k.wait(e, "m_ld", k.cnt["m_ld"])
            k.wait(e, "g_ld", k.cnt["g_ld"])
        k.op("vector", lambda v: v.memset(s["epsv"][:], 1e-5), inc="m_v")
        for half in range(2):
            k.op("tensor", lambda t, half=half: t.matmul(
                self.ps_o[:, half * 512:(half + 1) * 512],
                cst["ones1x128f"][0:1, :], s["stage"][0:1, half * 512:(half + 1) * 512],
                start=True, stop=True), inc="m_t")
        k.op("vector", lambda v: v.tensor_copy(s["b2bc"][:], self.ps_o[:, 0:1024]),
             wait=[("m_t", k.cnt["m_t"])], inc="m_v")

        def body(ch):
            k.op("gpsimd", lambda g: g.dma_start(
                out=s["hstg"][:].rearrange("p (a d) -> p a d", a=4)[:, :, 0:512],
                in_=io["x_pad"][1:T + 1, 0:512].rearrange("(t p) d -> p t d", p=128)[:, DynSlice(ch * 4, 4), :]),
                wait=[("m_t", k.cnt["m_t"]), ("m_s", k.cnt["m_s"])], inc="g_x", inc_n=16)
            k.op("gpsimd", lambda g: g.dma_start(
                out=s["hstg"][:].rearrange("p (a d) -> p a d", a=4)[:, :, 512:1024],
                in_=io["x_pad"][0:T, 512:1024].rearrange("(t p) d -> p t d", p=128)[:, DynSlice(ch * 4, 4), :]),
                inc="g_x", inc_n=16)
            k.op("gpsimd", lambda g: g.dma_start(
                out=s["xstg"][:].rearrange("p (a d) -> p a d", a=4),
                in_=io["x_pad"][1:T + 1, :].rearrange("(t p) d -> p t d", p=128)[:, DynSlice(ch * 4, 4), :]),
                inc="g_x", inc_n=16)
            gx = k.cnt["g_x"]
            for t4 in range(4):
                if t4 == 0:
                    k.wait("vector", "g_x", gx)
                    k.wait("scalar", "g_x", gx)
                self.emit_stats(k, s["hstg"][:, t4 * 1024:(t4 + 1) * 1024],
                                s["hln"][:, t4 * 1024:(t4 + 1) * 1024], s)
            for dt in range(8):
                for t4 in range(4):
                    w = []
                    if t4 == 0:
                        w = [("m_s", k.cnt["m_s"])]
                        if dt == 0:
                            w.append(("m_v", k.cnt["m_v"]))
                    k.op("tensor", lambda t, dt=dt, t4=t4: t.transpose(
                        self.ps_tr[:, t4 * 128:(t4 + 1) * 128],
                        s["hln"][:, t4 * 1024 + dt * 128: t4 * 1024 + (dt + 1) * 128],
                        ident[:]),
                        wait=w, inc=("m_t" if t4 == 3 else None))
                k.op("scalar", lambda sc, dt=dt: sc.activation(
                    s["hT"][:, dt * 512:(dt + 1) * 512], self.ps_tr[:, 0:512],
                    AF.Identity, scale=s["gcol"][:, dt:dt + 1], bias=s["bcol"][:, dt:dt + 1]),
                    wait=[("m_t", k.cnt["m_t"]), ("m_s", k.cnt["m_s"])], inc="m_s")
            for p in range(L0NP):
                for kt in range(8):
                    k.op("tensor", lambda t, p=p, kt=kt: t.matmul(
                        self.ps_a[:, 0:512],
                        s["w1p"][:, kt * (L0NP * 256) + (2 * p) * 128: kt * (L0NP * 256) + (2 * p) * 128 + 128],
                        s["hT"][:, kt * 512:(kt + 1) * 512],
                        start=(kt == 0), stop=(kt == 7)),
                        wait=([("m_s", k.cnt["m_s"]), ("m_v", k.cnt["m_v"])] if kt == 0 else []),
                        inc=("m_t" if kt == 7 else None))
                for kt in range(8):
                    k.op("tensor", lambda t, p=p, kt=kt: t.matmul(
                        self.ps_g[:, 0:512],
                        s["w1p"][:, kt * (L0NP * 256) + (2 * p + 1) * 128: kt * (L0NP * 256) + (2 * p + 1) * 128 + 128],
                        s["hT"][:, kt * 512:(kt + 1) * 512],
                        start=(kt == 0), stop=(kt == 7)),
                        inc=("m_t" if kt == 7 else None))
                k.op("scalar", lambda sc, p=p: sc.activation(
                    s["sq"][:, 0:512], self.ps_a[:, 0:512], AF.Identity, bias=s["b1sb"][:, 2 * p:2 * p + 1]),
                    wait=[("m_t", k.cnt["m_t"]), ("m_s", k.cnt["m_s"])], inc="m_s")
                k.op("scalar", lambda sc, p=p: sc.activation(
                    s["sq"][:, 512:1024], self.ps_g[:, 0:512], AF.Gelu, bias=s["b1sb"][:, 2 * p + 1:2 * p + 2]),
                    wait=[("m_s", k.cnt["m_s"])], inc="m_s")
                k.op("vector", lambda v, p=p: v.tensor_tensor(
                    s["hid"][:, p * 512:(p + 1) * 512], s["sq"][:, 0:512], s["sq"][:, 512:1024], op=ALU.mult),
                    wait=[("m_s", k.cnt["m_s"]), ("m_v", k.cnt["m_v"])], inc="m_v")
            hid_v = k.cnt["m_v"]
            for st4 in range(4):
                for half in range(2):
                    for p in range(L0NP):
                        w = []
                        if p == 0:
                            w = [("m_v", k.cnt["m_v"]), ("m_s", k.cnt["m_s"])]
                        k.op("tensor", lambda t, st4=st4, half=half, p=p: t.matmul(
                            self.ps_o[:, half * 512:(half + 1) * 512],
                            s["hid"][:, p * 512 + st4 * 128: p * 512 + st4 * 128 + 128],
                            s["w2p"][:, p * 1024 + half * 512: p * 1024 + (half + 1) * 512],
                            start=(p == 0), stop=(p == L0NP - 1)),
                            wait=w, inc=("m_t" if p == L0NP - 1 else None))
                mm_t = k.cnt["m_t"]
                if not self.postln:
                    for half in range(2):
                        cs = slice(st4 * 1024 + half * 512, st4 * 1024 + (half + 1) * 512)
                        k.op("vector", lambda v, half=half, cs=cs: v.tensor_tensor(
                            s["stage"][:, half * 512:(half + 1) * 512],
                            self.ps_o[:, half * 512:(half + 1) * 512],
                            s["b2bc"][:, half * 512:(half + 1) * 512], op=ALU.add),
                            wait=[("m_t", mm_t), ("m_v", k.cnt["m_v"])], inc="m_v")
                        k.op("vector", lambda v, half=half, cs=cs: v.tensor_tensor(
                            s["ostg"][:, cs], s["stage"][:, half * 512:(half + 1) * 512],
                            s["xstg"][:, cs], op=ALU.add),
                            wait=[("m_v", k.cnt["m_v"])], inc="m_v")
                else:
                    k.op("gpsimd", lambda g, st4=st4: g.dma_start(
                        out=s["astg"][:].rearrange("p (a d) -> p a d", a=1),
                        in_=io["aorig"].rearrange("(t p) d -> p t d", p=128)[:, DynSlice(ch * 4 + st4, 1), :]),
                        wait=[("m_v", k.cnt["m_v"])], inc="g_x", inc_n=16)
                    gxa = k.cnt["g_x"]
                    for half in range(2):
                        cs = slice(st4 * 1024 + half * 512, st4 * 1024 + (half + 1) * 512)
                        k.op("vector", lambda v, half=half, cs=cs: v.tensor_tensor(
                            s["stage"][:, half * 512:(half + 1) * 512],
                            self.ps_o[:, half * 512:(half + 1) * 512],
                            s["b2bc"][:, half * 512:(half + 1) * 512], op=ALU.add),
                            wait=[("m_t", mm_t), ("m_v", k.cnt["m_v"])], inc="m_v")
                        k.op("vector", lambda v, half=half, cs=cs: v.tensor_tensor(
                            s["stage"][:, half * 512:(half + 1) * 512],
                            s["stage"][:, half * 512:(half + 1) * 512],
                            s["xstg"][:, cs], op=ALU.add),
                            wait=[("m_v", k.cnt["m_v"])], inc="m_v")
                        k.op("vector", lambda v, half=half, gxa=gxa: v.tensor_tensor(
                            s["stage"][:, half * 512:(half + 1) * 512],
                            s["stage"][:, half * 512:(half + 1) * 512],
                            s["astg"][:, half * 512:(half + 1) * 512], op=ALU.add),
                            wait=[("m_v", k.cnt["m_v"]), ("g_x", gxa)], inc="m_v")
                    self.emit_stats(k, s["stage"][:, 0:1024],
                                    s["ostg"][:, st4 * 1024:(st4 + 1) * 1024], s)
            k.op("gpsimd", lambda g: g.dma_start(
                out=io["x_out"].rearrange("(t p) d -> p t d", p=128)[:, DynSlice(ch * 4, 4), :],
                in_=s["ostg"][:].rearrange("p (a d) -> p a d", a=4)),
                wait=[("m_v", k.cnt["m_v"]), ("m_s", k.cnt["m_s"])], inc="g_st", inc_n=16)

        k.autoloop(8, body)
        k.wait("sync", "g_st", k.cnt["g_st"])


def build_l0attn_program(layer):
    from contextlib import ExitStack
    nc = bass.Bass("TRN2", target_bir_lowering=False, detect_race_conditions=False)
    x_pad = nc.dram_tensor("x", [T + 1, 1024], BF16, kind="ExternalInput").ap()
    x_out = nc.dram_tensor("y", [T, 1024], BF16, kind="ExternalOutput").ap()
    io = {"x_pad": x_pad, "x_out": x_out}
    for nm, shp, dt in L0_W:
        io[nm] = nc.dram_tensor("w_" + nm, shp, dt, kind="ExternalInput").ap()
    ctx = ExitStack()
    with ctx:
        E = ctx.enter_context
        cst_np, cst_aps, cst_sb = const_tensors(nc, E)
        blk_obj = L0Attn(nc, E, "la_", layer)
        sems = {}
        for nm in ("m_ld", "g_ld", "m_t", "m_v", "m_s", "m_qk", "m_av", "g_x", "g_st", "st"):
            sems[nm] = E(nc.semaphore(nm))
        blk = E(nc.Block())
        k = K(nc, sems, {})

        def emit_all(k):
            for nm in cst_sb:
                k.op("sync", lambda sy, nm=nm: sy.dma_start(out=cst_sb[nm][:], in_=cst_aps[nm]),
                     inc="m_ld", inc_n=16)
            cst = {nm: cst_sb[nm][:] for nm in cst_sb}
            blk_obj.emit(k, io, cst)

        def section(name):
            def f(eng):
                k.run(name, eng, [emit_all])
            return f
        blk.sync(section("sync"))
        blk.gpsimd(section("gpsimd"))
        blk.tensor(section("tensor"))
        blk.vector(section("vector"))
        blk.scalar(section("scalar"))
    return nc, cst_np


def build_l0ffn_program(layer, postln):
    from contextlib import ExitStack
    nc = bass.Bass("TRN2", target_bir_lowering=False, detect_race_conditions=False)
    x_pad = nc.dram_tensor("x", [T + 1, 1024], BF16, kind="ExternalInput").ap()
    x_out = nc.dram_tensor("y", [T, 1024], BF16, kind="ExternalOutput").ap()
    io = {"x_pad": x_pad, "x_out": x_out}
    if postln:
        io["aorig"] = nc.dram_tensor("aorig", [T, 1024], BF16, kind="ExternalInput").ap()
    for nm, shp, dt in L0_W:
        io[nm] = nc.dram_tensor("w_" + nm, shp, dt, kind="ExternalInput").ap()
    ctx = ExitStack()
    with ctx:
        E = ctx.enter_context
        cst_np, cst_aps, cst_sb = const_tensors(nc, E, only=("ident_bf", "ones1x128f"))
        blk_obj = L0Ffn(nc, E, "lf_", layer, postln)
        sems = {}
        for nm in ("m_ld", "g_ld", "m_t", "m_v", "m_s", "g_x", "g_st", "st"):
            sems[nm] = E(nc.semaphore(nm))
        blk = E(nc.Block())
        k = K(nc, sems, {})

        def emit_all(k):
            for nm in cst_sb:
                k.op("sync", lambda sy, nm=nm: sy.dma_start(out=cst_sb[nm][:], in_=cst_aps[nm]),
                     inc="m_ld", inc_n=16)
            cst = {nm: cst_sb[nm][:] for nm in cst_sb}
            blk_obj.emit(k, io, cst)

        def section(name):
            def f(eng):
                k.run(name, eng, [emit_all])
            return f
        blk.sync(section("sync"))
        blk.gpsimd(section("gpsimd"))
        blk.tensor(section("tensor"))
        blk.vector(section("vector"))
        blk.scalar(section("scalar"))
    return nc, cst_np


def _l0_run_one(build_fn, args, x_pad, io_np, extra, sim):
    nc, cst_np = build_fn(*args)
    in_map = {"x": x_pad}
    in_map.update(extra)
    for nm, arr in io_np.items():
        in_map["w_" + nm] = arr
    for nm, arr in cst_np.items():
        in_map["c_" + nm] = arr
    if sim:
        from concourse.bass_interp import MultiCoreSim
        simu = MultiCoreSim(nc, num_cores=1, num_workers=1, trace=False)
        for kk, v in in_map.items():
            simu.cores[0].tensor(kk)[:] = v
        simu.simulate(check_with_hw=False)
        return np.asarray(simu.cores[0].tensor("y"))
    res = run_bass_kernel_spmd(nc, [in_map], core_ids=[0])
    return np.asarray(res.results[0]["y"])


def _pad_stream(x):
    return np.ascontiguousarray(
        np.concatenate([np.zeros((1, 1024), BF16NP), x.astype(BF16NP)], axis=0))


def run_l0_test(x_bf16, io_np, sim=False, upto=4):
    a0 = np.ascontiguousarray(x_bf16)
    x = a0
    outs = []
    steps = [(build_l0attn_program, (0,), {}),
             (build_l0ffn_program, (0, False), {}),
             (build_l0attn_program, (1,), {}),
             (build_l0ffn_program, (1, True), {"aorig": a0})]
    for i, (bf, args, extra) in enumerate(steps[:upto]):
        x = _l0_run_one(bf, args, _pad_stream(x), io_np, extra, sim)
        outs.append(x)
    return outs


_BUILT = {}


def _get_prog(key, builder, args):
    if key not in _BUILT:
        _BUILT[key] = builder(*args)
    return _BUILT[key]


def _launch(progkey, builder, bargs, wmap, xs_list, extra_list=None):
    nc, cst_np = _get_prog(progkey, builder, bargs)
    in_maps = []
    for b in range(8):
        m = dict(wmap)
        m["x"] = xs_list[b]
        if extra_list is not None:
            m.update(extra_list[b])
        for nm, arr in cst_np.items():
            m["c_" + nm] = arr
        in_maps.append(m)
    res = run_bass_kernel_spmd(nc, in_maps, core_ids=list(range(8)))
    return [np.asarray(res.results[b]["y"]) for b in range(8)]


def kernel(**inputs):
    _patch_fast_compile()
    B = 8
    x0 = np.asarray(inputs["L0_attn_x"])
    wm = {"w_" + nm: arr for nm, arr in mamba_host_prep(inputs).items()}
    w0 = {"w_" + nm: arr for nm, arr in l0_host_prep(inputs).items()}
    wa1 = {"w_" + nm: arr for nm, arr in attn_host_prep(inputs, "L1").items()}
    wf1 = {"w_" + nm: arr for nm, arr in l1ffn_host_prep(inputs).items()}
    wa2 = {"w_" + nm: arr for nm, arr in attn_host_prep(inputs, "L2").items()}
    wf2 = {"w_" + nm: arr for nm, arr in l23ffn_host_prep(inputs, "L2").items()}
    wa3 = {"w_" + nm: arr for nm, arr in attn_host_prep(inputs, "L3").items()}
    wf3 = {"w_" + nm: arr for nm, arr in l23ffn_host_prep(inputs, "L3").items()}
    xs = [np.ascontiguousarray(x0[b]) for b in range(B)]
    xs = _launch("mamba", build_mamba_program, (T,), wm, xs)
    a0 = xs
    xs = _launch("l0a0", build_l0attn_program, (0,), w0, [_pad_stream(x) for x in xs])
    xs = _launch("l0f0", build_l0ffn_program, (0, False), w0, [_pad_stream(x) for x in xs])
    xs = _launch("l0a1", build_l0attn_program, (1,), w0, [_pad_stream(x) for x in xs])
    xs = _launch("l0f1", build_l0ffn_program, (1, True), w0,
                 [_pad_stream(x) for x in xs], [{"aorig": a0[b]} for b in range(B)])
    xs = _launch("a1", build_attn_program, ("L1",), wa1, xs)
    xs = _launch("f1", build_l1ffn_program, (), wf1, xs)
    xs = _launch("a2", build_attn_program, ("L2",), wa2, xs)
    xs = _launch("f2", build_l23ffn_program, ("L2",), wf2, xs)
    xs = _launch("a3", build_attn_program, ("L3",), wa3, xs)
    xs = _launch("f3", build_l23ffn_program, ("L3", True), wf3, xs)
    return np.stack(xs, axis=0)


# version 04 (final)
# speedup vs baseline: 1.0343x; 1.0343x over previous
import math
import numpy as np
import ml_dtypes

import concourse.bass as bass
import concourse.mybir as mybir
from concourse.bass import ts
from concourse.bass_types import DynSlice
from concourse.bass_utils import run_bass_kernel_spmd

F32 = mybir.dt.float32
BF16 = mybir.dt.bfloat16
AF = mybir.ActivationFunctionType
ALU = mybir.AluOpType
AX = mybir.AxisListType
BF16NP = ml_dtypes.bfloat16

NCORES = 8
T = 4096
D = 1024
NTT = T // 128
NDT = D // 128

def _patch_fast_compile():
    try:
        import concourse.bass_utils as bu
        from pathlib import Path
        if getattr(bu, "_fast_compile_patched", False):
            return
        def patched(tmpdir, inp="bir.json", outp="file.neff", arch=None, *, dve_root=None):
            from concourse.bass_utils import (get_walrus_driver, get_walrus_args,
                                              get_bir_arch, run_command, aot_getenv)
            cmd = [
                get_walrus_driver(),
                "--pass",
                ",".join(["birverifier", "runtime_memory_reservation", "lower_act",
                          "lower_dve", "lower_ap_offset", "codegen", "neff_packager"]),
                "-i", inp,
                "--neff-output-filename", outp,
                "--enable-birsim=false",
                "--mem-mode=physical",
                "--policy=0",
                "--enable-ldw-opt=false",
                "--assign-static-dmas-to-sp=false",
                f"--dram-page-size={aot_getenv('NEURON_SCRATCHPAD_PAGE_SIZE', '256')}",
                "--enable-neff-debug-info=true",
                "--jobs", "16",
                *get_walrus_args(get_bir_arch(tmpdir, inp) if arch is None else arch,
                                 tmpdir, dve_root=dve_root),
            ]
            result = run_command(cmd, cwd=tmpdir)
            if result is not None:
                (Path(tmpdir) / "log.txt").write_text(result.stdout)
            return f"{tmpdir}/{outp}"
        bu.bir_verify_and_optimise = patched
        bu._fast_compile_patched = True
    except Exception:
        pass

_patch_fast_compile()


class K:
    ENGINES = ("sync", "gpsimd", "tensor", "vector", "scalar")

    def __init__(self, nc, sems, bufs):
        self.nc = nc
        self.sems = sems
        self.b = bufs
        self.cnt = {s: 0 for s in sems}
        self.eng_name = None
        self.eng = None
        self.dry = False
        self.dry_sems = set()
        self.loop_stack = []
        self.actrs = {}
        self.unroll = False

    def run(self, eng_name, eng, emit_fns):
        self.eng_name = eng_name
        self.eng = eng
        self.cnt = {s: 0 for s in self.sems}
        self.dry = False
        self.dry_sems = set()
        self.loop_stack = []
        self.actrs = {}
        for fn in emit_fns:
            fn(self)

    def op(self, eng_name, fn, wait=(), inc=None, inc_n=1):
        inst = None
        if self.eng_name == eng_name:
            if self.dry:
                for sm, v in wait:
                    self.dry_sems.add(sm)
            else:
                for sm, v in wait:
                    if callable(v):
                        v = v()
                    self.autowait(sm, v)
                inst = fn(self.eng)
                if inc is not None and inst is not None:
                    inst.then_inc(self.sems[inc], inc_n)
        if inc is not None:
            self.cnt[inc] += inc_n
        return inst

    def wait(self, eng_name, sem, value):
        if self.eng_name == eng_name:
            if self.dry:
                self.dry_sems.add(sem)
            else:
                self.autowait(sem, value)

    def autowait(self, sem, value):
        ctr = self.actrs.get(sem)
        if ctr is None:
            self.eng.wait_ge(self.sems[sem], value)
        else:
            ctr.wait(value)

    def autoloop(self, trip, body, engines=("tensor", "vector", "scalar", "gpsimd", "sync")):
        if trip == 0:
            return
        if self.unroll and not self.dry:
            for i in range(trip):
                body(i)
            return
        base = dict(self.cnt)
        if self.dry or trip == 1:
            body(0)
            if self.dry:
                for sm in self.cnt:
                    self.cnt[sm] = base[sm] + (self.cnt[sm] - base[sm]) * trip
            return
        top_level = not self.loop_stack
        self.dry = True
        dry_sems_save = self.dry_sems
        self.dry_sems = set()
        body(0)
        delta = {sm: self.cnt[sm] - base[sm] for sm in self.cnt}
        waited = self.dry_sems
        self.dry_sems = dry_sems_save | waited
        self.dry = False
        self.cnt = dict(base)
        created = []
        if top_level:
            for sm in sorted(waited):
                if sm not in self.actrs:
                    c = _AutoCtr(self, sm, base[sm])
                    self.actrs[sm] = c
                    created.append(sm)
        entry_moved = {sm: c._moved for sm, c in self.actrs.items()}
        entry_cur = {sm: c._cur for sm, c in self.actrs.items()}
        cm = None
        it = 0
        if self.eng_name in engines:
            cm = self.eng.Fori(0, trip)
            it = cm.__enter__()
        self.loop_stack.append(trip)
        body(it)
        self.loop_stack.pop()
        loops_here = self.eng_name in engines
        for sm, c in self.actrs.items():
            moved = c._moved - entry_moved[sm]
            want = delta.get(sm, 0) if loops_here else trip * delta.get(sm, 0)
            corr = want - moved
            if corr and self.eng_name == c.eng_name:
                self.eng.reg_add(c.reg, c.reg, corr)
            c._moved = entry_moved[sm] + trip * delta.get(sm, 0)
            c._cur = entry_cur[sm] + trip * delta.get(sm, 0)
        if cm is not None:
            cm.__exit__(None, None, None)
        for sm in self.cnt:
            self.cnt[sm] = base[sm] + (self.cnt[sm] - base[sm]) * trip
        if top_level:
            for sm in created:
                self.actrs[sm]._free()
                del self.actrs[sm]

    def fori(self, trip, engines=None):
        return _Fori(self, trip, engines or ("tensor", "vector", "scalar", "gpsimd", "sync"))

    def iterctr(self, fori_obj, eng_name, sem_name, base, per_iter):
        c = _IterCtr(self, eng_name, sem_name, per_iter, base)
        fori_obj.ctrs.append(c)
        return c


class _AutoCtr:
    def __init__(self, k, sem, base):
        self.k = k
        self.eng_name = k.eng_name
        self.sem = sem
        self._cur = base
        self._moved = 0
        self.reg = k.eng.alloc_register()
        k.eng.reg_mov(self.reg, base)

    def wait(self, value):
        d = value - self._cur
        if d < 0:
            import traceback
            raise AssertionError(
                f"autoctr {self.sem} eng={self.eng_name}: wait target decreased "
                f"({value} < {self._cur})\n" + "".join(traceback.format_stack()[-6:-1]))
        if d:
            self.k.eng.reg_add(self.reg, self.reg, d)
            self._cur = value
            self._moved += d
        self.k.eng.wait_ge(self.k.sems[self.sem], self.reg)

    def _free(self):
        self.k.eng.free_register(self.reg)


class _IterCtr:
    def __init__(self, k, eng_name, sem_name, per_iter, base):
        self.k = k
        self.eng_name = eng_name
        self.sem = sem_name
        self.per_iter = per_iter
        self.base = base
        self.reg = None
        self.first_off = None
        self.last_off = None

    def wait(self, off):
        if self.k.eng_name != self.eng_name:
            return
        eng = self.k.eng
        if self.reg is None:
            raise RuntimeError("iterctr used before loop entry init")
        if self.first_off is None:
            self.first_off = off
            if off != 0:
                eng.reg_add(self.reg, self.reg, off)
        else:
            d = off - self.last_off
            assert d >= 0, "iterctr offsets must be nondecreasing"
            if d:
                eng.reg_add(self.reg, self.reg, d)
        self.last_off = off
        eng.wait_ge(self.k.sems[self.sem], self.reg)

    def _init(self):
        if self.k.eng_name == self.eng_name:
            self.reg = self.k.eng.alloc_register()
            self.k.eng.reg_mov(self.reg, self.base)

    def _advance(self):
        if self.k.eng_name == self.eng_name and self.last_off is not None:
            d = self.per_iter - self.last_off
            if d:
                self.k.eng.reg_add(self.reg, self.reg, d)

    def _free(self):
        if self.reg is not None:
            self.k.eng.free_register(self.reg)
            self.reg = None


class _Fori:
    def __init__(self, k, trip, engines):
        self.k = k
        self.trip = trip
        self.engines = engines
        self.cm = None
        self.ctrs = []
        self.entered = False

    def __enter__(self):
        self.cnt0 = dict(self.k.cnt)
        return self

    def begin(self):
        for c in self.ctrs:
            c._init()
        self.entered = True
        if self.k.eng_name in self.engines and self.trip > 1:
            self.cm = self.k.eng.Fori(0, self.trip)
            return self.cm.__enter__()
        return 0

    def __exit__(self, *a):
        if self.entered:
            for c in self.ctrs:
                c._advance()
        if self.cm is not None:
            self.cm.__exit__(*a)
        for c in self.ctrs:
            c._free()
        if self.trip > 1:
            for s in self.k.cnt:
                d = self.k.cnt[s] - self.cnt0[s]
                self.k.cnt[s] = self.cnt0[s] + d * self.trip
        return False


def ceil_div(a, b):
    return (a + b - 1) // b


class Mamba:
    C = 512

    def __init__(self, nc, E, pfx, T):
        self.T = T
        self.NCH = T // self.C
        C = self.C
        s = self.sb = {}
        s["Wst"] = E(nc.sbuf_tensor(pfx + "Wst", [128, 8 * 1024], BF16))
        s["Wxp"] = E(nc.sbuf_tensor(pfx + "Wxp", [128, 8 * 32], BF16))
        s["A"] = E(nc.sbuf_tensor(pfx + "A", [128, 128], F32))
        s["wconv"] = E(nc.sbuf_tensor(pfx + "wconv", [128, 32], F32))
        s["bconv"] = E(nc.sbuf_tensor(pfx + "bconv", [128, 8], F32))
        s["bdt"] = E(nc.sbuf_tensor(pfx + "bdt", [128, 8], F32))
        s["gam"] = E(nc.sbuf_tensor(pfx + "gam", [128, 8], F32))
        s["beta"] = E(nc.sbuf_tensor(pfx + "beta", [128, 8], F32))
        s["beta_bf"] = E(nc.sbuf_tensor(pfx + "beta_bf", [128, 8], BF16))
        s["poolD"] = E(nc.sbuf_tensor(pfx + "poolD", [128, 2048], BF16))
        s["states"] = E(nc.sbuf_tensor(pfx + "states", [128, 128], F32))
        s["xT"] = E(nc.sbuf_tensor(pfx + "xT", [128, 8 * C], BF16))
        s["xstg"] = E(nc.sbuf_tensor(pfx + "xstg", [128, 4 * 1024], BF16))
        s["xin"] = E(nc.sbuf_tensor(pfx + "xin", [128, 8 * (C + 3)], BF16))
        s["zg"] = E(nc.sbuf_tensor(pfx + "zg", [128, 8 * C], BF16))
        s["delta"] = E(nc.sbuf_tensor(pfx + "delta", [128, 8 * C], F32))
        s["dg"] = E(nc.sbuf_tensor(pfx + "dg", [128, 8 * C], BF16))
        s["BC"] = E(nc.sbuf_tensor(pfx + "BC", [32, C], BF16))
        s["Bbc"] = E(nc.sbuf_tensor(pfx + "Bbc", [128, 16 * C], BF16))
        s["Cbc"] = E(nc.sbuf_tensor(pfx + "Cbc", [128, 16 * C], BF16))
        s["yT"] = E(nc.sbuf_tensor(pfx + "yT", [128, 8 * C], BF16))
        s["da"] = E(nc.sbuf_tensor(pfx + "da", [128, C], F32))
        s["A16"] = E(nc.sbuf_tensor(pfx + "A16", [128, 16], F32))
        s["epsv"] = E(nc.sbuf_tensor(pfx + "epsv", [128, 1], F32))
        s["st16"] = E(nc.sbuf_tensor(pfx + "st16", [128, 16], F32))
        s["u"] = E(nc.sbuf_tensor(pfx + "u", [128, C], F32))
        s["h"] = E(nc.sbuf_tensor(pfx + "h", [128, C], F32))
        s["hc"] = E(nc.sbuf_tensor(pfx + "hc", [128, C], BF16))
        s["poolA"] = E(nc.sbuf_tensor(pfx + "poolA", [128, 4 * C], F32))
        s["poolC"] = E(nc.sbuf_tensor(pfx + "poolC", [128, 2 * C], BF16))
        s["pstat"] = E(nc.sbuf_tensor(pfx + "pstat", [128, 8], F32))
        s["s1r"] = s["poolA"][0:1, 0:C]
        s["s2r"] = s["poolA"][0:1, C:2 * C]
        s["mur"] = s["poolA"][0:1, 2 * C:3 * C]
        s["varr"] = s["poolA"][0:1, 3 * C:4 * C]
        s["rbf"] = s["poolC"][0:1, 0:C]
        s["mrbf"] = s["poolC"][0:1, C:2 * C]
        s["bprime"] = s["poolD"][0:1, 0:1024]
        s["bout_bf"] = s["poolD"][0:1, 1024:2048]
        s["scr2"] = s["da"][:]
        self.ps = [E(nc.psum_tensor(pfx + f"ps{i}", [128, 512], F32)) for i in range(3)]
        self.ps_tr = E(nc.psum_tensor(pfx + "ps_tr", [128, 512], BF16))
        self.ps_y = E(nc.psum_tensor(pfx + "ps_y", [128, 512], F32))
        self.ps_bc = E(nc.psum_tensor(pfx + "ps_bc", [32, 512], F32))
        self.ps_r1 = E(nc.psum_tensor(pfx + "ps_r1", [1, 512], F32))
        self.ps_r2 = E(nc.psum_tensor(pfx + "ps_r2", [1, 512], F32))

    def emit_prologue(self, k, io, cst):
        s = self.sb
        k.op("gpsimd", lambda g: g.dma_start(
            out=s["bout_bf"], in_=io["bout"]), inc="g_ld", inc_n=16)
        k.op("gpsimd", lambda g: g.dma_start(
            out=s["Wxp"][:].rearrange("p (kt e) -> p kt e", kt=8),
            in_=io["Wxp"].rearrange("(kt p) e -> p kt e", p=128)), inc="g_ld", inc_n=16)
        for snm, inm in (("wconv", "wconv"), ("bconv", "bconv"), ("bdt", "bdt"),
                         ("gam", "gamma"), ("beta", "beta")):
            k.op("sync", lambda sy, snm=snm, inm=inm: sy.dma_start(out=s[snm][:], in_=io[inm]),
                 inc="m_ld", inc_n=16)
        k.op("sync", lambda sy: sy.dma_start(
            out=s["A"][:].rearrange("p (kt n) -> p kt n", kt=8),
            in_=io["A_log"].rearrange("(kt p) n -> p kt n", p=128)), inc="m_ld", inc_n=16)
        nld = k.cnt["m_ld"]
        k.op("scalar", lambda sc: sc.activation(s["A"][:], s["A"][:], AF.Exp),
             wait=[("m_ld", nld)], inc="m_s")
        k.op("vector", lambda v: v.tensor_scalar_mul(s["A"][:], s["A"][:], -1.0),
             wait=[("m_s", k.cnt["m_s"])], inc="m_v")
        k.op("vector", lambda v: v.tensor_copy(s["beta_bf"][:], s["beta"][:]),
             wait=[("m_ld", nld)], inc="m_v")
        k.op("vector", lambda v: v.memset(s["xin"][:], 0.0), inc="m_v")
        k.op("vector", lambda v: v.memset(s["epsv"][:], 1e-5), inc="m_v")
        k.op("vector", lambda v: v.memset(s["states"][:], 0.0), inc="m_v")
        self.prologue_v = k.cnt["m_v"]
        self.prologue_s = k.cnt["m_s"]
        self.lds = nld
        self.glds = k.cnt["g_ld"]

    def emit(self, k, io, cst, x_tm):
        self.emit_prologue(k, io, cst)
        for e in ("tensor", "vector", "scalar", "gpsimd"):
            k.wait(e, "m_ld", self.lds)
            k.wait(e, "g_ld", self.glds)
        self.emit_chunk(k, io, cst, x_tm, 0, first=True)
        if self.NCH > 1:
            k.autoloop(self.NCH - 1,
                       lambda ch: self.emit_chunk(k, io, cst, x_tm, ch + 1, first=False))

    def emit_chunk(self, k, io, cst, x_tm, ch, first=False):
        import os
        stop = os.environ.get("MAMBA_STOP", "Z")
        s = self.sb
        C = self.C
        ident = cst["ident_bf"]
        xoff = ch * (4 * 1024)

        k.op("gpsimd", lambda g: g.dma_start(
            out=s["Wst"][:].rearrange("p (kt e) -> p kt e", kt=8),
            in_=io["Win"][:, 0:1024].rearrange("(kt p) e -> p kt e", p=128)),
            wait=[("m_t", k.cnt["m_t"])], inc="m_wl", inc_n=16)
        wina_ready = k.cnt["m_wl"]

        k.op("vector", lambda v: v.tensor_copy(
            s["xstg"][:], x_tm[:, DynSlice(xoff, 4 * 1024)]), inc="m_v")
        stg_v = k.cnt["m_v"]
        for dt in range(8):
            w = [("m_v", k.cnt["m_v"] if dt > 0 else stg_v)]
            for tt4 in range(4):
                inst = k.op("tensor", lambda t, dt=dt, tt4=tt4: t.transpose(
                    self.ps_tr[:, tt4 * 128:(tt4 + 1) * 128],
                    s["xstg"][:, tt4 * 1024 + dt * 128: tt4 * 1024 + (dt + 1) * 128],
                    ident[:]),
                    wait=(w if tt4 == 0 else []),
                    inc=("m_t" if tt4 == 3 else None))
            k.op("vector", lambda v, dt=dt: v.tensor_copy(
                s["xT"][:, dt * C:(dt + 1) * C], self.ps_tr[:, 0:C]),
                wait=[("m_t", k.cnt["m_t"])], inc="m_v")
        if stop <= "A":
            return
        evb = k.cnt["m_v"]
        for half in range(2):
            if half == 1:
                k.op("gpsimd", lambda g: g.dma_start(
                    out=s["Wst"][:].rearrange("p (kt e) -> p kt e", kt=8),
                    in_=io["Win"][:, 1024:2048].rearrange("(kt p) e -> p kt e", p=128)),
                    wait=[("m_t", k.cnt["m_t"])], inc="m_wl", inc_n=16)
            half_ready = k.cnt["m_wl"]
            for et8 in range(8):
                et = half * 8 + et8
                bank = self.ps[et % 3]
                w = [("m_v", evb + et - 2)] if et >= 3 else []
                if et == 0:
                    w = [("m_v", evb), ("m_wl", wina_ready), ("m_s", k.cnt["m_s"])]
                if et8 == 0 and half == 1:
                    w = w + [("m_wl", half_ready)]
                for kt in range(8):
                    k.op("tensor", lambda t, et8=et8, kt=kt, bank=bank: t.matmul(
                        bank[:, 0:512],
                        s["Wst"][:, kt * 1024 + et8 * 128: kt * 1024 + (et8 + 1) * 128],
                        s["xT"][:, kt * C:(kt + 1) * C],
                        start=(kt == 0), stop=(kt == 7)),
                        wait=(w if kt == 0 else []),
                        inc=("m_t" if kt == 7 else None))
                mmv = k.cnt["m_t"]
                if half == 0:
                    k.op("vector", lambda v, et8=et8, bank=bank: v.tensor_copy(
                        s["xin"][:, et8 * (C + 3) + 3: et8 * (C + 3) + 3 + C], bank[:, 0:512]),
                        wait=[("m_t", mmv)], inc="m_v")
                else:
                    k.op("vector", lambda v, et8=et8, bank=bank: v.tensor_copy(
                        s["zg"][:, et8 * C:(et8 + 1) * C], bank[:, 0:512]),
                        wait=[("m_t", mmv)], inc="m_v")
        k.op("gpsimd", lambda g: g.dma_start(
            out=s["Wst"][:].rearrange("p (kt e) -> p kt e", kt=8),
            in_=io["Wdt"].rearrange("(kt p) e -> p kt e", p=128)),
            wait=[("m_t", k.cnt["m_t"])], inc="m_wl", inc_n=16)
        wdt_ready = k.cnt["m_wl"]
        if stop <= "B":
            return
        for dt in range(8):
            base = dt * (C + 3)
            k.op("vector", lambda v, dt=dt, base=base: v.tensor_scalar(
                s["delta"][:, dt * C:(dt + 1) * C], s["xin"][:, base: base + C],
                s["wconv"][:, dt * 4: dt * 4 + 1], None, op0=ALU.mult), inc="m_v")
            for j in range(1, 4):
                k.op("vector", lambda v, dt=dt, base=base, j=j: v.scalar_tensor_tensor(
                    s["delta"][:, dt * C:(dt + 1) * C], s["xin"][:, base + j: base + j + C],
                    s["wconv"][:, dt * 4 + j: dt * 4 + j + 1],
                    s["delta"][:, dt * C:(dt + 1) * C],
                    op0=ALU.mult, op1=ALU.add), inc="m_v")
            k.op("vector", lambda v, dt=dt, base=base: v.tensor_copy(
                s["xin"][:, base: base + 3], s["xin"][:, base + C: base + C + 3]), inc="m_v")
        conv_done_v = k.cnt["m_v"]
        if stop <= "C":
            return
        for dt in range(8):
            w = [("m_v", conv_done_v if dt == 0 else k.cnt["m_v"])]
            k.op("scalar", lambda sc, dt=dt: sc.activation(
                s["scr2"], s["delta"][:, dt * C:(dt + 1) * C], AF.Silu,
                bias=s["bconv"][:, dt: dt + 1]), wait=w, inc="m_s")
            k.op("vector", lambda v, dt=dt: v.tensor_tensor(
                s["zg"][:, dt * C:(dt + 1) * C], s["scr2"],
                s["zg"][:, dt * C:(dt + 1) * C], op=ALU.mult),
                wait=[("m_s", k.cnt["m_s"])], inc="m_v")
        if stop <= "D":
            return
        gate_v = k.cnt["m_v"]
        for mt in range(8):
            bank = self.ps[mt % 2]
            w = ([("m_s", k.cnt["m_s"] - 1)] if mt >= 2
                 else [("m_wl", wdt_ready), ("m_v", gate_v)])
            for kt in range(8):
                k.op("tensor", lambda t, mt=mt, kt=kt, bank=bank: t.matmul(
                    bank[:, 0:512],
                    s["Wst"][:, kt * 1024 + mt * 128: kt * 1024 + (mt + 1) * 128],
                    s["zg"][:, kt * C:(kt + 1) * C],
                    start=(kt == 0), stop=(kt == 7)),
                    wait=(w if kt == 0 else []),
                    inc=("m_t" if kt == 7 else None))
            k.op("scalar", lambda sc, mt=mt, bank=bank: sc.activation(
                s["delta"][:, mt * C:(mt + 1) * C], bank[:, 0:512], AF.Identity,
                bias=s["bdt"][:, mt: mt + 1]),
                wait=[("m_t", k.cnt["m_t"])], inc="m_s")
        sp_s0 = k.cnt["m_s"]
        for mt in range(8):
            dslice = s["delta"][:, mt * C:(mt + 1) * C]
            k.op("vector", lambda v, dslice=dslice: v.scalar_tensor_tensor(
                s["scr2"], dslice, -1.0, dslice, op0=ALU.mult, op1=ALU.max),
                wait=[("m_s", k.cnt["m_s"])], inc="m_v")
            k.op("scalar", lambda sc: sc.activation(s["scr2"], s["scr2"], AF.Exp,
                                                    scale=-1.0),
                 wait=[("m_v", k.cnt["m_v"])], inc="m_s")
            k.op("scalar", lambda sc: sc.activation(s["scr2"], s["scr2"], AF.Ln,
                                                    bias=1.0), inc="m_s")
            k.op("vector", lambda v, dslice=dslice: v.scalar_tensor_tensor(
                dslice, dslice, 0.0, s["scr2"], op0=ALU.max, op1=ALU.add),
                wait=[("m_s", k.cnt["m_s"])], inc="m_v")
        if stop <= "E":
            return
        k.op("sync", lambda sy: sy.dma_start(
            out=s["Wst"][0:32, 0:4096], in_=cst["sel32_hbm"]),
            wait=[("m_t", k.cnt["m_t"])], inc="m_sl", inc_n=16)
        sel_ready = k.cnt["m_sl"]
        for kt in range(8):
            k.op("tensor", lambda t, kt=kt: t.matmul(
                self.ps_bc[:, 0:512],
                s["Wxp"][:, kt * 32:(kt + 1) * 32],
                s["zg"][:, kt * C:(kt + 1) * C],
                start=(kt == 0), stop=(kt == 7)),
                wait=([("m_v", k.cnt["m_v"])] if kt == 0 else []),
                inc=("m_t" if kt == 7 else None))
        k.op("vector", lambda v: v.tensor_copy(s["BC"][:], self.ps_bc[:, 0:C]),
             wait=[("m_t", k.cnt["m_t"])], inc="m_v")
        for dt in range(8):
            k.op("vector", lambda v, dt=dt: v.tensor_tensor(
                s["dg"][:, dt * C:(dt + 1) * C], s["delta"][:, dt * C:(dt + 1) * C],
                s["zg"][:, dt * C:(dt + 1) * C], op=ALU.mult), inc="m_v")
        if stop <= "F":
            return
        bc_v = k.cnt["m_v"]
        for n in range(32):
            bank = self.ps[n % 2]
            tgt = s["Bbc"] if n < 16 else s["Cbc"]
            col = (n % 16) * C
            w = [("m_v", bc_v + n - 1)] if n >= 2 else [("m_v", bc_v)]
            if n < 2:
                w = w + [("m_sl", sel_ready)]
            k.op("tensor", lambda t, n=n, bank=bank, w=w: t.matmul(
                bank[:, 0:512], s["Wst"][0:32, n * 128:(n + 1) * 128],
                s["BC"][:, 0:C], start=True, stop=True),
                wait=w, inc="m_t")
            k.op("vector", lambda v, bank=bank, tgt=tgt, col=col: v.tensor_copy(
                tgt[:, col: col + C], bank[:, 0:512]),
                wait=[("m_t", k.cnt["m_t"])], inc="m_v")
        if stop <= "G":
            return
        k.op("gpsimd", lambda g: g.dma_start(
            out=s["Wst"][:].rearrange("p (kt e) -> p kt e", kt=8),
            in_=io["Wout"].rearrange("(kt p) e -> p kt e", p=128)),
            wait=[("m_s", k.cnt["m_s"]), ("m_t", k.cnt["m_t"])],
            inc="m_wl", inc_n=16)
        wout_ready = k.cnt["m_wl"]
        k.wait("vector", "m_v", k.cnt["m_v"])

        def scan_body(dt):
            dcol = dt * C
            d16 = dt * 16
            sl_c = DynSlice(dcol, C)
            sl_16 = DynSlice(d16, 16)
            ev0 = k.cnt["m_e"]
            k.op("vector", lambda v: v.tensor_copy(
                s["A16"][:], s["A"][:, sl_16]), inc="m_v")
            k.op("vector", lambda v: v.tensor_copy(
                s["st16"][:], s["states"][:, sl_16]), inc="m_v")
            prev_scan_v = k.cnt["m_v"]
            for n in range(16):
                k.op("scalar", lambda sc, n=n: sc.activation(
                    s["da"][:], s["delta"][:, sl_c], AF.Exp,
                    scale=s["A16"][:, n:n + 1]),
                    wait=[("m_v", prev_scan_v)], inc="m_s")
                k.op("vector", lambda v, n=n: v.tensor_tensor(
                    s["u"][:], s["dg"][:, sl_c], s["Bbc"][:, n * C:(n + 1) * C],
                    op=ALU.mult), inc="m_v")
                k.op("vector", lambda v, n=n: v.tensor_tensor_scan(
                    s["h"][:], s["da"][:], s["u"][:],
                    s["st16"][:, n:n + 1], op0=ALU.mult, op1=ALU.add),
                    wait=[("m_s", k.cnt["m_s"])], inc="m_v")
                prev_scan_v = k.cnt["m_v"]
                k.op("vector", lambda v, n=n: v.tensor_copy(
                    s["st16"][:, n:n + 1], s["h"][:, C - 1:C]), inc="m_v")
                k.op("vector", lambda v, n=n: v.tensor_tensor(
                    s["hc"][:], s["h"][:], s["Cbc"][:, n * C:(n + 1) * C],
                    op=ALU.mult),
                    wait=[("m_t", k.cnt["m_t"])], inc="m_v")
                k.op("tensor", lambda t, n=n: t.matmul(
                    self.ps_y[:, 0:512], ident[:], s["hc"][:],
                    start=(n == 0), stop=(n == 15)),
                    wait=[("m_v", k.cnt["m_v"])] + ([("m_e", ev0)] if n == 0 else []),
                    inc="m_t")
            k.op("vector", lambda v: v.tensor_copy(
                s["states"][:, sl_16], s["st16"][:]), inc="m_v")
            k.op("vector", lambda v: v.tensor_tensor(
                s["yT"][:, sl_c], self.ps_y[:, 0:512], s["xT"][:, sl_c],
                op=ALU.add),
                wait=[("m_t", k.cnt["m_t"])], inc="m_e")

        k.autoloop(8, scan_body, engines=("tensor", "vector", "scalar"))
        k.wait("vector", "m_wl", wout_ready)
        if first:
            k.wait("tensor", "m_wl", wout_ready)
            for dt in range(8):
                k.op("tensor", lambda t, dt=dt: t.matmul(
                    self.ps_r1[:, 0:512], s["beta_bf"][:, dt: dt + 1],
                    s["Wst"][:, dt * 1024: dt * 1024 + 512],
                    start=(dt == 0), stop=False),
                    wait=([("m_v", k.cnt["m_v"]), ("m_e", k.cnt["m_e"])] if dt == 0 else []),
                    inc=None)
                k.op("tensor", lambda t, dt=dt: t.matmul(
                    self.ps_r2[:, 0:512], s["beta_bf"][:, dt: dt + 1],
                    s["Wst"][:, dt * 1024 + 512: dt * 1024 + 1024],
                    start=(dt == 0), stop=False), inc=None)
            k.op("tensor", lambda t: t.matmul(
                self.ps_r1[:, 0:512], cst["ones1x128"][0:1, 0:1],
                s["bout_bf"][:, 0:512], start=False, stop=True), inc="m_t")
            k.op("tensor", lambda t: t.matmul(
                self.ps_r2[:, 0:512], cst["ones1x128"][0:1, 0:1],
                s["bout_bf"][:, 512:1024], start=False, stop=True), inc="m_t")
            k.op("vector", lambda v: v.tensor_copy(s["bprime"][:, 0:512], self.ps_r1[0:1, 0:512]),
                 wait=[("m_t", k.cnt["m_t"])], inc="m_v")
            k.op("vector", lambda v: v.tensor_copy(s["bprime"][:, 512:1024], self.ps_r2[0:1, 0:512]),
                 inc="m_v")
        bp_t = k.cnt["m_t"]
        for dt in range(8):
            k.op("vector", lambda v, dt=dt: v.tensor_scalar(
                s["Wst"][:, dt * 1024:(dt + 1) * 1024], s["Wst"][:, dt * 1024:(dt + 1) * 1024],
                s["gam"][:, dt: dt + 1], None, op0=ALU.mult),
                wait=([("m_t", bp_t)] if dt == 0 else []), inc="m_v")
        iln_v0 = k.cnt["m_v"]
        iln_s0 = k.cnt["m_s"]
        for dt in range(8):
            sqbuf = (s["xin"][:, 3:3 + C] if dt % 2 == 0
                     else s["xin"][:, (C + 3) + 3:(C + 3) + 3 + C])
            w = [("m_e", k.cnt["m_e"])] if dt == 0 else []
            if dt >= 2:
                w = [("m_t", k.cnt["m_t"] - 1)]
            k.op("scalar", lambda sc, dt=dt, sqbuf=sqbuf: sc.activation(
                sqbuf, s["yT"][:, dt * C:(dt + 1) * C], AF.Square),
                wait=w, inc="m_s")
            k.op("tensor", lambda t, dt=dt: t.matmul(
                self.ps_r1[:, 0:512], cst["ones128col_bf"][:, 0:1],
                s["yT"][:, dt * C:(dt + 1) * C],
                start=(dt == 0), stop=(dt == 7)),
                wait=([("m_e", k.cnt["m_e"]), ("m_v", k.cnt["m_v"])] if dt == 0 else []),
                inc=None)
            k.op("tensor", lambda t, dt=dt, sqbuf=sqbuf: t.matmul(
                self.ps_r2[:, 0:512], cst["ones128col_bf"][:, 0:1],
                sqbuf,
                start=(dt == 0), stop=(dt == 7)),
                wait=[("m_s", k.cnt["m_s"])],
                inc="m_t")
        sums_t = k.cnt["m_t"]
        k.op("vector", lambda v: v.tensor_copy(s["s1r"], self.ps_r1[0:1, 0:C]),
             wait=[("m_t", sums_t)], inc="m_v")
        k.op("vector", lambda v: v.tensor_copy(s["s2r"], self.ps_r2[0:1, 0:C]),
             inc="m_v")
        k.op("vector", lambda v: v.tensor_scalar_mul(s["mur"], s["s1r"], 1.0 / 1024.0),
             wait=[("m_v", k.cnt["m_v"])], inc="m_v")
        k.op("vector", lambda v: v.tensor_tensor(s["varr"], s["mur"],
                                                 s["mur"], op=ALU.mult),
             wait=[("m_v", k.cnt["m_v"])], inc="m_v")
        k.op("vector", lambda v: v.scalar_tensor_tensor(
            s["varr"], s["s2r"], 1.0 / 1024.0, s["varr"],
            op0=ALU.mult, op1=ALU.subtract), inc="m_v")
        k.op("scalar", lambda sc: sc.activation(s["varr"], s["varr"],
                                                AF.Ln, bias=s["epsv"][0:1, 0:1]),
             wait=[("m_v", k.cnt["m_v"])], inc="m_s")
        k.op("scalar", lambda sc: sc.activation(s["varr"], s["varr"],
                                                AF.Exp, scale=-0.5),
             wait=[("m_s", k.cnt["m_s"])], inc="m_s")
        k.op("vector", lambda v: v.tensor_copy(s["rbf"], s["varr"]),
             wait=[("m_s", k.cnt["m_s"])], inc="m_v")
        k.op("vector", lambda v: v.tensor_tensor(s["mur"], s["mur"],
                                                 s["varr"], op=ALU.mult), inc="m_v")
        k.op("vector", lambda v: v.tensor_copy(s["mrbf"], s["mur"]),
             wait=[("m_v", k.cnt["m_v"])], inc="m_v")
        k.op("tensor", lambda t: t.matmul(
            self.ps[0][:, 0:512], cst["ones1x128"][0:1, :], s["rbf"],
            start=True, stop=True), wait=[("m_v", k.cnt["m_v"])], inc="m_t")
        k.op("tensor", lambda t: t.matmul(
            self.ps[1][:, 0:512], cst["ones1x128"][0:1, :], s["mrbf"],
            start=True, stop=True), inc="m_t")
        k.op("vector", lambda v: v.tensor_copy(s["Bbc"][:, 0:C], self.ps[0][:, 0:512]),
             wait=[("m_t", k.cnt["m_t"])], inc="m_v")
        k.op("vector", lambda v: v.tensor_copy(s["Bbc"][:, C:2 * C], self.ps[1][:, 0:512]),
             inc="m_v")
        for dt in range(8):
            k.op("vector", lambda v, dt=dt: v.tensor_tensor(
                s["u"][:], s["yT"][:, dt * C:(dt + 1) * C], s["Bbc"][:, 0:C], op=ALU.mult),
                inc="m_v")
            k.op("vector", lambda v, dt=dt: v.tensor_tensor(
                s["xT"][:, dt * C:(dt + 1) * C], s["u"][:], s["Bbc"][:, C:2 * C],
                op=ALU.subtract), inc="m_v")
        apply_v = k.cnt["m_v"]
        if stop <= "I":
            return
        for tt4 in range(4):
            pa, pb = self.ps[0], self.ps[1]
            w0 = [("m_v", apply_v)] if tt4 == 0 else [("m_s", k.cnt["m_s"])]
            for nch in range(2):
                bank = pa if nch == 0 else pb
                for dt in range(8):
                    k.op("tensor", lambda t, tt4=tt4, nch=nch, dt=dt, bank=bank: t.matmul(
                        bank[:, 0:512],
                        s["xT"][:, dt * C + tt4 * 128: dt * C + (tt4 + 1) * 128],
                        s["Wst"][:, dt * 1024 + nch * 512: dt * 1024 + (nch + 1) * 512],
                        start=(dt == 0), stop=False),
                        wait=(w0 if (nch == 0 and dt == 0) else []), inc=None)
                k.op("tensor", lambda t, tt4=tt4, nch=nch, bank=bank: t.matmul(
                    bank[:, 0:512], cst["ones1x128"][0:1, :],
                    s["bprime"][0:1, nch * 512:(nch + 1) * 512],
                    start=False, stop=False), inc=None)
                k.op("tensor", lambda t, tt4=tt4, nch=nch, bank=bank: t.matmul(
                    bank[:, 0:512], ident[:],
                    x_tm[:, DynSlice(xoff + (tt4 * 1024 + nch * 512), 512)],
                    start=False, stop=True),
                    inc=("m_t" if nch == 1 else None))
            self.emit_postln_psum(k, cst, [pa, pb], x_tm, xoff, tt4 * 1024, "m_t", k.cnt["m_t"])

    def emit_postln_psum(self, k, cst, banks, x_tm, xoff, xcol, tsem, tval):
        s = self.sb
        st = s["pstat"]
        k.op("vector", lambda v: v.tensor_reduce(st[:, 0:1], banks[0][:, 0:512],
                                                 axis=AX.X, op=ALU.add),
             wait=[(tsem, tval)], inc="m_v")
        v_r0 = k.cnt["m_v"]
        k.op("vector", lambda v: v.tensor_reduce(st[:, 1:2], banks[1][:, 0:512],
                                                 axis=AX.X, op=ALU.add), inc="m_v")
        v_r1 = k.cnt["m_v"]
        k.op("scalar", lambda sc: sc.activation(s["da"][:], banks[0][:, 0:512],
                                                AF.Square, accum_out=st[:, 2:3]),
             wait=[(tsem, tval), ("m_v", v_r0)], inc="m_s")
        k.op("scalar", lambda sc: sc.activation(s["u"][:], banks[1][:, 0:512],
                                                AF.Square, accum_out=st[:, 3:4]),
             wait=[("m_v", v_r1)], inc="m_s")
        k.op("vector", lambda v: v.tensor_tensor(st[:, 0:1], st[:, 0:1], st[:, 1:2], op=ALU.add),
             wait=[("m_s", k.cnt["m_s"])], inc="m_v")
        k.op("vector", lambda v: v.tensor_scalar_mul(st[:, 4:5], st[:, 0:1], 1.0 / 1024.0),
             wait=[("m_v", k.cnt["m_v"])], inc="m_v")
        k.op("vector", lambda v: v.tensor_tensor(st[:, 2:3], st[:, 2:3], st[:, 3:4], op=ALU.add),
             inc="m_v")
        k.op("vector", lambda v: v.tensor_tensor(st[:, 5:6], st[:, 4:5], st[:, 4:5], op=ALU.mult),
             wait=[("m_v", k.cnt["m_v"])], inc="m_v")
        k.op("vector", lambda v: v.scalar_tensor_tensor(
            st[:, 5:6], st[:, 2:3], 1.0 / 1024.0, st[:, 5:6], op0=ALU.mult, op1=ALU.subtract),
            wait=[("m_v", k.cnt["m_v"])], inc="m_v")
        vv = k.cnt["m_v"]
        k.op("scalar", lambda sc: sc.activation(st[:, 5:6], st[:, 5:6], AF.Ln,
                                                bias=s["epsv"][:, 0:1]),
             wait=[("m_v", vv)], inc="m_s")
        k.op("scalar", lambda sc: sc.activation(st[:, 5:6], st[:, 5:6], AF.Exp, scale=-0.5),
             wait=[("m_s", k.cnt["m_s"])], inc="m_s")
        k.op("vector", lambda v: v.tensor_tensor(st[:, 6:7], st[:, 4:5], st[:, 5:6], op=ALU.mult),
             wait=[("m_s", k.cnt["m_s"])], inc="m_v")
        k.op("vector", lambda v: v.tensor_scalar_mul(st[:, 6:7], st[:, 6:7], -1.0),
             wait=[("m_v", k.cnt["m_v"])], inc="m_v")
        vv2 = k.cnt["m_v"]
        for h in range(2):
            k.op("scalar", lambda sc, h=h: sc.activation(
                x_tm[:, DynSlice(xoff + (xcol + h * 512), 512)], banks[h][:, 0:512],
                AF.Identity, bias=st[:, 6:7], scale=st[:, 5:6]),
                wait=[("m_v", vv2)] if h == 0 else [], inc="m_s")


def make_consts():
    ident = np.eye(128, dtype=BF16NP)
    ones1x128 = np.ones((1, 128), dtype=BF16NP)
    ones1x512 = np.ones((1, 512), dtype=BF16NP)
    zeros1x512 = np.zeros((1, 512), dtype=BF16NP)
    ident_f32 = np.eye(128, dtype=np.float32)
    ones128col = np.ones((128, 1), dtype=BF16NP)
    sel = np.zeros((32, 32 * 128), dtype=BF16NP)
    for n in range(32):
        sel[n, n * 128:(n + 1) * 128] = 1
    return {"ident_bf": ident, "ones1x128": ones1x128, "ones1x512": ones1x512,
            "zeros1x512": zeros1x512, "ones128col_bf": ones128col, "sel32": sel,
            "ident_f32": ident_f32,
            "ones1x128f": np.ones((1, 128), dtype=np.float32),
            "ones128colf": np.ones((128, 1), dtype=np.float32),
            "zeros1x512f": np.zeros((1, 512), dtype=np.float32)}


def const_tensors(nc, E, k_unused=None, only=None):
    cst_np = make_consts()
    aps = {}
    sb = {}
    for nm, arr in cst_np.items():
        dt = F32 if arr.dtype == np.float32 else BF16
        aps[nm] = nc.dram_tensor("c_" + nm, list(arr.shape), dt, kind="ExternalInput").ap()
        if nm != "sel32" and (only is None or nm in only):
            sb[nm] = E(nc.sbuf_tensor("cs_" + nm, list(arr.shape), dt))
    return cst_np, aps, sb


MAMBA_W_NAMES = ("Win", "Wdt", "Wxp", "Wout", "A_log", "wconv", "bconv", "bdt",
                 "gamma", "beta", "bout")


def mamba_host_prep(inp):
    f = np.ascontiguousarray
    def tile8(v):
        return f(v.reshape(8, 128).T)
    return {
        "Win": f(inp["L0_attn_W_in"].T),
        "Wdt": f(inp["L0_attn_W_dt"].T),
        "Wxp": f(inp["L0_attn_W_xproj"].T),
        "Wout": f(inp["L0_attn_W_out"].T),
        "A_log": f(inp["L0_attn_A_log"]),
        "wconv": f(inp["L0_attn_W_conv"][:, 0, :].reshape(8, 128, 4).transpose(1, 0, 2).reshape(128, 32)),
        "bconv": tile8(inp["L0_attn_b_conv"]),
        "bdt": tile8(inp["L0_attn_b_dt"]),
        "gamma": tile8(inp["L0_attn_ln_gamma"]),
        "beta": tile8(inp["L0_attn_ln_beta"]),
        "bout": f(inp["L0_attn_b_out"].reshape(1, 1024)),
    }


def build_mamba_program(T):
    from contextlib import ExitStack
    nc = bass.Bass("TRN2", target_bir_lowering=False, detect_race_conditions=False)
    ntt = T // 128
    x_in = nc.dram_tensor("x", [T, 1024], F32, kind="ExternalInput").ap()
    y_out = nc.dram_tensor("y", [T, 1024], BF16, kind="ExternalOutput").ap()
    dbg_out = nc.dram_tensor("dbg", [128, 8192], F32, kind="ExternalOutput").ap()
    io = {nm: nc.dram_tensor("w_" + nm, None, F32, kind="ExternalInput") for nm in ()}
    io = {}
    shp = {"Win": [1024, 2048], "Wdt": [1024, 1024], "Wxp": [1024, 32],
           "Wout": [1024, 1024], "A_log": [1024, 16], "wconv": [128, 32],
           "bconv": [128, 8], "bdt": [128, 8], "gamma": [128, 8], "beta": [128, 8],
           "bout": [1, 1024]}
    for nm in MAMBA_W_NAMES:
        io[nm] = nc.dram_tensor("w_" + nm, shp[nm], F32, kind="ExternalInput").ap()
    ctx = ExitStack()
    with ctx:
        E = ctx.enter_context
        cst_np, cst_aps, cst_sb = const_tensors(nc, E)
        x_tm = E(nc.sbuf_tensor("x_tm", [128, ntt * 1024], BF16))
        mam = Mamba(nc, E, "m_", T)
        sems = {}
        for nm in ("m_ld", "g_ld", "m_wl", "m_t", "m_v", "m_s", "m_e", "st", "g_st", "m_sl"):
            sems[nm] = E(nc.semaphore(nm))
        blk = E(nc.Block())
        k = K(nc, sems, {})

        def emit_all(k):
            for nm in cst_sb:
                k.op("sync", lambda sy, nm=nm: sy.dma_start(out=cst_sb[nm][:], in_=cst_aps[nm]),
                     inc="m_ld", inc_n=16)
            k.op("gpsimd", lambda g: g.dma_start(
                out=x_tm[:].rearrange("p (t d) -> p t d", t=ntt),
                in_=x_in.rearrange("(t p) d -> p t d", p=128)), inc="g_ld", inc_n=16)
            cst = {nm: cst_sb[nm][:] for nm in cst_sb}
            cst["sel32_hbm"] = cst_aps["sel32"]
            mam.emit(k, io, cst, x_tm[:])
            k.op("sync", lambda sy: sy.dma_start(
                out=y_out.rearrange("(t p) d -> p t d", p=128),
                in_=x_tm[:].rearrange("p (t d) -> p t d", t=ntt)),
                wait=[("m_s", k.cnt["m_s"]), ("m_v", k.cnt["m_v"])],
                inc="st", inc_n=16)
            import os
            dmp = os.environ.get("MAMBA_DUMP")
            if dmp:
                src = mam.sb[dmp]
                ncol = min(src.shape[1], 8192, int(os.environ.get('MAMBA_DUMP_NCOL', '8192')))
                k.op("gpsimd", lambda g: g.dma_start(
                    out=dbg_out[0:src.shape[0], 0:ncol], in_=src[:, 0:ncol]),
                    wait=[("m_s", k.cnt["m_s"]), ("m_v", k.cnt["m_v"]),
                          ("m_t", k.cnt["m_t"]), ("m_e", k.cnt["m_e"])],
                    inc="g_st", inc_n=16)
                k.wait("sync", "st", 16)
                k.wait("sync", "g_st", 16)
            else:
                k.wait("sync", "st", 16)

        def section(name):
            def f(eng):
                k.run(name, eng, [emit_all])
            return f
        blk.sync(section("sync"))
        blk.gpsimd(section("gpsimd"))
        blk.tensor(section("tensor"))
        blk.vector(section("vector"))
        blk.scalar(section("scalar"))
    return nc, cst_np


def run_mamba_test(x, weights, T, sim=False):
    nc, cst_np = build_mamba_program(T)
    in_map = {"x": np.ascontiguousarray(x[:T]).astype(np.float32)}
    for nm in MAMBA_W_NAMES:
        in_map["w_" + nm] = weights[nm]
    for nm, arr in cst_np.items():
        in_map["c_" + nm] = arr
    if sim:
        from concourse.bass_interp import MultiCoreSim
        simu = MultiCoreSim(nc, num_cores=1, num_workers=1, trace=False)
        for kk, v in in_map.items():
            simu.cores[0].tensor(kk)[:] = v
        simu.simulate(check_with_hw=False)
        return np.asarray(simu.cores[0].tensor("y"))
    res = run_bass_kernel_spmd(nc, [in_map], core_ids=[0])
    return np.asarray(res.results[0]["y"])


def emit_postln(k, sb, banks_or_sbuf, x_out_slices, tsem, tval, from_psum=True):
    st = sb["pstat"]
    halves = banks_or_sbuf
    k.op("vector", lambda v: v.tensor_reduce(st[:, 0:1], halves[0], axis=AX.X, op=ALU.add),
         wait=[(tsem, tval)], inc="m_v")
    v_r0 = k.cnt["m_v"]
    k.op("vector", lambda v: v.tensor_reduce(st[:, 1:2], halves[1], axis=AX.X, op=ALU.add),
         inc="m_v")
    v_r1 = k.cnt["m_v"]
    k.op("scalar", lambda sc: sc.activation(sb["sq_a"][:], halves[0], AF.Square,
                                            accum_out=st[:, 2:3]),
         wait=[(tsem, tval), ("m_v", v_r0)], inc="m_s")
    k.op("scalar", lambda sc: sc.activation(sb["sq_b"][:], halves[1], AF.Square,
                                            accum_out=st[:, 3:4]),
         wait=[("m_v", v_r1)], inc="m_s")
    k.op("vector", lambda v: v.tensor_tensor(st[:, 0:1], st[:, 0:1], st[:, 1:2], op=ALU.add),
         wait=[("m_s", k.cnt["m_s"])], inc="m_v")
    k.op("vector", lambda v: v.tensor_scalar_mul(st[:, 4:5], st[:, 0:1], 1.0 / 1024.0),
         wait=[("m_v", k.cnt["m_v"])], inc="m_v")
    k.op("vector", lambda v: v.tensor_tensor(st[:, 2:3], st[:, 2:3], st[:, 3:4], op=ALU.add),
         inc="m_v")
    k.op("vector", lambda v: v.tensor_tensor(st[:, 5:6], st[:, 4:5], st[:, 4:5], op=ALU.mult),
         wait=[("m_v", k.cnt["m_v"])], inc="m_v")
    k.op("vector", lambda v: v.scalar_tensor_tensor(
        st[:, 5:6], st[:, 2:3], 1.0 / 1024.0, st[:, 5:6], op0=ALU.mult, op1=ALU.subtract),
        wait=[("m_v", k.cnt["m_v"])], inc="m_v")
    vv = k.cnt["m_v"]
    k.op("scalar", lambda sc: sc.activation(st[:, 5:6], st[:, 5:6], AF.Ln,
                                            bias=sb["epsv"][:, 0:1]),
         wait=[("m_v", vv)], inc="m_s")
    k.op("scalar", lambda sc: sc.activation(st[:, 5:6], st[:, 5:6], AF.Exp, scale=-0.5),
         wait=[("m_s", k.cnt["m_s"])], inc="m_s")
    k.op("vector", lambda v: v.tensor_tensor(st[:, 6:7], st[:, 4:5], st[:, 5:6], op=ALU.mult),
         wait=[("m_s", k.cnt["m_s"])], inc="m_v")
    k.op("vector", lambda v: v.tensor_scalar_mul(st[:, 6:7], st[:, 6:7], -1.0),
         wait=[("m_v", k.cnt["m_v"])], inc="m_v")
    vv2 = k.cnt["m_v"]
    for h in range(2):
        k.op("scalar", lambda sc, h=h: sc.activation(
            x_out_slices[h], halves[h], AF.Identity, bias=st[:, 6:7], scale=st[:, 5:6]),
            wait=[("m_v", vv2)] if h == 0 else [], inc="m_s")


class Attn:

    def __init__(self, nc, E, pfx, mode):
        self.mode = mode
        self.pfx = pfx
        s = self.sb = {}
        self.scr = nc.dram_tensor(pfx + "qkv_scr", [3072, T], BF16, kind="Internal").ap()
        if mode != "L2":
            self.oscr = nc.dram_tensor(pfx + "o_scr", [1024, T], BF16, kind="Internal").ap()
        s["W"] = E(nc.sbuf_tensor(pfx + "W", [128, 8 * 3072], BF16))
        s["Wo"] = E(nc.sbuf_tensor(pfx + "Wo", [128, 8 * 1024], BF16)) if mode != "L2" else None
        s["bias_qkv"] = E(nc.sbuf_tensor(pfx + "bq", [1, 3072], BF16))
        s["bias_o"] = E(nc.sbuf_tensor(pfx + "bo", [1, 1024], BF16))
        s["xstg"] = E(nc.sbuf_tensor(pfx + "xstg", [128, 4 * 1024], BF16))
        s["xTc"] = E(nc.sbuf_tensor(pfx + "xTc", [128, 8 * 512], BF16))
        s["qkvstg"] = E(nc.sbuf_tensor(pfx + "qkvstg", [128, 24 * 512], BF16))
        s["ostg"] = E(nc.sbuf_tensor(pfx + "ostg", [128, 4 * 1024], BF16))
        if mode == "L2":
            s["kblk"] = E(nc.sbuf_tensor(pfx + "kblk", [128, 2 * 8 * 512], BF16))
            s["vblk"] = E(nc.sbuf_tensor(pfx + "vblk", [128, 2 * 8 * 512], BF16))
            s["vtm"] = E(nc.sbuf_tensor(pfx + "vtm", [128, 2 * 4 * 1024], BF16))
            s["qg"] = E(nc.sbuf_tensor(pfx + "qg", [128, 8 * 512], BF16))
            s["oacc"] = E(nc.sbuf_tensor(pfx + "oacc", [128, 4 * 1024], F32))
        else:
            s["qh"] = E(nc.sbuf_tensor(pfx + "qh", [64, T], BF16))
            s["kh"] = E(nc.sbuf_tensor(pfx + "kh", [64, T], BF16))
            s["vh"] = E(nc.sbuf_tensor(pfx + "vh", [64, T], BF16))
            s["vtm"] = E(nc.sbuf_tensor(pfx + "vtm", [128, 32 * 64], BF16))
            s["oh"] = E(nc.sbuf_tensor(pfx + "oh", [128, 32 * 64], BF16))
            s["oacc"] = E(nc.sbuf_tensor(pfx + "oacc", [128, 4 * 64], F32))
            s["oT"] = E(nc.sbuf_tensor(pfx + "oT", [128, 8 * 512], BF16))
            s["oTstg"] = E(nc.sbuf_tensor(pfx + "oTstg", [64, T], BF16))
        s["P"] = E(nc.sbuf_tensor(pfx + "P", [128, 512], BF16))
        s["pT"] = E(nc.sbuf_tensor(pfx + "pT", [128, 512], BF16))
        s["negm"] = E(nc.sbuf_tensor(pfx + "negm", [128, 32], F32))
        s["masks"] = E(nc.sbuf_tensor(pfx + "masks", [128, 4 * 512], F32))
        s["pstat"] = E(nc.sbuf_tensor(pfx + "pstat", [128, 8], F32))
        s["sq_a"] = E(nc.sbuf_tensor(pfx + "sq_a", [128, 512], BF16))
        s["sq_b"] = E(nc.sbuf_tensor(pfx + "sq_b", [128, 512], BF16))
        s["epsv"] = E(nc.sbuf_tensor(pfx + "epsv", [128, 1], F32))
        if mode == "L3":
            s["slopes"] = E(nc.sbuf_tensor(pfx + "slopes", [128, 16], F32))
            s["kaug"] = E(nc.sbuf_tensor(pfx + "kaug", [2, T], BF16))
            s["qaugc"] = E(nc.sbuf_tensor(pfx + "qaugc", [2, 128], F32))
            s["qaug"] = E(nc.sbuf_tensor(pfx + "qaug", [2, 128], BF16))
        self.ps = [E(nc.psum_tensor(pfx + f"ps{i}", [128, 512], F32)) for i in range(4)]
        self.ps_tr = E(nc.psum_tensor(pfx + "ps_tr", [128, 512], BF16))
        self.ps_av = E(nc.psum_tensor(pfx + "ps_av", [128, 512], F32))
        self.ps_av2 = E(nc.psum_tensor(pfx + "ps_av2", [128, 512], F32))

    def emit_phaseQ(self, k, io, cst):
        s = self.sb
        mode = self.mode
        if mode == "L1":
            k.op("gpsimd", lambda g: g.dma_start(
                out=s["W"][:].rearrange("p (kt e) -> p kt e", kt=8),
                in_=io["w_qkv"].rearrange("(kt p) e -> p kt e", p=128)), inc="g_ld", inc_n=16)
        elif mode == "L2":
            for i, nm in enumerate(("Wq", "Wk", "Wv")):
                k.op("gpsimd", lambda g, i=i, nm=nm: g.dma_start(
                    out=s["W"][:].rearrange("p (kt e) -> p kt e", kt=8)[:, :, i * 1024:(i + 1) * 1024],
                    in_=io[nm].rearrange("(kt p) e -> p kt e", p=128)), inc="g_ld", inc_n=16)
        else:
            k.op("gpsimd", lambda g: g.dma_start(
                out=s["W"][:].rearrange("p (kt e) -> p kt e", kt=8)[:, :, 0:1024],
                in_=io["q_w"].rearrange("(kt p) e -> p kt e", p=128)), inc="g_ld", inc_n=16)
            k.op("gpsimd", lambda g: g.dma_start(
                out=s["W"][:].rearrange("p (kt e) -> p kt e", kt=8)[:, :, 1024:3072],
                in_=io["kv_w"].rearrange("(kt p) e -> p kt e", p=128)), inc="g_ld", inc_n=16)
            k.op("gpsimd", lambda g: g.dma_start(out=s["bias_qkv"][0:1, 0:1024], in_=io["q_b"]),
                 inc="g_ld", inc_n=16)
            k.op("gpsimd", lambda g: g.dma_start(out=s["bias_qkv"][0:1, 1024:3072], in_=io["kv_b"]),
                 inc="g_ld", inc_n=16)
        if mode != "L2":
            k.op("gpsimd", lambda g: g.dma_start(
                out=s["Wo"][:].rearrange("p (kt e) -> p kt e", kt=8),
                in_=io["w_out"].rearrange("(kt p) e -> p kt e", p=128)), inc="g_ld", inc_n=16)
        if mode == "L3":
            k.op("gpsimd", lambda g: g.dma_start(out=s["bias_o"][:], in_=io["o_b"]),
                 inc="g_ld", inc_n=16)
            k.op("sync", lambda sy: sy.dma_start(out=s["slopes"][:], in_=io["slopes"]),
                 inc="m_ld", inc_n=16)
            k.op("sync", lambda sy: sy.dma_start(out=s["kaug"][:], in_=io["kaug"]),
                 inc="m_ld", inc_n=16)
            k.op("sync", lambda sy: sy.dma_start(out=s["qaugc"][:], in_=io["qaugc"]),
                 inc="m_ld", inc_n=16)
        k.op("sync", lambda sy: sy.dma_start(out=s["masks"][:], in_=io["masks"]),
             inc="m_ld", inc_n=16)
        for e in ("tensor", "vector", "scalar", "gpsimd", "sync"):
            k.wait(e, "m_ld", k.cnt["m_ld"])
            k.wait(e, "g_ld", k.cnt["g_ld"])
        k.op("vector", lambda v: v.memset(s["epsv"][:], 1e-5), inc="m_v")

        ident = cst["ident_bf"]

        def body(ch):
            xoff = ch * 512
            k.op("gpsimd", lambda g: g.dma_start(
                out=s["xstg"][:].rearrange("p (a d) -> p a d", a=4),
                in_=io["x_in"].rearrange("(t p) d -> p t d", p=128)[:, DynSlice(ch * 4, 4), :]),
                wait=[("m_t", k.cnt["m_t"])], inc="g_x", inc_n=16)
            gx = k.cnt["g_x"]
            for dt in range(8):
                w = [("g_x", gx)] if dt == 0 else []
                w.append(("m_v", k.cnt["m_v"]))
                for tt4 in range(4):
                    k.op("tensor", lambda t, dt=dt, tt4=tt4: t.transpose(
                        self.ps_tr[:, tt4 * 128:(tt4 + 1) * 128],
                        s["xstg"][:, tt4 * 1024 + dt * 128: tt4 * 1024 + (dt + 1) * 128],
                        ident[:]),
                        wait=(w if tt4 == 0 else []),
                        inc=("m_t" if tt4 == 3 else None))
                k.op("vector", lambda v, dt=dt: v.tensor_copy(
                    s["xTc"][:, dt * 512:(dt + 1) * 512], self.ps_tr[:, 0:512]),
                    wait=[("m_t", k.cnt["m_t"])], inc="m_v")
            xt_v = k.cnt["m_v"]
            evq = k.cnt["m_v"]
            for mt in range(24):
                bank = self.ps[mt % 3]
                w = [("m_v", evq + mt - 2)] if mt >= 3 else [("m_v", xt_v)]
                for kt in range(8):
                    k.op("tensor", lambda t, mt=mt, kt=kt, bank=bank: t.matmul(
                        bank[:, 0:512],
                        s["W"][:, kt * 3072 + mt * 128: kt * 3072 + (mt + 1) * 128],
                        s["xTc"][:, kt * 512:(kt + 1) * 512],
                        start=(kt == 0), stop=(kt == 7 and self.mode != "L3")),
                        wait=(w if kt == 0 else []),
                        inc=(None if self.mode == "L3" else ("m_t" if kt == 7 else None)))
                if self.mode == "L3":
                    k.op("tensor", lambda t, mt=mt, bank=bank: t.matmul(
                        bank[:, 0:512],
                        s["bias_qkv"][0:1, mt * 128:(mt + 1) * 128],
                        cst["ones1x512"][0:1, :],
                        start=False, stop=True), inc="m_t")
                if self.mode == "L3" and mt < 8:
                    k.op("vector", lambda v, mt=mt, bank=bank: v.tensor_scalar(
                        s["qkvstg"][:, mt * 512:(mt + 1) * 512], bank[:, 0:512],
                        0.125, None, op0=ALU.mult),
                        wait=[("m_t", k.cnt["m_t"])], inc="m_v")
                else:
                    k.op("vector", lambda v, mt=mt, bank=bank: v.tensor_copy(
                        s["qkvstg"][:, mt * 512:(mt + 1) * 512], bank[:, 0:512]),
                        wait=[("m_t", k.cnt["m_t"])], inc="m_v")
            k.op("gpsimd", lambda g: g.dma_start(
                out=self.scr.rearrange("(mt p) t -> p mt t", p=128)[:, :, DynSlice(xoff, 512)],
                in_=s["qkvstg"][:].rearrange("p (mt c) -> p mt c", mt=24)),
                wait=[("m_v", k.cnt["m_v"])], inc="g_st", inc_n=16)

        k.autoloop(8, body)
        for e in ("tensor", "vector", "scalar", "gpsimd", "sync"):
            k.wait(e, "g_st", k.cnt["g_st"])


    def emit_phaseA(self, k, io, cst):
        if self.mode == "L2":
            self.emit_phaseA_L2(k, io, cst)
        else:
            self.emit_phaseA_heads(k, io, cst)

    def emit_phaseA_heads(self, k, io, cst):
        s = self.sb
        mode = self.mode
        scale = 0.125 if mode == "L1" else 1.0
        ident = cst["ident_bf"]

        def head_body(h):
            k.op("sync", lambda sy: sy.dma_start(
                out=s["qh"][:], in_=self.scr[DynSlice(h * 64, 64), :]),
                wait=[("m_t", k.cnt["m_t"])], inc="m_qk", inc_n=16)
            k.op("sync", lambda sy: sy.dma_start(
                out=s["kh"][:], in_=self.scr[DynSlice(1024 + h * 64, 64), :]),
                inc="m_qk", inc_n=16)
            k.op("sync", lambda sy: sy.dma_start(
                out=s["vh"][:], in_=self.scr[DynSlice(2048 + h * 64, 64), :]),
                inc="m_qk", inc_n=16)
            ldv = k.cnt["m_qk"]
            if mode == "L3":
                k.op("vector", lambda v: v.tensor_scalar(
                    s["qaug"][:], s["qaugc"][:], s["slopes"][0:2, DynSlice(h, 1)], None,
                    op0=ALU.mult),
                    wait=[("m_t", k.cnt["m_t"])], inc="m_v")
            for tt in range(32):
                w = [("m_qk", ldv)] if tt == 0 else []
                if tt == 0:
                    w.append(("m_v", k.cnt["m_v"]))
                k.op("tensor", lambda t, tt=tt: t.transpose(
                    self.ps_tr[:, (tt % 8) * 64:(tt % 8) * 64 + 64],
                    s["vh"][:, tt * 128:(tt + 1) * 128], ident[0:64, 0:64]),
                    wait=w + ([("m_v", k.cnt["m_v"])] if tt >= 8 and tt % 8 == 0 else []),
                    inc=("m_t" if tt % 8 == 7 else None))
                if tt % 8 == 7:
                    k.op("vector", lambda v, tt=tt: v.tensor_copy(
                        s["vtm"][:, (tt - 7) * 64:(tt + 1) * 64], self.ps_tr[:, 0:512]),
                        wait=[("m_t", k.cnt["m_t"])], inc="m_v")
            vtm_v = k.cnt["m_v"]
            for g in range(8):
                kbs = list(range(0, g + 1)) if mode == "L1" else list(range(max(g - 1, 0), g + 1))
                for qt in range(4):
                    qcol = g * 512 + qt * 128
                    nm = s["negm"][:, 0:1]
                    fcol = s["negm"][:, 1:2]
                    lcol = s["negm"][:, 2:3]
                    lb = s["negm"][:, 3:4]
                    dm = s["negm"][:, 4:5]
                    k.op("vector", lambda v: v.memset(s["negm"][:, 0:8], 0.0),
                         wait=[("m_av", k.cnt["m_av"])], inc="m_v")
                    k.op("vector", lambda v, nm=nm: v.memset(nm, 1e30),
                         wait=[("m_v", k.cnt["m_v"])], inc="m_v")
                    k.op("vector", lambda v: v.memset(s["oacc"][:, qt * 64:(qt + 1) * 64], 0.0),
                         inc="m_v")
                    for kb in kbs:
                        bank = self.ps[kb % 2]
                        k.op("tensor", lambda t, qcol=qcol, kb=kb, bank=bank: t.matmul(
                            bank[:, 0:512],
                            s["qh"][:, qcol:qcol + 128],
                            s["kh"][:, kb * 512:(kb + 1) * 512],
                            start=True, stop=(mode == "L1")),
                            wait=[("m_v", k.cnt["m_v"]), ("m_s", k.cnt["m_s"])],
                            inc=("m_t" if mode == "L1" else None))
                        if mode == "L3":
                            k.op("tensor", lambda t, qt=qt, kb=kb, bank=bank: t.matmul(
                                bank[:, 0:512], s["qaug"][:, 0:128],
                                s["kaug"][:, kb * 512:(kb + 1) * 512],
                                start=False, stop=True),
                                inc="m_t")
                        mm_t = k.cnt["m_t"]
                        if kb == g:
                            k.op("vector", lambda v, qt=qt, bank=bank: v.tensor_tensor(
                                bank[:, 0:512], bank[:, 0:512],
                                s["masks"][:, qt * 512:(qt + 1) * 512], op=ALU.add),
                                wait=[("m_t", mm_t)], inc="m_v")
                        k.op("vector", lambda v, bank=bank: v.tensor_reduce(
                            lb, bank[:, 0:512], axis=AX.X, op=ALU.max, negate=True),
                            wait=[("m_t", mm_t), ("m_v", k.cnt["m_v"])], inc="m_v")
                        if mode == "L3":
                            K_c = (kb - g) * 512 - qt * 128 + 192
                            k.op("vector", lambda v, K_c=K_c: v.scalar_tensor_tensor(
                                lb, s["slopes"][:, DynSlice(h, 1)], float(-K_c), lb,
                                op0=ALU.mult, op1=ALU.add),
                                wait=[("m_v", k.cnt["m_v"])], inc="m_v")
                        k.op("vector", lambda v: v.tensor_tensor(dm, nm, lb, op=ALU.min),
                             wait=[("m_v", k.cnt["m_v"])], inc="m_v")
                        k.op("vector", lambda v: v.tensor_tensor(fcol, dm, nm, op=ALU.subtract),
                             wait=[("m_v", k.cnt["m_v"])], inc="m_v")
                        k.op("vector", lambda v: v.tensor_copy(nm, dm),
                             wait=[("m_v", k.cnt["m_v"])], inc="m_v")
                        k.op("scalar", lambda sc: sc.activation(fcol, fcol, AF.Exp, scale=scale),
                             wait=[("m_v", k.cnt["m_v"])], inc="m_s")
                        if mode == "L3":
                            K_c = (kb - g) * 512 - qt * 128 + 192
                            k.op("vector", lambda v, K_c=K_c: v.scalar_tensor_tensor(
                                dm, s["slopes"][:, DynSlice(h, 1)], float(K_c), nm,
                                op0=ALU.mult, op1=ALU.add),
                                wait=[("m_s", k.cnt["m_s"])], inc="m_v")
                            bsrc = dm
                        else:
                            bsrc = nm
                        k.op("vector", lambda v, bsrc=bsrc: v.tensor_scalar_mul(
                            s["negm"][:, 5:6], bsrc, scale),
                            wait=[("m_v", k.cnt["m_v"]), ("m_s", k.cnt["m_s"])], inc="m_v")
                        k.op("scalar", lambda sc, bank=bank: sc.activation(
                            s["P"][:], bank[:, 0:512], AF.Exp, bias=s["negm"][:, 5:6],
                            scale=scale, accum_out=s["negm"][:, 6:7]),
                            wait=[("m_v", k.cnt["m_v"])], inc="m_s")
                        k.op("vector", lambda v: v.scalar_tensor_tensor(
                            lcol, lcol, fcol, s["negm"][:, 6:7], op0=ALU.mult, op1=ALU.add),
                            wait=[("m_s", k.cnt["m_s"]), ("m_v", k.cnt["m_v"])], inc="m_v")
                        for c4 in range(4):
                            k.op("tensor", lambda t, c4=c4: t.transpose(
                                self.ps_tr[:, c4 * 128:(c4 + 1) * 128],
                                s["P"][:, c4 * 128:(c4 + 1) * 128], ident[:]),
                                wait=([("m_s", k.cnt["m_s"]), ("m_v", k.cnt["m_v"])] if c4 == 0 else []),
                                inc=("m_t" if c4 == 3 else None))
                        k.op("vector", lambda v: v.tensor_copy(s["pT"][:], self.ps_tr[:, 0:512]),
                             wait=[("m_t", k.cnt["m_t"])], inc="m_v")
                        for c4 in range(4):
                            k.op("tensor", lambda t, kb=kb, c4=c4: t.matmul(
                                self.ps_av[:, 0:64],
                                s["pT"][:, c4 * 128:(c4 + 1) * 128],
                                s["vtm"][:, (kb * 4 + c4) * 64:(kb * 4 + c4 + 1) * 64],
                                start=(c4 == 0), stop=(c4 == 3)),
                                wait=([("m_v", k.cnt["m_v"])] if c4 == 0 else []),
                                inc=("m_av" if c4 == 3 else None))
                        k.op("vector", lambda v, qt=qt: v.scalar_tensor_tensor(
                            s["oacc"][:, qt * 64:(qt + 1) * 64],
                            s["oacc"][:, qt * 64:(qt + 1) * 64], fcol,
                            self.ps_av[:, 0:64], op0=ALU.mult, op1=ALU.add),
                            wait=[("m_av", k.cnt["m_av"])], inc="m_v")
                    k.op("vector", lambda v: v.reciprocal(fcol, lcol),
                         wait=[("m_v", k.cnt["m_v"])], inc="m_v")
                    k.op("vector", lambda v, g=g, qt=qt: v.tensor_scalar(
                        s["oh"][:, (g * 4 + qt) * 64:(g * 4 + qt + 1) * 64],
                        s["oacc"][:, qt * 64:(qt + 1) * 64], fcol, None, op0=ALU.mult),
                        wait=[("m_v", k.cnt["m_v"])], inc="m_v")
            for tt in range(32):
                w = [("m_v", k.cnt["m_v"])] if tt % 4 == 0 else []
                k.op("tensor", lambda t, tt=tt: t.transpose(
                    self.ps_tr[0:64, (tt % 4) * 128:(tt % 4) * 128 + 128],
                    s["oh"][:, tt * 64:(tt + 1) * 64], ident[:]),
                    wait=w, inc=("m_t" if tt % 4 == 3 else None))
                if tt % 4 == 3:
                    k.op("vector", lambda v, tt=tt: v.tensor_copy(
                        s["oTstg"][0:64, (tt - 3) * 128:(tt + 1) * 128],
                        self.ps_tr[0:64, 0:512]),
                        wait=[("m_t", k.cnt["m_t"])], inc="m_v")
            k.op("gpsimd", lambda g: g.dma_start(
                out=self.oscr[DynSlice(h * 64, 64), :], in_=s["oTstg"][0:64, :]),
                wait=[("m_v", k.cnt["m_v"])], inc="g_st", inc_n=16)

        k.autoloop(16, head_body, engines=("tensor", "vector", "scalar", "sync", "gpsimd"))
        for e in ("tensor", "vector", "scalar", "gpsimd", "sync"):
            k.wait(e, "g_st", k.cnt["g_st"])

    def emit_phaseO(self, k, io, cst):
        s = self.sb
        ident = cst["ident_bf"]

        def body(ch):
            k.op("sync", lambda sy: sy.dma_start(
                out=s["oT"][:].rearrange("p (kt c) -> p kt c", kt=8),
                in_=self.oscr.rearrange("(kt p) t -> p kt t", p=128)[:, :, DynSlice(ch * 512, 512)]),
                wait=[("m_t", k.cnt["m_t"])], inc="m_qk", inc_n=16)
            k.op("gpsimd", lambda g: g.dma_start(
                out=s["xstg"][:].rearrange("p (a d) -> p a d", a=4),
                in_=io["x_in"].rearrange("(t p) d -> p t d", p=128)[:, DynSlice(ch * 4, 4), :]),
                wait=[("m_t", k.cnt["m_t"])], inc="g_x", inc_n=16)
            ld = k.cnt["m_qk"]
            gx = k.cnt["g_x"]
            for tt4 in range(4):
                pa, pb = self.ps[0], self.ps[1]
                w0 = [("m_qk", ld), ("g_x", gx), ("m_s", k.cnt["m_s"])]
                for nch in range(2):
                    bank = pa if nch == 0 else pb
                    for kt in range(8):
                        k.op("tensor", lambda t, tt4=tt4, nch=nch, kt=kt, bank=bank: t.matmul(
                            bank[:, 0:512],
                            s["oT"][:, kt * 512 + tt4 * 128: kt * 512 + (tt4 + 1) * 128],
                            s["Wo"][:, kt * 1024 + nch * 512: kt * 1024 + (nch + 1) * 512],
                            start=(kt == 0), stop=False),
                            wait=(w0 if (nch == 0 and kt == 0) else []), inc=None)
                    if self.mode == "L3":
                        k.op("tensor", lambda t, nch=nch, bank=bank: t.matmul(
                            bank[:, 0:512], cst["ones1x128"][0:1, :],
                            s["bias_o"][0:1, nch * 512:(nch + 1) * 512],
                            start=False, stop=False), inc=None)
                    k.op("tensor", lambda t, tt4=tt4, nch=nch, bank=bank: t.matmul(
                        bank[:, 0:512], ident[:],
                        s["xstg"][:, tt4 * 1024 + nch * 512: tt4 * 1024 + (nch + 1) * 512],
                        start=False, stop=True),
                        inc=("m_t" if nch == 1 else None))
                emit_postln(k, s, [pa[:, 0:512], pb[:, 0:512]],
                            [s["ostg"][:, tt4 * 1024: tt4 * 1024 + 512],
                             s["ostg"][:, tt4 * 1024 + 512: tt4 * 1024 + 1024]],
                            "m_t", k.cnt["m_t"])
            k.op("gpsimd", lambda g: g.dma_start(
                out=io["x_out"].rearrange("(t p) d -> p t d", p=128)[:, DynSlice(ch * 4, 4), :],
                in_=s["ostg"][:].rearrange("p (a d) -> p a d", a=4)),
                wait=[("m_s", k.cnt["m_s"])], inc="g_st", inc_n=16)

        k.autoloop(8, body)
        k.wait("sync", "g_st", k.cnt["g_st"])

    def emit_phaseA_L2(self, k, io, cst):
        s = self.sb
        ident = cst["ident_bf"]
        scale = 1.0 / 32.0
        for g in range(8):
            k.op("sync", lambda sy, g=g: sy.dma_start(
                out=s["qg"][:].rearrange("p (kt c) -> p kt c", kt=8),
                in_=self.scr[0:1024, :].rearrange("(kt p) t -> p kt t", p=128)[:, :, g * 512:(g + 1) * 512]),
                wait=[("m_t", k.cnt["m_t"])], inc="m_qk", inc_n=16)
            ldq = k.cnt["m_qk"]
            k.op("gpsimd", lambda g_, g=g: g_.dma_start(
                out=s["xstg"][:].rearrange("p (a d) -> p a d", a=4),
                in_=io["x_in"].rearrange("(t p) d -> p t d", p=128)[:, g * 4:(g + 1) * 4, :]),
                wait=[("m_s", k.cnt["m_s"])], inc="g_x", inc_n=16)
            gx = k.cnt["g_x"]
            for qt in range(4):
                k.op("vector", lambda v, qt=qt: v.memset(s["negm"][:, qt * 8: qt * 8 + 8], 0.0),
                     wait=[("m_av", k.cnt["m_av"])] if qt == 0 else [], inc="m_v")
                k.op("vector", lambda v, qt=qt: v.memset(s["negm"][:, qt * 8: qt * 8 + 1], 1e30),
                     wait=[("m_v", k.cnt["m_v"])], inc="m_v")
                k.op("vector", lambda v, qt=qt: v.memset(s["oacc"][:, qt * 1024:(qt + 1) * 1024], 0.0),
                     inc="m_v")
            init_v = k.cnt["m_v"]
            for kb in range(g + 1):
                par = kb % 2
                k.op("sync", lambda sy, kb=kb, par=par: sy.dma_start(
                    out=s["kblk"][:, par * 4096:(par + 1) * 4096].rearrange("p (kt c) -> p kt c", kt=8),
                    in_=self.scr[1024:2048, :].rearrange("(kt p) t -> p kt t", p=128)[:, :, kb * 512:(kb + 1) * 512]),
                    wait=[("m_t", k.cnt["m_t"])], inc="m_qk", inc_n=16)
                k.op("sync", lambda sy, kb=kb, par=par: sy.dma_start(
                    out=s["vblk"][:, par * 4096:(par + 1) * 4096].rearrange("p (kt c) -> p kt c", kt=8),
                    in_=self.scr[2048:3072, :].rearrange("(kt p) t -> p kt t", p=128)[:, :, kb * 512:(kb + 1) * 512]),
                    inc="m_qk", inc_n=16)
                ldkv = k.cnt["m_qk"]
                for tt4 in range(4):
                    for dt in range(8):
                        k.op("tensor", lambda t, tt4=tt4, dt=dt, par=par: t.transpose(
                            self.ps_tr[:, (dt % 4) * 128:(dt % 4) * 128 + 128],
                            s["vblk"][:, par * 4096 + dt * 512 + tt4 * 128:
                                      par * 4096 + dt * 512 + (tt4 + 1) * 128],
                            ident[:]),
                            wait=([("m_qk", ldkv), ("m_v", k.cnt["m_v"])] if dt % 4 == 0 else []),
                            inc=("m_t" if dt % 4 == 3 else None))
                        if dt % 4 == 3:
                            k.op("vector", lambda v, tt4=tt4, dt=dt, par=par: v.tensor_copy(
                                s["vtm"][:, par * 4096 + tt4 * 1024 + (dt - 3) * 128:
                                         par * 4096 + tt4 * 1024 + (dt + 1) * 128],
                                self.ps_tr[:, 0:512]),
                                wait=[("m_t", k.cnt["m_t"])], inc="m_v")
                vtm_v = k.cnt["m_v"]
                for qt in range(4):
                    nm = s["negm"][:, qt * 8: qt * 8 + 1]
                    fcol = s["negm"][:, qt * 8 + 1: qt * 8 + 2]
                    lcol = s["negm"][:, qt * 8 + 2: qt * 8 + 3]
                    lb = s["negm"][:, qt * 8 + 3: qt * 8 + 4]
                    dm = s["negm"][:, qt * 8 + 4: qt * 8 + 5]
                    bexp = s["negm"][:, qt * 8 + 5: qt * 8 + 6]
                    lsum = s["negm"][:, qt * 8 + 6: qt * 8 + 7]
                    bank = self.ps[2 + qt % 2]
                    for kt in range(8):
                        k.op("tensor", lambda t, qt=qt, kt=kt, par=par, bank=bank: t.matmul(
                            bank[:, 0:512],
                            s["qg"][:, kt * 512 + qt * 128: kt * 512 + (qt + 1) * 128],
                            s["kblk"][:, par * 4096 + kt * 512:par * 4096 + (kt + 1) * 512],
                            start=(kt == 0), stop=(kt == 7)),
                            wait=([("m_v", k.cnt["m_v"]), ("m_s", k.cnt["m_s"]),
                                   ("m_qk", ldkv if qt == 0 else 0),
                                   ("m_qk", ldq)] if kt == 0 else []),
                            inc=("m_t" if kt == 7 else None))
                    mm_t = k.cnt["m_t"]
                    if kb == g:
                        k.op("vector", lambda v, qt=qt, bank=bank: v.tensor_tensor(
                            bank[:, 0:512], bank[:, 0:512],
                            s["masks"][:, qt * 512:(qt + 1) * 512], op=ALU.add),
                            wait=[("m_t", mm_t)], inc="m_v")
                    k.op("vector", lambda v, bank=bank, lb=lb: v.tensor_reduce(
                        lb, bank[:, 0:512], axis=AX.X, op=ALU.max, negate=True),
                        wait=[("m_t", mm_t), ("m_v", k.cnt["m_v"])], inc="m_v")
                    k.op("vector", lambda v, dm=dm, nm=nm, lb=lb: v.tensor_tensor(dm, nm, lb, op=ALU.min),
                         wait=[("m_v", k.cnt["m_v"])], inc="m_v")
                    k.op("vector", lambda v, fcol=fcol, dm=dm, nm=nm: v.tensor_tensor(fcol, dm, nm, op=ALU.subtract),
                         wait=[("m_v", k.cnt["m_v"])], inc="m_v")
                    k.op("vector", lambda v, nm=nm, dm=dm: v.tensor_copy(nm, dm),
                         wait=[("m_v", k.cnt["m_v"])], inc="m_v")
                    k.op("scalar", lambda sc, fcol=fcol: sc.activation(fcol, fcol, AF.Exp, scale=scale),
                         wait=[("m_v", k.cnt["m_v"])], inc="m_s")
                    k.op("vector", lambda v, bexp=bexp, nm=nm: v.tensor_scalar_mul(bexp, nm, scale),
                         wait=[("m_v", k.cnt["m_v"]), ("m_s", k.cnt["m_s"])], inc="m_v")
                    k.op("scalar", lambda sc, bank=bank, bexp=bexp, lsum=lsum: sc.activation(
                        s["P"][:], bank[:, 0:512], AF.Exp, bias=bexp,
                        scale=scale, accum_out=lsum),
                        wait=[("m_v", k.cnt["m_v"])], inc="m_s")
                    k.op("vector", lambda v, lcol=lcol, fcol=fcol, lsum=lsum: v.scalar_tensor_tensor(
                        lcol, lcol, fcol, lsum, op0=ALU.mult, op1=ALU.add),
                        wait=[("m_s", k.cnt["m_s"]), ("m_v", k.cnt["m_v"])], inc="m_v")
                    for c4 in range(4):
                        k.op("tensor", lambda t, c4=c4: t.transpose(
                            self.ps_tr[:, c4 * 128:(c4 + 1) * 128],
                            s["P"][:, c4 * 128:(c4 + 1) * 128], ident[:]),
                            wait=([("m_s", k.cnt["m_s"]), ("m_v", k.cnt["m_v"])] if c4 == 0 else []),
                            inc=("m_t" if c4 == 3 else None))
                    k.op("vector", lambda v: v.tensor_copy(s["pT"][:], self.ps_tr[:, 0:512]),
                         wait=[("m_t", k.cnt["m_t"])], inc="m_v")
                    for half in range(2):
                        pav = self.ps_av if half == 0 else self.ps_av2
                        for c4 in range(4):
                            k.op("tensor", lambda t, c4=c4, half=half, pav=pav, par=par: t.matmul(
                                pav[:, 0:512],
                                s["pT"][:, c4 * 128:(c4 + 1) * 128],
                                s["vtm"][:, par * 4096 + c4 * 1024 + half * 512:
                                         par * 4096 + c4 * 1024 + (half + 1) * 512],
                                start=(c4 == 0), stop=(c4 == 3)),
                                wait=([("m_v", k.cnt["m_v"])] if (c4 == 0 and half == 0) else []),
                                inc=("m_av" if c4 == 3 else None))
                    for half in range(2):
                        pav = self.ps_av if half == 0 else self.ps_av2
                        k.op("vector", lambda v, qt=qt, half=half, pav=pav, fcol=fcol: v.scalar_tensor_tensor(
                            s["oacc"][:, qt * 1024 + half * 512: qt * 1024 + (half + 1) * 512],
                            s["oacc"][:, qt * 1024 + half * 512: qt * 1024 + (half + 1) * 512],
                            fcol, pav[:, 0:512], op0=ALU.mult, op1=ALU.add),
                            wait=[("m_av", k.cnt["m_av"])], inc="m_v")
            for qt in range(4):
                fcol = s["negm"][:, qt * 8 + 1: qt * 8 + 2]
                lcol = s["negm"][:, qt * 8 + 2: qt * 8 + 3]
                k.op("vector", lambda v, fcol=fcol, lcol=lcol: v.reciprocal(fcol, lcol),
                     wait=[("m_v", k.cnt["m_v"])], inc="m_v")
                for half in range(2):
                    k.op("vector", lambda v, qt=qt, half=half, fcol=fcol: v.tensor_scalar(
                        s["oacc"][:, qt * 1024 + half * 512: qt * 1024 + (half + 1) * 512],
                        s["oacc"][:, qt * 1024 + half * 512: qt * 1024 + (half + 1) * 512],
                        fcol, None, op0=ALU.mult),
                        wait=([("m_v", k.cnt["m_v"])] if half == 0 else []), inc="m_v")
                    k.op("vector", lambda v, qt=qt, half=half: v.tensor_tensor(
                        s["oacc"][:, qt * 1024 + half * 512: qt * 1024 + (half + 1) * 512],
                        s["oacc"][:, qt * 1024 + half * 512: qt * 1024 + (half + 1) * 512],
                        s["xstg"][:, qt * 1024 + half * 512: qt * 1024 + (half + 1) * 512],
                        op=ALU.add),
                        wait=([("g_x", gx)] if (qt == 0 and half == 0) else []), inc="m_v")
                emit_postln(k, s, [s["oacc"][:, qt * 1024: qt * 1024 + 512],
                                   s["oacc"][:, qt * 1024 + 512: qt * 1024 + 1024]],
                            [s["ostg"][:, qt * 1024: qt * 1024 + 512],
                             s["ostg"][:, qt * 1024 + 512: qt * 1024 + 1024]],
                            "m_v", k.cnt["m_v"], from_psum=False)
            k.op("gpsimd", lambda g_, g=g: g_.dma_start(
                out=io["x_out"].rearrange("(t p) d -> p t d", p=128)[:, g * 4:(g + 1) * 4, :],
                in_=s["ostg"][:].rearrange("p (a d) -> p a d", a=4)),
                wait=[("m_s", k.cnt["m_s"])], inc="g_st", inc_n=16)
        k.wait("sync", "g_st", k.cnt["g_st"])

    def emit(self, k, io, cst):
        self.emit_phaseQ(k, io, cst)
        self.emit_phaseA(k, io, cst)
        if self.mode != "L2":
            self.emit_phaseO(k, io, cst)



def attn_masks_np():
    m = np.zeros((128, 4 * 512), np.float32)
    for qt in range(4):
        i = np.arange(128)[:, None] + qt * 128
        j = np.arange(512)[None, :]
        m[:, qt * 512:(qt + 1) * 512] = np.where(i >= j, 0.0, -1e9)
    return m


def attn_host_prep(inp, mode):
    f = np.ascontiguousarray
    io = {"masks": attn_masks_np()}
    if mode == "L1":
        io["w_qkv"] = f(inp["L1_attn_w_qkv"])
        io["w_out"] = f(inp["L1_attn_w_out"])
    elif mode == "L2":
        io["Wq"] = f(inp["L2_attn_Wq"])
        io["Wk"] = f(inp["L2_attn_Wk"])
        io["Wv"] = f(inp["L2_attn_Wv"])
    else:
        io["q_w"] = f(inp["L3_attn_q_w"])
        io["kv_w"] = f(inp["L3_attn_kv_w"])
        io["w_out"] = f(inp["L3_attn_o_w"])
        io["q_b"] = f(inp["L3_attn_q_b"].reshape(1, 1024))
        io["kv_b"] = f(inp["L3_attn_kv_b"].reshape(1, 2048))
        io["o_b"] = f(inp["L3_attn_o_b"].reshape(1, 1024))
        slopes = (np.arange(1, 17, dtype=np.float32) / 16.0)
        io["slopes"] = f(np.tile(slopes[None, :], (128, 1)))
        j = np.arange(T)
        kaug = np.stack([(j % 512) - 256.0, np.ones(T)]).astype(BF16NP)
        io["kaug"] = kaug
        qi = np.arange(128)
        io["qaugc"] = np.stack([np.ones(128), -(qi - 64.0)]).astype(np.float32)
    return io


ATTN_W = {
    "L1": [("w_qkv", [1024, 3072], F32), ("w_out", [1024, 1024], F32),
           ("masks", [128, 2048], F32)],
    "L2": [("Wq", [1024, 1024], F32), ("Wk", [1024, 1024], F32),
           ("Wv", [1024, 1024], F32), ("masks", [128, 2048], F32)],
    "L3": [("q_w", [1024, 1024], F32), ("kv_w", [1024, 2048], F32),
           ("w_out", [1024, 1024], F32), ("q_b", [1, 1024], F32),
           ("kv_b", [1, 2048], F32), ("o_b", [1, 1024], F32),
           ("slopes", [128, 16], F32), ("kaug", [2, T], BF16),
           ("qaugc", [2, 128], F32), ("masks", [128, 2048], F32)],
}


def build_attn_program(mode):
    from contextlib import ExitStack
    nc = bass.Bass("TRN2", target_bir_lowering=False, detect_race_conditions=False)
    x_in = nc.dram_tensor("x", [T, 1024], BF16, kind="ExternalInput").ap()
    x_out = nc.dram_tensor("y", [T, 1024], BF16, kind="ExternalOutput").ap()
    io = {"x_in": x_in, "x_out": x_out}
    for nm, shp, dt in ATTN_W[mode]:
        io[nm] = nc.dram_tensor("w_" + nm, shp, dt, kind="ExternalInput").ap()
    ctx = ExitStack()
    with ctx:
        E = ctx.enter_context
        cst_np, cst_aps, cst_sb = const_tensors(nc, E)
        blk_obj = Attn(nc, E, "a_", mode)
        sems = {}
        for nm in ("m_ld", "g_ld", "m_t", "m_v", "m_s", "m_qk", "m_av", "g_x", "g_st", "st"):
            sems[nm] = E(nc.semaphore(nm))
        blk = E(nc.Block())
        k = K(nc, sems, {})

        def emit_all(k):
            for nm in cst_sb:
                k.op("sync", lambda sy, nm=nm: sy.dma_start(out=cst_sb[nm][:], in_=cst_aps[nm]),
                     inc="m_ld", inc_n=16)
            cst = {nm: cst_sb[nm][:] for nm in cst_sb}
            blk_obj.emit(k, io, cst)

        def section(name):
            def f(eng):
                k.run(name, eng, [emit_all])
            return f
        blk.sync(section("sync"))
        blk.gpsimd(section("gpsimd"))
        blk.tensor(section("tensor"))
        blk.vector(section("vector"))
        blk.scalar(section("scalar"))
    return nc, cst_np


def run_attn_test(x_bf16, io_np, mode, sim=False):
    nc, cst_np = build_attn_program(mode)
    in_map = {"x": x_bf16}
    for nm, arr in io_np.items():
        in_map["w_" + nm] = arr
    for nm, arr in cst_np.items():
        in_map["c_" + nm] = arr
    if sim:
        from concourse.bass_interp import MultiCoreSim
        simu = MultiCoreSim(nc, num_cores=1, num_workers=1, trace=False)
        for kk, v in in_map.items():
            simu.cores[0].tensor(kk)[:] = v
        simu.simulate(check_with_hw=False)
        return np.asarray(simu.cores[0].tensor("y"))
    res = run_bass_kernel_spmd(nc, [in_map], core_ids=[0])
    return np.asarray(res.results[0]["y"])


class L1FFN:
    CH = 256

    def __init__(self, nc, E, pfx):
        s = self.sb = {}
        C = self.CH
        s["rw"] = E(nc.sbuf_tensor(pfx + "rw", [128, 8 * 32], BF16))
        s["rb"] = E(nc.sbuf_tensor(pfx + "rb", [1, 32], BF16))
        s["sel"] = E(nc.sbuf_tensor(pfx + "sel", [32, 32 * 128], BF16))
        s["selbuf"] = E(nc.sbuf_tensor(pfx + "selbuf", [32, 128], BF16))
        s["wg"] = E(nc.sbuf_tensor(pfx + "wg", [128, 2 * 1024], BF16))
        s["wu"] = E(nc.sbuf_tensor(pfx + "wu", [128, 2 * 1024], BF16))
        s["wd"] = E(nc.sbuf_tensor(pfx + "wd", [128, 2 * 1024], BF16))
        s["wsg"] = E(nc.sbuf_tensor(pfx + "wsg", [128, 8 * 512], BF16))
        s["wsu"] = E(nc.sbuf_tensor(pfx + "wsu", [128, 8 * 512], BF16))
        s["wsd"] = E(nc.sbuf_tensor(pfx + "wsd", [128, 4 * 1024], BF16))
        s["xstg"] = E(nc.sbuf_tensor(pfx + "xstg", [128, 2 * 1024], BF16))
        s["xTc"] = E(nc.sbuf_tensor(pfx + "xTc", [128, 8 * C], BF16))
        s["ostg"] = E(nc.sbuf_tensor(pfx + "ostg", [128, 2 * 1024], BF16))
        s["aff"] = E(nc.sbuf_tensor(pfx + "aff", [128, 32], F32))
        s["wtok"] = E(nc.sbuf_tensor(pfx + "wtok", [128, 32], BF16))
        s["wT"] = E(nc.sbuf_tensor(pfx + "wT", [32, C], BF16))
        s["wbc"] = E(nc.sbuf_tensor(pfx + "wbc", [128, C], BF16))
        s["top8"] = E(nc.sbuf_tensor(pfx + "top8", [128, 8], F32))
        s["sg"] = E(nc.sbuf_tensor(pfx + "sg", [128, C], BF16))
        s["hp"] = E(nc.sbuf_tensor(pfx + "hp", [128, C], BF16))
        s["hs"] = E(nc.sbuf_tensor(pfx + "hs", [128, 4 * C], BF16))
        s["rows"] = E(nc.sbuf_tensor(pfx + "rows", [128, 8], F32))
        s["pstat"] = E(nc.sbuf_tensor(pfx + "pstat", [128, 8], F32))
        s["sq_a"] = E(nc.sbuf_tensor(pfx + "sq_a", [128, 512], BF16))
        s["sq_b"] = E(nc.sbuf_tensor(pfx + "sq_b", [128, 512], BF16))
        s["epsv"] = E(nc.sbuf_tensor(pfx + "epsv", [128, 1], F32))
        self.ps_tr = E(nc.psum_tensor(pfx + "ps_tr", [128, 512], BF16))
        self.ps_g = E(nc.psum_tensor(pfx + "ps_g", [128, 512], F32))
        self.ps_u = E(nc.psum_tensor(pfx + "ps_u", [128, 512], F32))
        self.ps_r = E(nc.psum_tensor(pfx + "ps_r", [128, 512], F32))
        self.ps_o = [E(nc.psum_tensor(pfx + f"pso{i}", [128, 512], F32)) for i in range(4)]

    def emit(self, k, io, cst):
        s = self.sb
        C = self.CH
        ident = cst["ident_bf"]
        k.op("gpsimd", lambda g: g.dma_start(
            out=s["rw"][:].rearrange("p (kt e) -> p kt e", kt=8),
            in_=io["router_wT"].rearrange("(kt p) e -> p kt e", p=128)), inc="g_ld", inc_n=16)
        k.op("gpsimd", lambda g: g.dma_start(out=s["rb"][:], in_=io["router_bias"]),
             inc="g_ld", inc_n=16)
        k.op("sync", lambda sy: sy.dma_start(out=s["sel"][:], in_=io["sel32d"]),
             inc="m_ld", inc_n=16)
        for nm, src in (("wsg", "sgateT"), ("wsu", "supT")):
            k.op("gpsimd", lambda g, nm=nm, src=src: g.dma_start(
                out=s[nm][:].rearrange("p (kt e) -> p kt e", kt=8),
                in_=io[src].rearrange("(kt p) e -> p kt e", p=128)), inc="g_ld", inc_n=16)
        k.op("gpsimd", lambda g: g.dma_start(
            out=s["wsd"][:].rearrange("p (kt e) -> p kt e", kt=4),
            in_=io["sdownT"].rearrange("(kt p) e -> p kt e", p=128)), inc="g_ld", inc_n=16)
        for e in ("tensor", "vector", "scalar", "gpsimd", "sync"):
            k.wait(e, "g_ld", k.cnt["g_ld"])
            k.wait(e, "m_ld", k.cnt["m_ld"])
        k.op("vector", lambda v: v.memset(s["epsv"][:], 1e-5), inc="m_v")

        def chunk_body(ch):
            k.op("gpsimd", lambda g: g.dma_start(
                out=s["xstg"][:].rearrange("p (a d) -> p a d", a=2),
                in_=io["x_in"].rearrange("(t p) d -> p t d", p=128)[:, DynSlice(ch * 2, 2), :]),
                wait=[("m_t", k.cnt["m_t"]), ("m_s", k.cnt["m_s"])], inc="g_x", inc_n=16)
            gx = k.cnt["g_x"]
            for dt in range(8):
                w = [("g_x", gx)] if dt == 0 else []
                w.append(("m_v", k.cnt["m_v"]))
                for tt in range(2):
                    k.op("tensor", lambda t, dt=dt, tt=tt: t.transpose(
                        self.ps_tr[:, tt * 128:(tt + 1) * 128],
                        s["xstg"][:, tt * 1024 + dt * 128: tt * 1024 + (dt + 1) * 128],
                        ident[:]),
                        wait=(w if tt == 0 else []),
                        inc=("m_t" if tt == 1 else None))
                k.op("vector", lambda v, dt=dt: v.tensor_copy(
                    s["xTc"][:, dt * C:(dt + 1) * C], self.ps_tr[:, 0:C]),
                    wait=[("m_t", k.cnt["m_t"])], inc="m_v")
            for tt in range(2):
                for kt in range(8):
                    k.op("tensor", lambda t, tt=tt, kt=kt: t.matmul(
                        self.ps_r[:, 0:32],
                        s["xTc"][:, kt * C + tt * 128: kt * C + (tt + 1) * 128],
                        s["rw"][:, kt * 32:(kt + 1) * 32],
                        start=(kt == 0), stop=False),
                        wait=([("m_v", k.cnt["m_v"]), ("m_s", k.cnt["m_s"])] if kt == 0 else []),
                        inc=None)
                k.op("tensor", lambda t: t.matmul(
                    self.ps_r[:, 0:32], cst["ones1x128"][0:1, :], s["rb"][0:1, :],
                    start=False, stop=True), inc="m_t")
                k.op("vector", lambda v: v.tensor_reduce(
                    s["rows"][:, 0:1], self.ps_r[:, 0:32], axis=AX.X, op=ALU.max, negate=True),
                    wait=[("m_t", k.cnt["m_t"])], inc="m_v")
                k.op("scalar", lambda sc: sc.activation(
                    s["aff"][:], self.ps_r[:, 0:32], AF.Exp, bias=s["rows"][:, 0:1],
                    accum_out=s["rows"][:, 1:2]),
                    wait=[("m_v", k.cnt["m_v"])], inc="m_s")
                k.op("vector", lambda v: v.reciprocal(s["rows"][:, 2:3], s["rows"][:, 1:2]),
                     wait=[("m_s", k.cnt["m_s"])], inc="m_v")
                k.op("vector", lambda v: v.tensor_scalar(
                    s["aff"][:], s["aff"][:], s["rows"][:, 2:3], None, op0=ALU.mult),
                    wait=[("m_v", k.cnt["m_v"])], inc="m_v")
                k.op("vector", lambda v: v.max(out=s["top8"][:], in_=s["aff"][:]),
                     wait=[("m_v", k.cnt["m_v"])], inc="m_v")
                k.op("vector", lambda v: v.tensor_scalar(
                    s["wtok"][:], s["aff"][:], s["top8"][:, 3:4], None, op0=ALU.is_ge),
                    wait=[("m_v", k.cnt["m_v"])], inc="m_v")
                k.op("vector", lambda v: v.tensor_tensor(
                    s["wtok"][:], s["wtok"][:], s["aff"][:], op=ALU.mult),
                    wait=[("m_v", k.cnt["m_v"])], inc="m_v")
                k.op("vector", lambda v: v.tensor_reduce(
                    s["rows"][:, 3:4], s["wtok"][:], axis=AX.X, op=ALU.add),
                    wait=[("m_v", k.cnt["m_v"])], inc="m_v")
                k.op("vector", lambda v: v.tensor_scalar_max(
                    s["rows"][:, 3:4], s["rows"][:, 3:4], 1e-12),
                    wait=[("m_v", k.cnt["m_v"])], inc="m_v")
                k.op("vector", lambda v: v.reciprocal(s["rows"][:, 4:5], s["rows"][:, 3:4]),
                     wait=[("m_v", k.cnt["m_v"])], inc="m_v")
                k.op("vector", lambda v: v.tensor_scalar(
                    s["wtok"][:], s["wtok"][:], s["rows"][:, 4:5], None, op0=ALU.mult),
                    wait=[("m_v", k.cnt["m_v"])], inc="m_v")
                k.op("tensor", lambda t, tt=tt: t.transpose(
                    self.ps_tr[0:32, tt * 128:(tt + 1) * 128],
                    s["wtok"][:, 0:32], ident[:]),
                    wait=[("m_v", k.cnt["m_v"])], inc="m_t")
            k.op("vector", lambda v: v.tensor_copy(s["wT"][0:32, 0:C], self.ps_tr[0:32, 0:C]),
                 wait=[("m_t", k.cnt["m_t"])], inc="m_v")
            wt_v = k.cnt["m_v"]

            def pair_body(ep):
                for par in range(2):
                    eidx = ep * 2 + par
                    k.op("gpsimd", lambda g, par=par, eidx=eidx: g.dma_start(
                        out=s["wg"][:, par * 1024:(par + 1) * 1024].rearrange("p (o kt c) -> p o kt c", o=1, kt=8),
                        in_=io["gateT"].rearrange("eo (kt p) c -> p eo kt c", p=128)[:, DynSlice(eidx, 1), :, :]),
                        wait=[("m_t", k.cnt["m_t"])], inc="g_w", inc_n=16)
                    k.op("gpsimd", lambda g, par=par, eidx=eidx: g.dma_start(
                        out=s["wu"][:, par * 1024:(par + 1) * 1024].rearrange("p (o kt c) -> p o kt c", o=1, kt=8),
                        in_=io["upT"].rearrange("eo (kt p) c -> p eo kt c", p=128)[:, DynSlice(eidx, 1), :, :]),
                        inc="g_w", inc_n=16)
                    k.op("gpsimd", lambda g, par=par, eidx=eidx: g.dma_start(
                        out=s["wd"][:, par * 1024:(par + 1) * 1024].rearrange("p (o c) -> p o c", o=1),
                        in_=io["downT"].rearrange("eo p c -> p eo c")[:, DynSlice(eidx, 1), :]),
                        inc="g_w", inc_n=16)
                    gw = k.cnt["g_w"]
                    k.op("vector", lambda v, eidx=eidx: v.tensor_copy(
                        s["selbuf"][:], s["sel"][0:32, DynSlice(eidx * 128, 128)]),
                        wait=[("m_t", k.cnt["m_t"])], inc="m_v")
                    k.op("tensor", lambda t: t.matmul(
                        self.ps_r[:, 0:C], s["selbuf"][0:32, 0:128],
                        s["wT"][0:32, 0:C], start=True, stop=True),
                        wait=[("m_v", k.cnt["m_v"])], inc="m_t")
                    k.op("vector", lambda v: v.tensor_copy(s["wbc"][:], self.ps_r[:, 0:C]),
                         wait=[("m_t", k.cnt["m_t"])], inc="m_v")
                    for which, bank in (("wg", self.ps_g), ("wu", self.ps_u)):
                        for kt in range(8):
                            k.op("tensor", lambda t, which=which, par=par, kt=kt, bank=bank: t.matmul(
                                bank[:, 0:C],
                                s[which][:, par * 1024 + kt * 128: par * 1024 + (kt + 1) * 128],
                                s["xTc"][:, kt * C:(kt + 1) * C],
                                start=(kt == 0), stop=(kt == 7)),
                                wait=([("g_w", gw), ("m_v", k.cnt["m_v"]),
                                       ("m_s", k.cnt["m_s"])] if kt == 0 else []),
                                inc=("m_t" if kt == 7 else None))
                    k.op("scalar", lambda sc: sc.activation(
                        s["sg"][:], self.ps_g[:, 0:C], AF.Silu),
                        wait=[("m_t", k.cnt["m_t"])], inc="m_s")
                    k.op("vector", lambda v: v.tensor_tensor(
                        s["hp"][:], s["sg"][:], self.ps_u[:, 0:C], op=ALU.mult),
                        wait=[("m_s", k.cnt["m_s"]), ("m_v", k.cnt["m_v"])], inc="m_v")
                    k.op("vector", lambda v: v.tensor_tensor(
                        s["hp"][:], s["hp"][:], s["wbc"][:], op=ALU.mult),
                        wait=[("m_v", k.cnt["m_v"])], inc="m_v")
                    hp_v = k.cnt["m_v"]
                    for tt in range(2):
                        for half in range(2):
                            k.op("tensor", lambda t, tt=tt, half=half, par=par: t.matmul(
                                self.ps_o[tt * 2 + half][:, 0:512],
                                s["hp"][:, tt * 128:(tt + 1) * 128],
                                s["wd"][:, par * 1024 + half * 512: par * 1024 + (half + 1) * 512],
                                start=False, stop=False),
                                wait=([("m_v", hp_v)] if (tt == 0 and half == 0) else []),
                                inc=("m_t" if (tt == 1 and half == 1) else None))

            z_v = k.cnt["m_v"]
            for i in range(4):
                k.op("tensor", lambda t, i=i: t.matmul(
                    self.ps_o[i][:, 0:512], cst["ones1x128"][0:1, :],
                    cst["zeros1x512"][0:1, :], start=True, stop=False),
                    wait=([("m_v", z_v), ("m_s", k.cnt["m_s"])] if i == 0 else []), inc=None)
            k.autoloop(16, pair_body, engines=("tensor", "vector", "scalar", "gpsimd"))
            for mt in range(4):
                for which, bank in (("wsg", self.ps_g), ("wsu", self.ps_u)):
                    for kt in range(8):
                        k.op("tensor", lambda t, mt=mt, which=which, kt=kt, bank=bank: t.matmul(
                            bank[:, 0:C],
                            s[which][:, kt * 512 + mt * 128: kt * 512 + (mt + 1) * 128],
                            s["xTc"][:, kt * C:(kt + 1) * C],
                            start=(kt == 0), stop=(kt == 7)),
                            wait=([("m_v", k.cnt["m_v"]), ("m_s", k.cnt["m_s"])] if kt == 0 else []),
                            inc=("m_t" if kt == 7 else None))
                k.op("scalar", lambda sc, mt=mt: sc.activation(
                    s["sg"][:], self.ps_g[:, 0:C], AF.Silu),
                    wait=[("m_t", k.cnt["m_t"])], inc="m_s")
                k.op("vector", lambda v, mt=mt: v.tensor_tensor(
                    s["hs"][:, mt * C:(mt + 1) * C], s["sg"][:], self.ps_u[:, 0:C], op=ALU.mult),
                    wait=[("m_s", k.cnt["m_s"]), ("m_v", k.cnt["m_v"])], inc="m_v")
            hs_v = k.cnt["m_v"]
            for tt in range(2):
                for half in range(2):
                    for kt in range(4):
                        k.op("tensor", lambda t, tt=tt, half=half, kt=kt: t.matmul(
                            self.ps_o[tt * 2 + half][:, 0:512],
                            s["hs"][:, kt * C + tt * 128: kt * C + (tt + 1) * 128],
                            s["wsd"][:, kt * 1024 + half * 512: kt * 1024 + (half + 1) * 512],
                            start=False, stop=False),
                            wait=([("m_v", hs_v)] if (tt == 0 and half == 0 and kt == 0) else []),
                            inc=None)
                    k.op("tensor", lambda t, tt=tt, half=half: t.matmul(
                        self.ps_o[tt * 2 + half][:, 0:512], ident[:],
                        s["xstg"][:, tt * 1024 + half * 512: tt * 1024 + (half + 1) * 512],
                        start=False, stop=True),
                        inc="m_t")
            for tt in range(2):
                emit_postln(k, s, [self.ps_o[tt * 2][:, 0:512], self.ps_o[tt * 2 + 1][:, 0:512]],
                            [s["ostg"][:, tt * 1024: tt * 1024 + 512],
                             s["ostg"][:, tt * 1024 + 512: tt * 1024 + 1024]],
                            "m_t", k.cnt["m_t"])
            k.op("gpsimd", lambda g: g.dma_start(
                out=io["x_out"].rearrange("(t p) d -> p t d", p=128)[:, DynSlice(ch * 2, 2), :],
                in_=s["ostg"][:].rearrange("p (a d) -> p a d", a=2)),
                wait=[("m_s", k.cnt["m_s"])], inc="g_st", inc_n=16)

        k.autoloop(16, chunk_body)
        k.wait("sync", "g_st", k.cnt["g_st"])


def l1ffn_host_prep(inp):
    f = np.ascontiguousarray
    return {
        "router_wT": f(inp["L1_ffn_router_w"].T),
        "router_bias": f(inp["L1_ffn_router_bias"].reshape(1, 32)),
        "gateT": f(np.transpose(inp["L1_ffn_gate_w"], (0, 2, 1))),
        "upT": f(np.transpose(inp["L1_ffn_up_w"], (0, 2, 1))),
        "downT": f(np.transpose(inp["L1_ffn_down_w"], (0, 2, 1))),
        "sgateT": f(inp["L1_ffn_sgate_w"].T),
        "supT": f(inp["L1_ffn_sup_w"].T),
        "sdownT": f(inp["L1_ffn_sdown_w"].T),
        "sel32d": make_consts()["sel32"],
    }


L1FFN_W = [("router_wT", [1024, 32], F32), ("router_bias", [1, 32], F32),
           ("gateT", [32, 1024, 128], F32), ("upT", [32, 1024, 128], F32),
           ("downT", [32, 128, 1024], F32), ("sgateT", [1024, 512], F32),
           ("supT", [1024, 512], F32), ("sdownT", [512, 1024], F32),
           ("sel32d", [32, 4096], BF16)]


def build_l1ffn_program():
    from contextlib import ExitStack
    nc = bass.Bass("TRN2", target_bir_lowering=False, detect_race_conditions=False)
    x_in = nc.dram_tensor("x", [T, 1024], BF16, kind="ExternalInput").ap()
    x_out = nc.dram_tensor("y", [T, 1024], BF16, kind="ExternalOutput").ap()
    io = {"x_in": x_in, "x_out": x_out}
    for nm, shp, dt in L1FFN_W:
        io[nm] = nc.dram_tensor("w_" + nm, shp, dt, kind="ExternalInput").ap()
    ctx = ExitStack()
    with ctx:
        E = ctx.enter_context
        cst_np, cst_aps, cst_sb = const_tensors(nc, E)
        blk_obj = L1FFN(nc, E, "f1_")
        sems = {}
        for nm in ("m_ld", "g_ld", "m_t", "m_v", "m_s", "g_x", "g_w", "g_st", "st"):
            sems[nm] = E(nc.semaphore(nm))
        blk = E(nc.Block())
        k = K(nc, sems, {})

        def emit_all(k):
            for nm in cst_sb:
                k.op("sync", lambda sy, nm=nm: sy.dma_start(out=cst_sb[nm][:], in_=cst_aps[nm]),
                     inc="m_ld", inc_n=16)
            cst = {nm: cst_sb[nm][:] for nm in cst_sb}
            blk_obj.emit(k, io, cst)

        def section(name):
            def fny(eng):
                k.run(name, eng, [emit_all])
            return fny
        blk.sync(section("sync"))
        blk.gpsimd(section("gpsimd"))
        blk.tensor(section("tensor"))
        blk.vector(section("vector"))
        blk.scalar(section("scalar"))
    return nc, cst_np


def run_l1ffn_test(x_bf16, io_np, sim=False):
    nc, cst_np = build_l1ffn_program()
    in_map = {"x": x_bf16}
    for nm, arr in io_np.items():
        in_map["w_" + nm] = arr
    for nm, arr in cst_np.items():
        in_map["c_" + nm] = arr
    if sim:
        from concourse.bass_interp import MultiCoreSim
        simu = MultiCoreSim(nc, num_cores=1, num_workers=1, trace=False)
        for kk, v in in_map.items():
            simu.cores[0].tensor(kk)[:] = v
        simu.simulate(check_with_hw=False)
        return np.asarray(simu.cores[0].tensor("y"))
    res = run_bass_kernel_spmd(nc, [in_map], core_ids=[0])
    return np.asarray(res.results[0]["y"])


class L23FFN:

    def __init__(self, nc, E, pfx, mode, final_f32=False):
        self.mode = mode
        self.final_f32 = final_f32
        self.I = 2048 if mode == "L2" else 4096
        s = self.sb = {}
        self.scr = nc.dram_tensor(pfx + "acc_scr", [T, 1024], F32, kind="Internal").ap()
        s["gw"] = E(nc.sbuf_tensor(pfx + "gw", [128, 8 * 8], BF16))
        s["w1"] = E(nc.sbuf_tensor(pfx + "w1", [128, 8 * (self.I)], BF16))
        if mode == "L2":
            s["wu"] = E(nc.sbuf_tensor(pfx + "wu", [128, 8 * self.I], BF16))
        s["w2"] = E(nc.sbuf_tensor(pfx + "w2", [128, (self.I // 128) * 1024], BF16))
        s["xstg"] = E(nc.sbuf_tensor(pfx + "xstg", [128, 2 * 1024], BF16))
        s["xTc"] = E(nc.sbuf_tensor(pfx + "xTc", [128, 8 * 256], BF16))
        s["h"] = E(nc.sbuf_tensor(pfx + "h", [128, (self.I // 128) * 256], BF16))
        s["stage"] = E(nc.sbuf_tensor(pfx + "stage", [128, 2 * 1024], F32))
        s["ostg"] = E(nc.sbuf_tensor(pfx + "ostg", [128, 2 * 1024],
                                     F32 if final_f32 else BF16))
        s["affs"] = E(nc.sbuf_tensor(pfx + "affs", [128, 8], F32))
        s["wgt_tm"] = E(nc.sbuf_tensor(pfx + "wgt_tm", [128, 32 * 8], F32))
        s["wstage"] = E(nc.sbuf_tensor(pfx + "wstage", [128, 2], F32))
        s["affT"] = E(nc.sbuf_tensor(pfx + "affT", [8, T], F32))
        s["cmp"] = E(nc.sbuf_tensor(pfx + "cmp", [8, T], BF16))
        s["thr"] = E(nc.sbuf_tensor(pfx + "thr", [8, 8], F32))
        s["rows"] = E(nc.sbuf_tensor(pfx + "rows", [128, 8], F32))
        s["pstat"] = E(nc.sbuf_tensor(pfx + "pstat", [128, 8], F32))
        s["sq_a"] = E(nc.sbuf_tensor(pfx + "sq_a", [128, 512], BF16))
        s["sq_b"] = E(nc.sbuf_tensor(pfx + "sq_b", [128, 512], BF16))
        s["epsv"] = E(nc.sbuf_tensor(pfx + "epsv", [128, 1], F32))
        self.ps_tr = E(nc.psum_tensor(pfx + "ps_tr", [128, 512], BF16))
        self.ps_trf = E(nc.psum_tensor(pfx + "ps_trf", [128, 128], F32))
        self.ps_g = E(nc.psum_tensor(pfx + "ps_g", [128, 512], F32))
        self.ps_u = E(nc.psum_tensor(pfx + "ps_u", [128, 512], F32))
        self.ps_o = [E(nc.psum_tensor(pfx + f"pso{i}", [128, 512], F32)) for i in range(2)]
        self.ps_r = E(nc.psum_tensor(pfx + "ps_r", [128, 512], F32))

    def _xTc(self, k, io, cst, ch, gate_t=("m_t",)):
        s = self.sb
        ident = cst["ident_bf"]
        k.op("gpsimd", lambda g: g.dma_start(
            out=s["xstg"][:].rearrange("p (a d) -> p a d", a=2),
            in_=io["x_in"].rearrange("(t p) d -> p t d", p=128)[:, DynSlice(ch * 2, 2), :]),
            wait=[("m_t", k.cnt["m_t"]), ("m_s", k.cnt["m_s"])], inc="g_x", inc_n=16)
        gx = k.cnt["g_x"]
        for dt in range(8):
            w = [("g_x", gx)] if dt == 0 else []
            w.append(("m_v", k.cnt["m_v"]))
            for tt4 in range(2):
                k.op("tensor", lambda t, dt=dt, tt4=tt4: t.transpose(
                    self.ps_tr[:, tt4 * 128:(tt4 + 1) * 128],
                    s["xstg"][:, tt4 * 1024 + dt * 128: tt4 * 1024 + (dt + 1) * 128],
                    ident[:]),
                    wait=(w if tt4 == 0 else []),
                    inc=("m_t" if tt4 == 1 else None))
            k.op("vector", lambda v, dt=dt: v.tensor_copy(
                s["xTc"][:, dt * 256:(dt + 1) * 256], self.ps_tr[:, 0:256]),
                wait=[("m_t", k.cnt["m_t"])], inc="m_v")
        return gx

    def emit(self, k, io, cst):
        s = self.sb
        mode = self.mode
        I = self.I
        NI = I // 128
        ident = cst["ident_bf"]
        k.op("gpsimd", lambda g: g.dma_start(
            out=s["gw"][:].rearrange("p (kt e) -> p kt e", kt=8),
            in_=io["gate_w"].rearrange("(kt p) e -> p kt e", p=128)), inc="g_ld", inc_n=16)
        for e in ("tensor", "vector", "scalar", "gpsimd", "sync"):
            k.wait(e, "g_ld", k.cnt["g_ld"])
        k.op("vector", lambda v: v.memset(s["epsv"][:], 1e-5), inc="m_v")

        def rbody(ch):
            self._xTc(k, io, cst, ch)
            for tt4 in range(2):
                for kt in range(8):
                    k.op("tensor", lambda t, tt4=tt4, kt=kt: t.matmul(
                        self.ps_r[:, 0:8],
                        s["xTc"][:, kt * 256 + tt4 * 128: kt * 256 + (tt4 + 1) * 128],
                        s["gw"][:, kt * 8:(kt + 1) * 8],
                        start=(kt == 0), stop=(kt == 7)),
                        wait=([("m_v", k.cnt["m_v"])] if kt == 0 else []),
                        inc=("m_t" if kt == 7 else None))
                k.op("vector", lambda v: v.tensor_reduce(
                    s["rows"][:, 0:1], self.ps_r[:, 0:8], axis=AX.X, op=ALU.max, negate=True),
                    wait=[("m_t", k.cnt["m_t"])], inc="m_v")
                k.op("scalar", lambda sc, tt4=tt4: sc.activation(
                    s["affs"][:], self.ps_r[:, 0:8],
                    AF.Exp, bias=s["rows"][:, 0:1], accum_out=s["rows"][:, 1:2]),
                    wait=[("m_v", k.cnt["m_v"])], inc="m_s")
                k.op("vector", lambda v: v.reciprocal(s["rows"][:, 2:3], s["rows"][:, 1:2]),
                     wait=[("m_s", k.cnt["m_s"])], inc="m_v")
                k.op("vector", lambda v, tt4=tt4: v.tensor_scalar(
                    s["affs"][:], s["affs"][:],
                    s["rows"][:, 2:3], None, op0=ALU.mult),
                    wait=[("m_v", k.cnt["m_v"])], inc="m_v")
                k.op("tensor", lambda t, tt4=tt4: t.transpose(
                    self.ps_trf[0:8, 0:128],
                    s["affs"][:], cst["ident_f32"][:]),
                    wait=[("m_v", k.cnt["m_v"])], inc="m_t")
                k.op("vector", lambda v, tt4=tt4: v.tensor_copy(
                    s["affT"][0:8, DynSlice((ch * 2 + tt4) * 128, 128)],
                    self.ps_trf[0:8, 0:128]),
                    wait=[("m_t", k.cnt["m_t"])], inc="m_v")
        k.autoloop(16, rbody)

        th = s["thr"]
        k.op("vector", lambda v: v.memset(th[0:8, 0:1], 0.0),
             wait=[("m_v", k.cnt["m_v"])], inc="m_v")
        k.op("vector", lambda v: v.memset(th[0:8, 1:2], 1.0),
             wait=[("m_v", k.cnt["m_v"])], inc="m_v")
        for it in range(25):
            k.op("vector", lambda v: v.tensor_tensor(th[0:8, 2:3], th[0:8, 0:1], th[0:8, 1:2], op=ALU.add),
                 wait=[("m_v", k.cnt["m_v"])], inc="m_v")
            k.op("vector", lambda v: v.tensor_scalar_mul(th[0:8, 2:3], th[0:8, 2:3], 0.5),
                 wait=[("m_v", k.cnt["m_v"])], inc="m_v")
            k.op("vector", lambda v: v.tensor_scalar(
                s["cmp"][0:8, :], s["affT"][0:8, :], th[0:8, 2:3], None, op0=ALU.is_ge),
                wait=[("m_v", k.cnt["m_v"])], inc="m_v")
            k.op("vector", lambda v: v.tensor_reduce(
                th[0:8, 3:4], s["cmp"][0:8, :], axis=AX.X, op=ALU.add),
                wait=[("m_v", k.cnt["m_v"])], inc="m_v")
            k.op("vector", lambda v: v.tensor_scalar(
                th[0:8, 4:5], th[0:8, 3:4], 1024.0, None, op0=ALU.is_ge),
                wait=[("m_v", k.cnt["m_v"])], inc="m_v")
            k.op("vector", lambda v: v.tensor_tensor(
                th[0:8, 6:7], th[0:8, 2:3], th[0:8, 0:1], op=ALU.subtract),
                wait=[("m_v", k.cnt["m_v"])], inc="m_v")
            k.op("vector", lambda v: v.scalar_tensor_tensor(
                th[0:8, 0:1], th[0:8, 6:7], th[0:8, 4:5], th[0:8, 0:1],
                op0=ALU.mult, op1=ALU.add),
                wait=[("m_v", k.cnt["m_v"])], inc="m_v")
            k.op("vector", lambda v: v.tensor_tensor(
                th[0:8, 6:7], th[0:8, 1:2], th[0:8, 2:3], op=ALU.subtract),
                wait=[("m_v", k.cnt["m_v"])], inc="m_v")
            k.op("vector", lambda v: v.scalar_tensor_tensor(
                th[0:8, 1:2], th[0:8, 6:7], th[0:8, 4:5], th[0:8, 2:3],
                op0=ALU.mult, op1=ALU.add),
                wait=[("m_v", k.cnt["m_v"])], inc="m_v")
        k.op("vector", lambda v: v.tensor_copy(th[0:8, 5:6], th[0:8, 0:1]),
             wait=[("m_v", k.cnt["m_v"])], inc="m_v")
        k.op("vector", lambda v: v.tensor_scalar(
            s["cmp"][0:8, :], s["affT"][0:8, :], th[0:8, 5:6], None, op0=ALU.is_ge),
            wait=[("m_v", k.cnt["m_v"])], inc="m_v")
        k.op("vector", lambda v: v.tensor_tensor(
            s["affT"][0:8, :], s["affT"][0:8, :], s["cmp"][0:8, :], op=ALU.mult),
            wait=[("m_v", k.cnt["m_v"])], inc="m_v")
        wg_v = k.cnt["m_v"]
        for tt in range(32):
            k.op("tensor", lambda t, tt=tt: t.transpose(
                self.ps_trf[:, 0:8],
                s["affT"][0:8, tt * 128:(tt + 1) * 128], cst["ident_f32"][0:8, 0:8]),
                wait=[("m_v", k.cnt["m_v"])], inc="m_t")
            k.op("vector", lambda v, tt=tt: v.tensor_copy(
                s["wgt_tm"][:, tt * 8:(tt + 1) * 8], self.ps_trf[:, 0:8]),
                wait=[("m_t", k.cnt["m_t"])], inc="m_v")

        def expert_block(e, first):
            k.op("gpsimd", lambda g, e=e: g.dma_start(
                out=s["w1"][:].rearrange("p (o kt c) -> p o kt c", o=1, kt=8),
                in_=io["w1"].rearrange("eo (kt p) c -> p eo kt c", p=128)[:, DynSlice(e, 1), :, :]),
                wait=[("m_t", k.cnt["m_t"])], inc="g_w", inc_n=16)
            if mode == "L2":
                k.op("gpsimd", lambda g, e=e: g.dma_start(
                    out=s["wu"][:].rearrange("p (o kt c) -> p o kt c", o=1, kt=8),
                    in_=io["wu"].rearrange("eo (kt p) c -> p eo kt c", p=128)[:, DynSlice(e, 1), :, :]),
                    inc="g_w", inc_n=16)
            k.op("gpsimd", lambda g, e=e: g.dma_start(
                out=s["w2"][:].rearrange("p (o kt c) -> p o kt c", o=1, kt=NI),
                in_=io["w2"].rearrange("eo (kt p) c -> p eo kt c", p=128)[:, DynSlice(e, 1), :, :]),
                inc="g_w", inc_n=16)
            gw = k.cnt["g_w"]

            def ebody(ch):
                self._xTc(k, io, cst, ch)
                xt_v = k.cnt["m_v"]
                k.op("vector", lambda v: v.tensor_copy(
                    s["wstage"][:],
                    s["wgt_tm"][:].rearrange("p (t e) -> p t e", e=8)[:, DynSlice(ch * 2, 2), DynSlice(e, 1)].rearrange("p t e -> p (t e)")),
                    wait=[("m_v", k.cnt["m_v"])], inc="m_v")
                for mt in range(NI):
                    for kt in range(8):
                        k.op("tensor", lambda t, mt=mt, kt=kt: t.matmul(
                            self.ps_g[:, 0:256],
                            s["w1"][:, kt * I + mt * 128: kt * I + (mt + 1) * 128],
                            s["xTc"][:, kt * 256:(kt + 1) * 256],
                            start=(kt == 0), stop=(kt == 7)),
                            wait=([("g_w", gw), ("m_v", k.cnt["m_v"]),
                                   ("m_s", k.cnt["m_s"])] if kt == 0 else []),
                            inc=("m_t" if kt == 7 else None))
                    if mode == "L2":
                        for kt in range(8):
                            k.op("tensor", lambda t, mt=mt, kt=kt: t.matmul(
                                self.ps_u[:, 0:256],
                                s["wu"][:, kt * I + mt * 128: kt * I + (mt + 1) * 128],
                                s["xTc"][:, kt * 256:(kt + 1) * 256],
                                start=(kt == 0), stop=(kt == 7)),
                                inc=("m_t" if kt == 7 else None))
                        k.op("scalar", lambda sc, mt=mt: sc.activation(
                            s["sq_a"][:, 0:256], self.ps_g[:, 0:256], AF.Silu),
                            wait=[("m_t", k.cnt["m_t"])], inc="m_s")
                        k.op("vector", lambda v, mt=mt: v.tensor_tensor(
                            s["h"][:, mt * 256:(mt + 1) * 256], s["sq_a"][:, 0:256],
                            self.ps_u[:, 0:256], op=ALU.mult),
                            wait=[("m_s", k.cnt["m_s"]), ("m_v", k.cnt["m_v"])], inc="m_v")
                    else:
                        k.op("scalar", lambda sc, mt=mt: sc.activation(
                            s["h"][:, mt * 256:(mt + 1) * 256], self.ps_g[:, 0:256], AF.Gelu),
                            wait=[("m_t", k.cnt["m_t"])], inc="m_s")
                for tt in range(2):
                    for half in range(2):
                        bank = self.ps_o[(tt * 2 + half) % 2]
                        for kt in range(NI):
                            k.op("tensor", lambda t, tt=tt, half=half, kt=kt, bank=bank: t.matmul(
                                bank[:, 0:512],
                                s["h"][:, kt * 256 + tt * 128: kt * 256 + (tt + 1) * 128],
                                s["w2"][:, kt * 1024 + half * 512: kt * 1024 + (half + 1) * 512],
                                start=(kt == 0), stop=(kt == NI - 1)),
                                wait=([("m_v", k.cnt["m_v"]), ("m_s", k.cnt["m_s"])] if kt == 0 else []),
                                inc=("m_t" if kt == NI - 1 else None))
                        k.op("vector", lambda v, tt=tt, half=half, bank=bank: v.tensor_scalar(
                            s["stage"][:, tt * 1024 + half * 512: tt * 1024 + (half + 1) * 512],
                            bank[:, 0:512],
                            s["wstage"][:, tt:tt + 1], None,
                            op0=ALU.mult),
                            wait=[("m_t", k.cnt["m_t"])], inc="m_v")
                k.op("gpsimd", lambda g: g.dma_start(
                    out=self.scr.rearrange("(t p) d -> p t d", p=128)[:, DynSlice(ch * 2, 2), :],
                    in_=s["stage"][:].rearrange("p (a d) -> p a d", a=2),
                    accum_op=(ALU.bypass if first else ALU.add)),
                    wait=[("m_v", k.cnt["m_v"])], inc="g_st", inc_n=16)
            k.autoloop(16, ebody)
            for en in ("tensor", "vector", "scalar", "gpsimd", "sync"):
                k.wait(en, "g_st", k.cnt["g_st"])

        expert_block(0, True)
        k.autoloop(7, lambda e: expert_block(e + 1, False),
                   engines=("tensor", "vector", "scalar", "gpsimd", "sync"))

        def fbody(ch):
            k.op("gpsimd", lambda g: g.dma_start(
                out=s["xstg"][:].rearrange("p (a d) -> p a d", a=2),
                in_=io["x_in"].rearrange("(t p) d -> p t d", p=128)[:, DynSlice(ch * 2, 2), :]),
                wait=[("m_s", k.cnt["m_s"])], inc="g_x", inc_n=16)
            k.op("sync", lambda sy: sy.dma_start(
                out=s["stage"][:].rearrange("p (a d) -> p a d", a=2),
                in_=self.scr.rearrange("(t p) d -> p t d", p=128)[:, DynSlice(ch * 2, 2), :]),
                wait=[("m_s", k.cnt["m_s"])], inc="m_qk", inc_n=16)
            gx = k.cnt["g_x"]
            lq = k.cnt["m_qk"]
            for tt in range(2):
                for half in range(2):
                    k.op("vector", lambda v, tt=tt, half=half: v.tensor_tensor(
                        s["stage"][:, tt * 1024 + half * 512: tt * 1024 + (half + 1) * 512],
                        s["stage"][:, tt * 1024 + half * 512: tt * 1024 + (half + 1) * 512],
                        s["xstg"][:, tt * 1024 + half * 512: tt * 1024 + (half + 1) * 512],
                        op=ALU.add),
                        wait=([("g_x", gx), ("m_qk", lq)] if (tt == 0 and half == 0) else []),
                        inc="m_v")
                emit_postln(k, s, [s["stage"][:, tt * 1024: tt * 1024 + 512],
                                   s["stage"][:, tt * 1024 + 512: tt * 1024 + 1024]],
                            [s["ostg"][:, tt * 1024: tt * 1024 + 512],
                             s["ostg"][:, tt * 1024 + 512: tt * 1024 + 1024]],
                            "m_v", k.cnt["m_v"])
            k.op("gpsimd", lambda g: g.dma_start(
                out=io["x_out"].rearrange("(t p) d -> p t d", p=128)[:, DynSlice(ch * 2, 2), :],
                in_=s["ostg"][:].rearrange("p (a d) -> p a d", a=2)),
                wait=[("m_s", k.cnt["m_s"])], inc="g_st", inc_n=16)
        k.autoloop(16, fbody)
        k.wait("sync", "g_st", k.cnt["g_st"])


def l23ffn_host_prep(inp, mode):
    f = np.ascontiguousarray
    if mode == "L2":
        return {"gate_w": f(inp["L2_ffn_gate_w"]),
                "w1": f(inp["L2_ffn_Wg"]), "wu": f(inp["L2_ffn_Wu"]),
                "w2": f(inp["L2_ffn_Wd"])}
    return {"gate_w": f(inp["L3_ffn_gate_w"]),
            "w1": f(inp["L3_ffn_w1"]), "w2": f(inp["L3_ffn_w2"])}


L23FFN_W = {
    "L2": [("gate_w", [1024, 8], F32), ("w1", [8, 1024, 2048], F32),
           ("wu", [8, 1024, 2048], F32), ("w2", [8, 2048, 1024], F32)],
    "L3": [("gate_w", [1024, 8], F32), ("w1", [8, 1024, 4096], F32),
           ("w2", [8, 4096, 1024], F32)],
}


def build_l23ffn_program(mode, final_f32=False):
    from contextlib import ExitStack
    nc = bass.Bass("TRN2", target_bir_lowering=False, detect_race_conditions=False)
    x_in = nc.dram_tensor("x", [T, 1024], BF16, kind="ExternalInput").ap()
    x_out = nc.dram_tensor("y", [T, 1024], F32 if final_f32 else BF16,
                           kind="ExternalOutput").ap()
    io = {"x_in": x_in, "x_out": x_out}
    for nm, shp, dt in L23FFN_W[mode]:
        io[nm] = nc.dram_tensor("w_" + nm, shp, dt, kind="ExternalInput").ap()
    ctx = ExitStack()
    with ctx:
        E = ctx.enter_context
        cst_np, cst_aps, cst_sb = const_tensors(nc, E)
        blk_obj = L23FFN(nc, E, "f_", mode, final_f32=final_f32)
        sems = {}
        for nm in ("m_ld", "g_ld", "m_t", "m_v", "m_s", "m_qk", "g_x", "g_w", "g_st", "st"):
            sems[nm] = E(nc.semaphore(nm))
        blk = E(nc.Block())
        k = K(nc, sems, {})

        def emit_all(k):
            for nm in cst_sb:
                k.op("sync", lambda sy, nm=nm: sy.dma_start(out=cst_sb[nm][:], in_=cst_aps[nm]),
                     inc="m_ld", inc_n=16)
            for en in ("tensor", "vector", "scalar", "gpsimd", "sync"):
                k.wait(en, "m_ld", k.cnt["m_ld"])
            cst = {nm: cst_sb[nm][:] for nm in cst_sb}
            blk_obj.emit(k, io, cst)

        def section(name):
            def fny(eng):
                k.run(name, eng, [emit_all])
            return fny
        blk.sync(section("sync"))
        blk.gpsimd(section("gpsimd"))
        blk.tensor(section("tensor"))
        blk.vector(section("vector"))
        blk.scalar(section("scalar"))
    return nc, cst_np


def run_l23ffn_test(x_bf16, io_np, mode, sim=False):
    nc, cst_np = build_l23ffn_program(mode)
    in_map = {"x": x_bf16}
    for nm, arr in io_np.items():
        in_map["w_" + nm] = arr
    for nm, arr in cst_np.items():
        in_map["c_" + nm] = arr
    if sim:
        from concourse.bass_interp import MultiCoreSim
        simu = MultiCoreSim(nc, num_cores=1, num_workers=1, trace=False)
        for kk, v in in_map.items():
            simu.cores[0].tensor(kk)[:] = v
        simu.simulate(check_with_hw=False)
        return np.asarray(simu.cores[0].tensor("y"))
    res = run_bass_kernel_spmd(nc, [in_map], core_ids=[0])
    return np.asarray(res.results[0]["y"])


L0H = 2730
L0NP = 22


def l0_tables_np():
    DH, W = 64, 512
    f64 = np.float64
    half = np.arange(0, DH, 2, dtype=f64)
    inv_freq = 1.0 / 10000.0 ** (half / DH)
    t = np.arange(T, dtype=f64)
    freqs = t[:, None] * inv_freq[None, :]
    freqs = np.concatenate([freqs, freqs], axis=-1)
    base = (half + 0.4 * DH) / (1.4 * DH)
    power = (t - T // 2) / float(W // 2)
    scale = base[None, :] ** power[:, None]
    scale = np.concatenate([scale, scale], axis=-1)
    cos, sin = np.cos(freqs), np.sin(freqs)
    sgn = np.ones(DH); sgn[:DH // 2] = -1.0
    Aq = (8.0 * cos * scale).T
    Bq = (8.0 * sin * scale * sgn[None, :]).T
    Ak = (cos / scale).T
    Bk = (sin / scale * sgn[None, :]).T
    tabs = {nm: np.ascontiguousarray(a).astype(BF16NP)
            for nm, a in (("rAq", Aq), ("rBq", Bq), ("rAk", Ak), ("rBk", Bk))}
    m = np.zeros((128, 4 * 1024), np.float32)
    for qt in range(4):
        r = np.arange(128)[:, None] + qt * 128
        c = np.arange(512)[None, :]
        m[:, qt * 1024: qt * 1024 + 512] = np.where(c >= r, 0.0, -1e9)
        m[:, qt * 1024 + 512: (qt + 1) * 1024] = np.where(c <= r, 0.0, -1e9)
    tabs["l0masks"] = m
    return tabs


def l0_host_prep(inp):
    f = np.ascontiguousarray
    io = dict(l0_tables_np())
    io["wqkv"] = f(inp["L0_ffn_wqkv"])
    io["qs"] = f(inp["L0_ffn_q_scale"].reshape(2, 64, 1))
    io["ks"] = f(inp["L0_ffn_k_scale"].reshape(2, 64, 1))
    io["wo"] = f(inp["L0_ffn_wo"])
    io["lng"] = f(inp["L0_ffn_ln_g"])
    io["lnb"] = f(inp["L0_ffn_ln_b"])
    w1 = inp["L0_ffn_w1"]
    b1 = inp["L0_ffn_b1"]
    w1p = np.zeros((2, 1024, L0NP * 2 * 128), np.float32)
    b1p = np.zeros((2, L0NP * 2, 128), np.float32)
    for p in range(L0NP):
        wa = min(128, L0H - p * 128)
        w1p[:, :, (2 * p) * 128:(2 * p) * 128 + wa] = w1[:, :, p * 128:p * 128 + wa]
        w1p[:, :, (2 * p + 1) * 128:(2 * p + 1) * 128 + wa] = w1[:, :, L0H + p * 128:L0H + p * 128 + wa]
        b1p[:, 2 * p, :wa] = b1[:, p * 128:p * 128 + wa]
        b1p[:, 2 * p + 1, :wa] = b1[:, L0H + p * 128:L0H + p * 128 + wa]
    io["w1p"] = w1p
    io["b1p"] = b1p
    w2p = np.zeros((2, L0NP * 128, 1024), np.float32)
    w2p[:, :L0H, :] = inp["L0_ffn_w2"]
    io["w2p"] = w2p
    io["b2"] = f(inp["L0_ffn_b2"])
    return io


L0_W = [("wqkv", [2, 1024, 1536], F32), ("qs", [2, 64, 1], F32),
        ("ks", [2, 64, 1], F32), ("wo", [2, 512, 1024], F32),
        ("lng", [2, 1024], F32), ("lnb", [2, 1024], F32),
        ("w1p", [2, 1024, L0NP * 256], F32), ("b1p", [2, L0NP * 2, 128], F32),
        ("w2p", [2, L0NP * 128, 1024], F32), ("b2", [2, 1024], F32),
        ("rAq", [64, T], BF16), ("rBq", [64, T], BF16),
        ("rAk", [64, T], BF16), ("rBk", [64, T], BF16),
        ("l0masks", [128, 4 * 1024], F32)]


class L0Attn:

    def __init__(self, nc, E, pfx, layer):
        self.layer = layer
        self.pfx = pfx
        s = self.sb = {}
        self.qscr = nc.dram_tensor(pfx + "qkv_scr", [1536, T], BF16, kind="Internal").ap()
        self.oscr = nc.dram_tensor(pfx + "o_scr", [512, T], BF16, kind="Internal").ap()
        s["W"] = E(nc.sbuf_tensor(pfx + "W", [128, 8 * 1536], BF16))
        s["Wo"] = E(nc.sbuf_tensor(pfx + "Wo", [128, 4 * 1024], BF16))
        s["hstg"] = E(nc.sbuf_tensor(pfx + "hstg", [128, 4 * 1024], BF16))
        s["hTc"] = E(nc.sbuf_tensor(pfx + "hTc", [128, 8 * 512], BF16))
        s["qkvstg"] = E(nc.sbuf_tensor(pfx + "qkvstg", [128, 12 * 512], BF16))
        s["xstg"] = E(nc.sbuf_tensor(pfx + "xstg", [128, 4 * 1024], BF16))
        s["ostg"] = E(nc.sbuf_tensor(pfx + "ostg", [128, 4 * 1024], BF16))
        s["otc"] = E(nc.sbuf_tensor(pfx + "otc", [128, 4 * 512], BF16))
        s["qh"] = E(nc.sbuf_tensor(pfx + "qh", [64, T], BF16))
        s["kh"] = E(nc.sbuf_tensor(pfx + "kh", [64, T], BF16))
        s["vh"] = E(nc.sbuf_tensor(pfx + "vh", [64, T], BF16))
        s["vtm"] = E(nc.sbuf_tensor(pfx + "vtm", [128, 32 * 64], BF16))
        s["qf"] = E(nc.sbuf_tensor(pfx + "qf", [64, T], BF16))
        s["kf"] = E(nc.sbuf_tensor(pfx + "kf", [64, T], BF16))
        s["oTb"] = E(nc.sbuf_tensor(pfx + "oTb", [64, T], BF16))
        s["tmpA"] = E(nc.sbuf_tensor(pfx + "tmpA", [64, 512], F32))
        s["tmpB"] = E(nc.sbuf_tensor(pfx + "tmpB", [64, 512], F32))
        s["tmpC"] = E(nc.sbuf_tensor(pfx + "tmpC", [64, 512], F32))
        s["rows"] = E(nc.sbuf_tensor(pfx + "rows", [1, 3 * 512], F32))
        s["P"] = E(nc.sbuf_tensor(pfx + "P", [128, 1024], BF16))
        s["pT"] = E(nc.sbuf_tensor(pfx + "pT", [128, 1024], BF16))
        s["otmp"] = E(nc.sbuf_tensor(pfx + "otmp", [128, 64], BF16))
        s["st"] = E(nc.sbuf_tensor(pfx + "st", [128, 8], F32))
        s["l0masks"] = E(nc.sbuf_tensor(pfx + "l0masks", [128, 4 * 1024], F32))
        s["rAq"] = E(nc.sbuf_tensor(pfx + "rAq", [64, T], BF16))
        s["rBq"] = E(nc.sbuf_tensor(pfx + "rBq", [64, T], BF16))
        s["rAk"] = E(nc.sbuf_tensor(pfx + "rAk", [64, T], BF16))
        s["rBk"] = E(nc.sbuf_tensor(pfx + "rBk", [64, T], BF16))
        s["qscol"] = E(nc.sbuf_tensor(pfx + "qscol", [64, 1], F32))
        s["kscol"] = E(nc.sbuf_tensor(pfx + "kscol", [64, 1], F32))
        self.ps = [E(nc.psum_tensor(pfx + f"ps{i}", [128, 512], F32)) for i in range(2)]
        self.ps_trL = E(nc.psum_tensor(pfx + "ps_trL", [128, 1024], BF16))
        self.ps_s = E(nc.psum_tensor(pfx + "ps_s", [128, 1024], F32))
        self.ps_o = E(nc.psum_tensor(pfx + "ps_o", [128, 64], F32))
        self.ps_n = E(nc.psum_tensor(pfx + "ps_n", [1, 512], F32))
        self.ps_b = E(nc.psum_tensor(pfx + "ps_b", [64, 512], F32))

    def emit(self, k, io, cst):
        s = self.sb
        l = self.layer
        ident = cst["ident_bf"]
        k.op("gpsimd", lambda g: g.dma_start(
            out=s["W"][:].rearrange("p (o kt m) -> p o kt m", o=1, kt=8),
            in_=io["wqkv"].rearrange("l (kt p) m -> p l kt m", p=128)[:, l:l + 1, :, :]),
            inc="g_ld", inc_n=16)
        k.op("gpsimd", lambda g: g.dma_start(
            out=s["Wo"][:].rearrange("p (o kt m) -> p o kt m", o=1, kt=4),
            in_=io["wo"].rearrange("l (kt p) m -> p l kt m", p=128)[:, l:l + 1, :, :]),
            inc="g_ld", inc_n=16)
        for nm in ("rAq", "rBq", "rAk", "rBk", "l0masks"):
            k.op("sync", lambda sy, nm=nm: sy.dma_start(out=s[nm][:], in_=io[nm]),
                 inc="m_ld", inc_n=16)
        k.op("sync", lambda sy: sy.dma_start(out=s["qscol"][:], in_=io["qs"][l:l + 1].rearrange("o p c -> (o p) c")),
             inc="m_ld", inc_n=16)
        k.op("sync", lambda sy: sy.dma_start(out=s["kscol"][:], in_=io["ks"][l:l + 1].rearrange("o p c -> (o p) c")),
             inc="m_ld", inc_n=16)
        for e in ("tensor", "vector", "scalar", "gpsimd", "sync"):
            k.wait(e, "m_ld", k.cnt["m_ld"])
            k.wait(e, "g_ld", k.cnt["g_ld"])
        zf = cst["zeros1x512f"]
        for half in range(2):
            k.op("tensor", lambda t, half=half: t.matmul(
                self.ps_s[:, half * 512:(half + 1) * 512],
                zf[0:1, 0:128], zf[0:1, 0:512], start=True, stop=True), inc="m_t")

        def qbody(ch):
            k.op("gpsimd", lambda g: g.dma_start(
                out=s["hstg"][:].rearrange("p (a d) -> p a d", a=4)[:, :, 0:512],
                in_=io["x_pad"][1:T + 1, 0:512].rearrange("(t p) d -> p t d", p=128)[:, DynSlice(ch * 4, 4), :]),
                wait=[("m_t", k.cnt["m_t"])], inc="g_x", inc_n=16)
            k.op("gpsimd", lambda g: g.dma_start(
                out=s["hstg"][:].rearrange("p (a d) -> p a d", a=4)[:, :, 512:1024],
                in_=io["x_pad"][0:T, 512:1024].rearrange("(t p) d -> p t d", p=128)[:, DynSlice(ch * 4, 4), :]),
                inc="g_x", inc_n=16)
            gx = k.cnt["g_x"]
            for dt in range(8):
                w = [("g_x", gx)] if dt == 0 else []
                w.append(("m_v", k.cnt["m_v"]))
                for tt4 in range(4):
                    k.op("tensor", lambda t, dt=dt, tt4=tt4: t.transpose(
                        self.ps_trL[:, tt4 * 128:(tt4 + 1) * 128],
                        s["hstg"][:, tt4 * 1024 + dt * 128: tt4 * 1024 + (dt + 1) * 128],
                        ident[:]),
                        wait=(w if tt4 == 0 else []),
                        inc=("m_t" if tt4 == 3 else None))
                k.op("vector", lambda v, dt=dt: v.tensor_copy(
                    s["hTc"][:, dt * 512:(dt + 1) * 512], self.ps_trL[:, 0:512]),
                    wait=[("m_t", k.cnt["m_t"])], inc="m_v")
            xt_v = k.cnt["m_v"]
            evq = k.cnt["m_v"]
            for mt in range(12):
                bank = self.ps[mt % 2]
                w = [("m_v", evq + mt - 1)] if mt >= 2 else [("m_v", xt_v)]
                if mt == 0:
                    w.append(("g_st", k.cnt["g_st"]))
                for kt in range(8):
                    k.op("tensor", lambda t, mt=mt, kt=kt, bank=bank: t.matmul(
                        bank[:, 0:512],
                        s["W"][:, kt * 1536 + mt * 128: kt * 1536 + (mt + 1) * 128],
                        s["hTc"][:, kt * 512:(kt + 1) * 512],
                        start=(kt == 0), stop=(kt == 7)),
                        wait=(w if kt == 0 else []),
                        inc=("m_t" if kt == 7 else None))
                k.op("vector", lambda v, mt=mt, bank=bank: v.tensor_copy(
                    s["qkvstg"][:, mt * 512:(mt + 1) * 512], bank[:, 0:512]),
                    wait=[("m_t", k.cnt["m_t"])], inc="m_v")
            k.op("gpsimd", lambda g: g.dma_start(
                out=self.qscr.rearrange("(mt p) t -> p mt t", p=128)[:, :, DynSlice(ch * 512, 512)],
                in_=s["qkvstg"][:].rearrange("p (mt c) -> p mt c", mt=12)),
                wait=[("m_v", k.cnt["m_v"])], inc="g_st", inc_n=16)

        k.autoloop(8, qbody)
        for e in ("tensor", "vector", "scalar", "gpsimd", "sync"):
            k.wait(e, "g_st", k.cnt["g_st"])

        def prep(k, dst, src, scol, Anm, Bnm, g):
            c0, c1 = g * 512, (g + 1) * 512
            k.op("scalar", lambda sc: sc.activation(
                s["tmpA"][0:64, :], src[0:64, c0:c1], AF.Square),
                wait=[("m_s", k.cnt["m_s"]), ("m_v", k.cnt["m_v"])], inc="m_s")
            k.op("tensor", lambda t: t.matmul(
                self.ps_n[0:1, 0:512], cst["ones128colf"][0:64, 0:1],
                s["tmpA"][0:64, :], start=True, stop=True),
                wait=[("m_s", k.cnt["m_s"])], inc="m_t")
            k.op("vector", lambda v: v.tensor_scalar(
                s["rows"][0:1, 0:512], self.ps_n[0:1, 0:512], 1e-24, None, op0=ALU.max),
                wait=[("m_t", k.cnt["m_t"]), ("m_v", k.cnt["m_v"])], inc="m_v")
            k.op("scalar", lambda sc: sc.activation(
                s["rows"][0:1, 512:1024], s["rows"][0:1, 0:512], AF.Ln),
                wait=[("m_v", k.cnt["m_v"]), ("m_s", k.cnt["m_s"])], inc="m_s")
            k.op("scalar", lambda sc: sc.activation(
                s["rows"][0:1, 1024:1536], s["rows"][0:1, 512:1024], AF.Exp, scale=-0.5),
                wait=[("m_s", k.cnt["m_s"])], inc="m_s")
            k.op("tensor", lambda t: t.matmul(
                self.ps_b[0:64, 0:512], cst["ones1x128f"][0:1, 0:64],
                s["rows"][0:1, 1024:1536], start=True, stop=True),
                wait=[("m_s", k.cnt["m_s"])], inc="m_t")
            k.op("vector", lambda v: v.scalar_tensor_tensor(
                s["tmpB"][0:64, :], src[0:64, c0:c1], scol, self.ps_b[0:64, 0:512],
                op0=ALU.mult, op1=ALU.mult),
                wait=[("m_t", k.cnt["m_t"]), ("m_v", k.cnt["m_v"])], inc="m_v")
            k.op("vector", lambda v: v.tensor_copy(s["tmpC"][0:32, :], s["tmpB"][32:64, :]),
                 wait=[("m_v", k.cnt["m_v"])], inc="m_v")
            k.op("vector", lambda v: v.tensor_copy(s["tmpC"][32:64, :], s["tmpB"][0:32, :]),
                 wait=[("m_v", k.cnt["m_v"])], inc="m_v")
            k.op("vector", lambda v: v.tensor_tensor(
                s["tmpB"][0:64, :], s["tmpB"][0:64, :], s[Anm][0:64, c0:c1], op=ALU.mult),
                wait=[("m_v", k.cnt["m_v"])], inc="m_v")
            k.op("vector", lambda v: v.tensor_tensor(
                s["tmpC"][0:64, :], s["tmpC"][0:64, :], s[Bnm][0:64, c0:c1], op=ALU.mult),
                wait=[("m_v", k.cnt["m_v"])], inc="m_v")
            k.op("vector", lambda v: v.tensor_tensor(
                dst[0:64, c0:c1], s["tmpB"][0:64, :], s["tmpC"][0:64, :], op=ALU.add),
                wait=[("m_v", k.cnt["m_v"])], inc="m_v")

        def head_body(h):
            k.op("sync", lambda sy: sy.dma_start(
                out=s["qh"][:], in_=self.qscr[DynSlice(h * 64, 64), :]),
                wait=[("m_t", k.cnt["m_t"])], inc="m_qk", inc_n=16)
            k.op("sync", lambda sy: sy.dma_start(
                out=s["kh"][:], in_=self.qscr[DynSlice(512 + h * 64, 64), :]),
                inc="m_qk", inc_n=16)
            k.op("sync", lambda sy: sy.dma_start(
                out=s["vh"][:], in_=self.qscr[DynSlice(1024 + h * 64, 64), :]),
                inc="m_qk", inc_n=16)
            ldv = k.cnt["m_qk"]
            for tt in range(32):
                w = [("m_qk", ldv)] if tt == 0 else []
                if tt % 8 == 0:
                    w.append(("m_v", k.cnt["m_v"]))
                k.op("tensor", lambda t, tt=tt: t.transpose(
                    self.ps_trL[:, (tt % 8) * 64:(tt % 8) * 64 + 64],
                    s["vh"][:, tt * 128:(tt + 1) * 128], ident[0:64, 0:64]),
                    wait=w, inc=("m_t" if tt % 8 == 7 else None))
                if tt % 8 == 7:
                    k.op("vector", lambda v, tt=tt: v.tensor_copy(
                        s["vtm"][:, (tt - 7) * 64:(tt + 1) * 64], self.ps_trL[:, 0:512]),
                        wait=[("m_t", k.cnt["m_t"])], inc="m_v")
            for g in range(8):
                prep(k, s["kf"][:], s["kh"][:], s["kscol"][0:64, 0:1], "rAk", "rBk", g)
                prep(k, s["qf"][:], s["qh"][:], s["qscol"][0:64, 0:1], "rAq", "rBq", g)
                for qt in range(4):
                    qcol = g * 512 + qt * 128
                    mm_w = [("m_s", k.cnt["m_s"]), ("m_v", k.cnt["m_v"])]
                    if g > 0:
                        k.op("tensor", lambda t, qcol=qcol, g=g: t.matmul(
                            self.ps_s[:, 0:512], s["qf"][0:64, qcol:qcol + 128],
                            s["kf"][0:64, (g - 1) * 512:g * 512], start=True, stop=True),
                            wait=mm_w, inc="m_t")
                    k.op("tensor", lambda t, qcol=qcol, g=g: t.matmul(
                        self.ps_s[:, 512:1024], s["qf"][0:64, qcol:qcol + 128],
                        s["kf"][0:64, g * 512:(g + 1) * 512], start=True, stop=True),
                        wait=([] if g > 0 else mm_w), inc="m_t")
                    k.op("vector", lambda v, qt=qt: v.tensor_tensor(
                        self.ps_s[:, 0:1024], self.ps_s[:, 0:1024],
                        s["l0masks"][:, qt * 1024:(qt + 1) * 1024], op=ALU.add),
                        wait=[("m_t", k.cnt["m_t"]), ("m_v", k.cnt["m_v"])], inc="m_v")
                    if g == 0:
                        k.op("vector", lambda v: v.tensor_scalar(
                            self.ps_s[:, 0:512], self.ps_s[:, 0:512], -1e9, None, op0=ALU.add),
                            wait=[("m_v", k.cnt["m_v"])], inc="m_v")
                    k.op("vector", lambda v: v.tensor_reduce(
                        s["st"][:, 0:1], self.ps_s[:, 0:1024], axis=AX.X, op=ALU.max, negate=True),
                        wait=[("m_v", k.cnt["m_v"])], inc="m_v")
                    k.op("scalar", lambda sc: sc.activation(
                        s["P"][:], self.ps_s[:, 0:1024], AF.Exp, bias=s["st"][:, 0:1],
                        accum_out=s["st"][:, 1:2]),
                        wait=[("m_v", k.cnt["m_v"])], inc="m_s")
                    for j in range(8):
                        w = ([("m_s", k.cnt["m_s"]), ("m_av", k.cnt["m_av"]),
                              ("m_v", k.cnt["m_v"])] if j == 0 else [])
                        k.op("tensor", lambda t, j=j: t.transpose(
                            self.ps_trL[:, j * 128:(j + 1) * 128],
                            s["P"][:, j * 128:(j + 1) * 128], ident[:]),
                            wait=w, inc=("m_t" if j == 7 else None))
                    k.op("vector", lambda v: v.tensor_copy(s["pT"][:], self.ps_trL[:, 0:1024]),
                         wait=[("m_t", k.cnt["m_t"])], inc="m_v")
                    js = list(range(8)) if g > 0 else list(range(4, 8))
                    for idx, j in enumerate(js):
                        wt = (g - 1) * 4 + j
                        k.op("tensor", lambda t, j=j, wt=wt, idx=idx, js=js: t.matmul(
                            self.ps_o[:, 0:64], s["pT"][:, j * 128:(j + 1) * 128],
                            s["vtm"][:, wt * 64:(wt + 1) * 64],
                            start=(idx == 0), stop=(idx == len(js) - 1)),
                            wait=([("m_v", k.cnt["m_v"]), ("m_s", k.cnt["m_s"])] if idx == 0 else []),
                            inc=("m_av" if idx == len(js) - 1 else None))
                    k.op("vector", lambda v: v.reciprocal(s["st"][:, 2:3], s["st"][:, 1:2]),
                         wait=[("m_av", k.cnt["m_av"]), ("m_v", k.cnt["m_v"])], inc="m_v")
                    k.op("scalar", lambda sc: sc.activation(
                        s["otmp"][:], self.ps_o[:, 0:64], AF.Identity, scale=s["st"][:, 2:3]),
                        wait=[("m_v", k.cnt["m_v"]), ("m_s", k.cnt["m_s"])], inc="m_s")
                    k.op("tensor", lambda t: t.transpose(
                        self.ps_trL[0:64, 0:128], s["otmp"][:], ident[:]),
                        wait=[("m_s", k.cnt["m_s"]), ("m_v", k.cnt["m_v"])], inc="m_t")
                    ow = [("m_t", k.cnt["m_t"])]
                    if g == 0 and qt == 0:
                        ow.append(("g_st", k.cnt["g_st"]))
                    k.op("vector", lambda v, qcol=qcol, ow=ow: v.tensor_copy(
                        s["oTb"][0:64, qcol:qcol + 128], self.ps_trL[0:64, 0:128]),
                        wait=ow, inc="m_v")
            k.op("gpsimd", lambda g_: g_.dma_start(
                out=self.oscr[DynSlice(h * 64, 64), :], in_=s["oTb"][0:64, :]),
                wait=[("m_v", k.cnt["m_v"])], inc="g_st", inc_n=16)

        k.autoloop(8, head_body)
        for e in ("tensor", "vector", "scalar", "gpsimd", "sync"):
            k.wait(e, "g_st", k.cnt["g_st"])

        def obody(ch):
            k.op("sync", lambda sy: sy.dma_start(
                out=s["otc"][:].rearrange("p (kt c) -> p kt c", kt=4),
                in_=self.oscr.rearrange("(kt p) t -> p kt t", p=128)[:, :, DynSlice(ch * 512, 512)]),
                wait=[("m_t", k.cnt["m_t"])], inc="m_qk", inc_n=16)
            k.op("gpsimd", lambda g: g.dma_start(
                out=s["xstg"][:].rearrange("p (a d) -> p a d", a=4),
                in_=io["x_pad"][1:T + 1, :].rearrange("(t p) d -> p t d", p=128)[:, DynSlice(ch * 4, 4), :]),
                wait=[("m_t", k.cnt["m_t"])], inc="g_x", inc_n=16)
            ld = k.cnt["m_qk"]
            gx = k.cnt["g_x"]
            for tt4 in range(4):
                for nch in range(2):
                    bank = self.ps[nch]
                    w0 = [("m_qk", ld), ("g_x", gx), ("m_v", k.cnt["m_v"])]
                    for kt in range(4):
                        k.op("tensor", lambda t, tt4=tt4, nch=nch, kt=kt, bank=bank: t.matmul(
                            bank[:, 0:512],
                            s["otc"][:, kt * 512 + tt4 * 128: kt * 512 + (tt4 + 1) * 128],
                            s["Wo"][:, kt * 1024 + nch * 512: kt * 1024 + (nch + 1) * 512],
                            start=(kt == 0), stop=False),
                            wait=(w0 if kt == 0 else []), inc=None)
                    k.op("tensor", lambda t, tt4=tt4, nch=nch, bank=bank: t.matmul(
                        bank[:, 0:512], ident[:],
                        s["xstg"][:, tt4 * 1024 + nch * 512: tt4 * 1024 + (nch + 1) * 512],
                        start=False, stop=True),
                        inc="m_t")
                    ew = [("m_t", k.cnt["m_t"])]
                    if tt4 == 0 and nch == 0:
                        ew.append(("g_st", k.cnt["g_st"]))
                    k.op("vector", lambda v, tt4=tt4, nch=nch, bank=bank, ew=ew: v.tensor_copy(
                        s["ostg"][:, tt4 * 1024 + nch * 512: tt4 * 1024 + (nch + 1) * 512],
                        bank[:, 0:512]),
                        wait=ew, inc="m_v")
            k.op("gpsimd", lambda g: g.dma_start(
                out=io["x_out"].rearrange("(t p) d -> p t d", p=128)[:, DynSlice(ch * 4, 4), :],
                in_=s["ostg"][:].rearrange("p (a d) -> p a d", a=4)),
                wait=[("m_v", k.cnt["m_v"])], inc="g_st", inc_n=16)

        k.autoloop(8, obody)
        k.wait("sync", "g_st", k.cnt["g_st"])


class L0Ffn:

    def __init__(self, nc, E, pfx, layer, postln):
        self.layer = layer
        self.postln = postln
        s = self.sb = {}
        s["w1p"] = E(nc.sbuf_tensor(pfx + "w1p", [128, 8 * L0NP * 256], BF16))
        s["w2p"] = E(nc.sbuf_tensor(pfx + "w2p", [128, L0NP * 1024], BF16))
        s["hstg"] = E(nc.sbuf_tensor(pfx + "hstg", [128, 4 * 1024], BF16))
        s["hln"] = E(nc.sbuf_tensor(pfx + "hln", [128, 4 * 1024], BF16))
        s["hT"] = E(nc.sbuf_tensor(pfx + "hT", [128, 8 * 512], BF16))
        s["hid"] = E(nc.sbuf_tensor(pfx + "hid", [128, L0NP * 512], BF16))
        s["xstg"] = E(nc.sbuf_tensor(pfx + "xstg", [128, 1024], BF16))
        if postln:
            s["astg"] = E(nc.sbuf_tensor(pfx + "astg", [128, 1024], BF16))
        s["ostg"] = E(nc.sbuf_tensor(pfx + "ostg", [128, 4 * 1024], BF16))
        s["stage"] = E(nc.sbuf_tensor(pfx + "stage", [128, 1024], F32))
        s["sq"] = E(nc.sbuf_tensor(pfx + "sq", [128, 1024], BF16))
        s["st"] = E(nc.sbuf_tensor(pfx + "st", [128, 12], F32))
        s["gcol"] = E(nc.sbuf_tensor(pfx + "gcol", [128, 8], F32))
        s["bcol"] = E(nc.sbuf_tensor(pfx + "bcol", [128, 8], F32))
        s["b1sb"] = E(nc.sbuf_tensor(pfx + "b1sb", [128, 2 * L0NP], F32))
        s["b2bc"] = E(nc.sbuf_tensor(pfx + "b2bc", [128, 1024], BF16))
        s["epsv"] = E(nc.sbuf_tensor(pfx + "epsv", [128, 1], F32))
        self.ps_tr = E(nc.psum_tensor(pfx + "ps_tr", [128, 512], BF16))
        self.ps_a = E(nc.psum_tensor(pfx + "ps_a", [128, 512], F32))
        self.ps_g = E(nc.psum_tensor(pfx + "ps_g", [128, 512], F32))
        self.ps_o = E(nc.psum_tensor(pfx + "ps_o", [128, 1024], F32))

    def emit_stats(self, k, src, dst, sb):
        s = sb
        k.op("vector", lambda v: v.tensor_reduce(
            s["st"][:, 0:1], src, axis=AX.X, op=ALU.add),
            wait=[("m_v", k.cnt["m_v"]), ("m_s", k.cnt["m_s"])], inc="m_v")
        k.op("scalar", lambda sc: sc.activation(
            s["sq"][:], src, AF.Square, accum_out=s["st"][:, 1:2]),
            wait=[("m_v", k.cnt["m_v"]), ("m_s", k.cnt["m_s"])], inc="m_s")
        k.op("vector", lambda v: v.tensor_scalar_mul(
            s["st"][:, 2:3], s["st"][:, 0:1], 1.0 / 1024.0),
            wait=[("m_v", k.cnt["m_v"]), ("m_s", k.cnt["m_s"])], inc="m_v")
        k.op("vector", lambda v: v.tensor_scalar_mul(
            s["st"][:, 3:4], s["st"][:, 1:2], 1.0 / 1024.0),
            wait=[("m_v", k.cnt["m_v"])], inc="m_v")
        k.op("vector", lambda v: v.tensor_tensor(
            s["st"][:, 4:5], s["st"][:, 2:3], s["st"][:, 2:3], op=ALU.mult),
            wait=[("m_v", k.cnt["m_v"])], inc="m_v")
        k.op("vector", lambda v: v.tensor_tensor(
            s["st"][:, 5:6], s["st"][:, 3:4], s["st"][:, 4:5], op=ALU.subtract),
            wait=[("m_v", k.cnt["m_v"])], inc="m_v")
        k.op("scalar", lambda sc: sc.activation(
            s["st"][:, 6:7], s["st"][:, 5:6], AF.Ln, bias=s["epsv"][:, 0:1]),
            wait=[("m_v", k.cnt["m_v"]), ("m_s", k.cnt["m_s"])], inc="m_s")
        k.op("scalar", lambda sc: sc.activation(
            s["st"][:, 7:8], s["st"][:, 6:7], AF.Exp, scale=-0.5),
            wait=[("m_s", k.cnt["m_s"])], inc="m_s")
        k.op("vector", lambda v: v.tensor_tensor(
            s["st"][:, 8:9], s["st"][:, 2:3], s["st"][:, 7:8], op=ALU.mult),
            wait=[("m_v", k.cnt["m_v"]), ("m_s", k.cnt["m_s"])], inc="m_v")
        k.op("vector", lambda v: v.tensor_scalar_mul(
            s["st"][:, 9:10], s["st"][:, 8:9], -1.0),
            wait=[("m_v", k.cnt["m_v"])], inc="m_v")
        k.op("scalar", lambda sc: sc.activation(
            dst, src, AF.Identity, scale=s["st"][:, 7:8], bias=s["st"][:, 9:10]),
            wait=[("m_v", k.cnt["m_v"]), ("m_s", k.cnt["m_s"])], inc="m_s")

    def emit(self, k, io, cst):
        s = self.sb
        l = self.layer
        ident = cst["ident_bf"]
        k.op("gpsimd", lambda g: g.dma_start(
            out=s["w1p"][:].rearrange("p (o kt m) -> p o kt m", o=1, kt=8),
            in_=io["w1p"].rearrange("l (kt p) m -> p l kt m", p=128)[:, l:l + 1, :, :]),
            inc="g_ld", inc_n=16)
        k.op("gpsimd", lambda g: g.dma_start(
            out=s["w2p"][:].rearrange("p (o pr m) -> p o pr m", o=1, pr=L0NP),
            in_=io["w2p"].rearrange("l (pr p) m -> p l pr m", p=128)[:, l:l + 1, :, :]),
            inc="g_ld", inc_n=16)
        k.op("sync", lambda sy: sy.dma_start(
            out=s["gcol"][:], in_=io["lng"][l:l + 1].rearrange("o (kt p) -> p (o kt)", p=128),
            allow_slow_non_contiguous=True),
            inc="m_ld", inc_n=16)
        k.op("sync", lambda sy: sy.dma_start(
            out=s["bcol"][:], in_=io["lnb"][l:l + 1].rearrange("o (kt p) -> p (o kt)", p=128),
            allow_slow_non_contiguous=True),
            inc="m_ld", inc_n=16)
        k.op("sync", lambda sy: sy.dma_start(
            out=s["b1sb"][:], in_=io["b1p"][l].rearrange("pr p -> p pr"),
            allow_slow_non_contiguous=True),
            inc="m_ld", inc_n=16)
        k.op("sync", lambda sy: sy.dma_start(out=s["stage"][0:1, 0:1024], in_=io["b2"][l:l + 1, :]),
             inc="m_ld", inc_n=16)
        for e in ("tensor", "vector", "scalar", "gpsimd", "sync"):
            k.wait(e, "m_ld", k.cnt["m_ld"])
            k.wait(e, "g_ld", k.cnt["g_ld"])
        k.op("vector", lambda v: v.memset(s["epsv"][:], 1e-5), inc="m_v")
        for half in range(2):
            k.op("tensor", lambda t, half=half: t.matmul(
                self.ps_o[:, half * 512:(half + 1) * 512],
                cst["ones1x128f"][0:1, :], s["stage"][0:1, half * 512:(half + 1) * 512],
                start=True, stop=True), inc="m_t")
        k.op("vector", lambda v: v.tensor_copy(s["b2bc"][:], self.ps_o[:, 0:1024]),
             wait=[("m_t", k.cnt["m_t"])], inc="m_v")

        def body(ch):
            k.op("gpsimd", lambda g: g.dma_start(
                out=s["hstg"][:].rearrange("p (a d) -> p a d", a=4)[:, :, 0:512],
                in_=io["x_pad"][1:T + 1, 0:512].rearrange("(t p) d -> p t d", p=128)[:, DynSlice(ch * 4, 4), :]),
                wait=[("m_t", k.cnt["m_t"]), ("m_s", k.cnt["m_s"])], inc="g_x", inc_n=16)
            k.op("gpsimd", lambda g: g.dma_start(
                out=s["hstg"][:].rearrange("p (a d) -> p a d", a=4)[:, :, 512:1024],
                in_=io["x_pad"][0:T, 512:1024].rearrange("(t p) d -> p t d", p=128)[:, DynSlice(ch * 4, 4), :]),
                inc="g_x", inc_n=16)
            gx = k.cnt["g_x"]
            for t4 in range(4):
                if t4 == 0:
                    k.wait("vector", "g_x", gx)
                    k.wait("scalar", "g_x", gx)
                self.emit_stats(k, s["hstg"][:, t4 * 1024:(t4 + 1) * 1024],
                                s["hln"][:, t4 * 1024:(t4 + 1) * 1024], s)
            for dt in range(8):
                for t4 in range(4):
                    w = []
                    if t4 == 0:
                        w = [("m_s", k.cnt["m_s"])]
                        if dt == 0:
                            w.append(("m_v", k.cnt["m_v"]))
                    k.op("tensor", lambda t, dt=dt, t4=t4: t.transpose(
                        self.ps_tr[:, t4 * 128:(t4 + 1) * 128],
                        s["hln"][:, t4 * 1024 + dt * 128: t4 * 1024 + (dt + 1) * 128],
                        ident[:]),
                        wait=w, inc=("m_t" if t4 == 3 else None))
                k.op("scalar", lambda sc, dt=dt: sc.activation(
                    s["hT"][:, dt * 512:(dt + 1) * 512], self.ps_tr[:, 0:512],
                    AF.Identity, scale=s["gcol"][:, dt:dt + 1], bias=s["bcol"][:, dt:dt + 1]),
                    wait=[("m_t", k.cnt["m_t"]), ("m_s", k.cnt["m_s"])], inc="m_s")
            for p in range(L0NP):
                for kt in range(8):
                    k.op("tensor", lambda t, p=p, kt=kt: t.matmul(
                        self.ps_a[:, 0:512],
                        s["w1p"][:, kt * (L0NP * 256) + (2 * p) * 128: kt * (L0NP * 256) + (2 * p) * 128 + 128],
                        s["hT"][:, kt * 512:(kt + 1) * 512],
                        start=(kt == 0), stop=(kt == 7)),
                        wait=([("m_s", k.cnt["m_s"]), ("m_v", k.cnt["m_v"])] if kt == 0 else []),
                        inc=("m_t" if kt == 7 else None))
                for kt in range(8):
                    k.op("tensor", lambda t, p=p, kt=kt: t.matmul(
                        self.ps_g[:, 0:512],
                        s["w1p"][:, kt * (L0NP * 256) + (2 * p + 1) * 128: kt * (L0NP * 256) + (2 * p + 1) * 128 + 128],
                        s["hT"][:, kt * 512:(kt + 1) * 512],
                        start=(kt == 0), stop=(kt == 7)),
                        inc=("m_t" if kt == 7 else None))
                k.op("scalar", lambda sc, p=p: sc.activation(
                    s["sq"][:, 0:512], self.ps_a[:, 0:512], AF.Identity, bias=s["b1sb"][:, 2 * p:2 * p + 1]),
                    wait=[("m_t", k.cnt["m_t"]), ("m_s", k.cnt["m_s"])], inc="m_s")
                k.op("scalar", lambda sc, p=p: sc.activation(
                    s["sq"][:, 512:1024], self.ps_g[:, 0:512], AF.Gelu, bias=s["b1sb"][:, 2 * p + 1:2 * p + 2]),
                    wait=[("m_s", k.cnt["m_s"])], inc="m_s")
                k.op("vector", lambda v, p=p: v.tensor_tensor(
                    s["hid"][:, p * 512:(p + 1) * 512], s["sq"][:, 0:512], s["sq"][:, 512:1024], op=ALU.mult),
                    wait=[("m_s", k.cnt["m_s"]), ("m_v", k.cnt["m_v"])], inc="m_v")
            hid_v = k.cnt["m_v"]
            for st4 in range(4):
                for half in range(2):
                    for p in range(L0NP):
                        w = []
                        if p == 0:
                            w = [("m_v", k.cnt["m_v"]), ("m_s", k.cnt["m_s"])]
                        k.op("tensor", lambda t, st4=st4, half=half, p=p: t.matmul(
                            self.ps_o[:, half * 512:(half + 1) * 512],
                            s["hid"][:, p * 512 + st4 * 128: p * 512 + st4 * 128 + 128],
                            s["w2p"][:, p * 1024 + half * 512: p * 1024 + (half + 1) * 512],
                            start=(p == 0), stop=(p == L0NP - 1)),
                            wait=w, inc=("m_t" if p == L0NP - 1 else None))
                mm_t = k.cnt["m_t"]
                k.op("gpsimd", lambda g, st4=st4: g.dma_start(
                    out=s["xstg"][:].rearrange("p (a d) -> p a d", a=1),
                    in_=io["x_pad"][1:T + 1, :].rearrange("(t p) d -> p t d", p=128)[:, DynSlice(ch * 4 + st4, 1), :]),
                    wait=[("m_v", k.cnt["m_v"])], inc="g_x", inc_n=16)
                gxr = k.cnt["g_x"]
                if not self.postln:
                    for half in range(2):
                        cs = slice(st4 * 1024 + half * 512, st4 * 1024 + (half + 1) * 512)
                        k.op("vector", lambda v, half=half, cs=cs: v.tensor_tensor(
                            s["stage"][:, half * 512:(half + 1) * 512],
                            self.ps_o[:, half * 512:(half + 1) * 512],
                            s["b2bc"][:, half * 512:(half + 1) * 512], op=ALU.add),
                            wait=[("m_t", mm_t), ("m_v", k.cnt["m_v"])], inc="m_v")
                        k.op("vector", lambda v, half=half, cs=cs, gxr=gxr: v.tensor_tensor(
                            s["ostg"][:, cs], s["stage"][:, half * 512:(half + 1) * 512],
                            s["xstg"][:, half * 512:(half + 1) * 512], op=ALU.add),
                            wait=([("m_v", k.cnt["m_v"]), ("g_x", gxr)] if half == 0
                                  else [("m_v", k.cnt["m_v"])]), inc="m_v")
                else:
                    k.op("gpsimd", lambda g, st4=st4: g.dma_start(
                        out=s["astg"][:].rearrange("p (a d) -> p a d", a=1),
                        in_=io["aorig"].rearrange("(t p) d -> p t d", p=128)[:, DynSlice(ch * 4 + st4, 1), :]),
                        inc="g_x", inc_n=16)
                    gxa = k.cnt["g_x"]
                    for half in range(2):
                        cs = slice(st4 * 1024 + half * 512, st4 * 1024 + (half + 1) * 512)
                        k.op("vector", lambda v, half=half, cs=cs: v.tensor_tensor(
                            s["stage"][:, half * 512:(half + 1) * 512],
                            self.ps_o[:, half * 512:(half + 1) * 512],
                            s["b2bc"][:, half * 512:(half + 1) * 512], op=ALU.add),
                            wait=[("m_t", mm_t), ("m_v", k.cnt["m_v"])], inc="m_v")
                        k.op("vector", lambda v, half=half, gxr=gxr: v.tensor_tensor(
                            s["stage"][:, half * 512:(half + 1) * 512],
                            s["stage"][:, half * 512:(half + 1) * 512],
                            s["xstg"][:, half * 512:(half + 1) * 512], op=ALU.add),
                            wait=([("m_v", k.cnt["m_v"]), ("g_x", gxr)] if half == 0
                                  else [("m_v", k.cnt["m_v"])]), inc="m_v")
                        k.op("vector", lambda v, half=half, gxa=gxa: v.tensor_tensor(
                            s["stage"][:, half * 512:(half + 1) * 512],
                            s["stage"][:, half * 512:(half + 1) * 512],
                            s["astg"][:, half * 512:(half + 1) * 512], op=ALU.add),
                            wait=([("m_v", k.cnt["m_v"]), ("g_x", gxa)] if half == 0
                                  else [("m_v", k.cnt["m_v"])]), inc="m_v")
                    self.emit_stats(k, s["stage"][:, 0:1024],
                                    s["ostg"][:, st4 * 1024:(st4 + 1) * 1024], s)
            k.op("gpsimd", lambda g: g.dma_start(
                out=io["x_out"].rearrange("(t p) d -> p t d", p=128)[:, DynSlice(ch * 4, 4), :],
                in_=s["ostg"][:].rearrange("p (a d) -> p a d", a=4)),
                wait=[("m_v", k.cnt["m_v"]), ("m_s", k.cnt["m_s"])], inc="g_st", inc_n=16)

        k.autoloop(8, body)
        k.wait("sync", "g_st", k.cnt["g_st"])


def build_l0attn_program(layer):
    from contextlib import ExitStack
    nc = bass.Bass("TRN2", target_bir_lowering=False, detect_race_conditions=False)
    x_pad = nc.dram_tensor("x", [T + 1, 1024], BF16, kind="ExternalInput").ap()
    x_out = nc.dram_tensor("y", [T, 1024], BF16, kind="ExternalOutput").ap()
    io = {"x_pad": x_pad, "x_out": x_out}
    for nm, shp, dt in L0_W:
        io[nm] = nc.dram_tensor("w_" + nm, shp, dt, kind="ExternalInput").ap()
    ctx = ExitStack()
    with ctx:
        E = ctx.enter_context
        cst_np, cst_aps, cst_sb = const_tensors(nc, E)
        blk_obj = L0Attn(nc, E, "la_", layer)
        sems = {}
        for nm in ("m_ld", "g_ld", "m_t", "m_v", "m_s", "m_qk", "m_av", "g_x", "g_st", "st"):
            sems[nm] = E(nc.semaphore(nm))
        blk = E(nc.Block())
        k = K(nc, sems, {})

        def emit_all(k):
            for nm in cst_sb:
                k.op("sync", lambda sy, nm=nm: sy.dma_start(out=cst_sb[nm][:], in_=cst_aps[nm]),
                     inc="m_ld", inc_n=16)
            cst = {nm: cst_sb[nm][:] for nm in cst_sb}
            blk_obj.emit(k, io, cst)

        def section(name):
            def f(eng):
                k.run(name, eng, [emit_all])
            return f
        blk.sync(section("sync"))
        blk.gpsimd(section("gpsimd"))
        blk.tensor(section("tensor"))
        blk.vector(section("vector"))
        blk.scalar(section("scalar"))
    return nc, cst_np


def build_l0ffn_program(layer, postln):
    from contextlib import ExitStack
    nc = bass.Bass("TRN2", target_bir_lowering=False, detect_race_conditions=False)
    x_pad = nc.dram_tensor("x", [T + 1, 1024], BF16, kind="ExternalInput").ap()
    x_out = nc.dram_tensor("y", [T, 1024], BF16, kind="ExternalOutput").ap()
    io = {"x_pad": x_pad, "x_out": x_out}
    if postln:
        io["aorig"] = nc.dram_tensor("aorig", [T, 1024], BF16, kind="ExternalInput").ap()
    for nm, shp, dt in L0_W:
        io[nm] = nc.dram_tensor("w_" + nm, shp, dt, kind="ExternalInput").ap()
    ctx = ExitStack()
    with ctx:
        E = ctx.enter_context
        cst_np, cst_aps, cst_sb = const_tensors(nc, E, only=("ident_bf", "ones1x128f"))
        blk_obj = L0Ffn(nc, E, "lf_", layer, postln)
        sems = {}
        for nm in ("m_ld", "g_ld", "m_t", "m_v", "m_s", "g_x", "g_st", "st"):
            sems[nm] = E(nc.semaphore(nm))
        blk = E(nc.Block())
        k = K(nc, sems, {})

        def emit_all(k):
            for nm in cst_sb:
                k.op("sync", lambda sy, nm=nm: sy.dma_start(out=cst_sb[nm][:], in_=cst_aps[nm]),
                     inc="m_ld", inc_n=16)
            cst = {nm: cst_sb[nm][:] for nm in cst_sb}
            blk_obj.emit(k, io, cst)

        def section(name):
            def f(eng):
                k.run(name, eng, [emit_all])
            return f
        blk.sync(section("sync"))
        blk.gpsimd(section("gpsimd"))
        blk.tensor(section("tensor"))
        blk.vector(section("vector"))
        blk.scalar(section("scalar"))
    return nc, cst_np


def _l0_run_one(build_fn, args, x_pad, io_np, extra, sim):
    nc, cst_np = build_fn(*args)
    in_map = {"x": x_pad}
    in_map.update(extra)
    for nm, arr in io_np.items():
        in_map["w_" + nm] = arr
    for nm, arr in cst_np.items():
        in_map["c_" + nm] = arr
    if sim:
        from concourse.bass_interp import MultiCoreSim
        simu = MultiCoreSim(nc, num_cores=1, num_workers=1, trace=False)
        for kk, v in in_map.items():
            simu.cores[0].tensor(kk)[:] = v
        simu.simulate(check_with_hw=False)
        return np.asarray(simu.cores[0].tensor("y"))
    res = run_bass_kernel_spmd(nc, [in_map], core_ids=[0])
    return np.asarray(res.results[0]["y"])


def _pad_stream(x):
    return np.ascontiguousarray(
        np.concatenate([np.zeros((1, 1024), BF16NP), x.astype(BF16NP)], axis=0))


def run_l0_test(x_bf16, io_np, sim=False, upto=4):
    a0 = np.ascontiguousarray(x_bf16)
    x = a0
    outs = []
    steps = [(build_l0attn_program, (0,), {}),
             (build_l0ffn_program, (0, False), {}),
             (build_l0attn_program, (1,), {}),
             (build_l0ffn_program, (1, True), {"aorig": a0})]
    for i, (bf, args, extra) in enumerate(steps[:upto]):
        x = _l0_run_one(bf, args, _pad_stream(x), io_np, extra, sim)
        outs.append(x)
    return outs


_BUILT = {}


def _get_prog(key, builder, args):
    if key not in _BUILT:
        _BUILT[key] = builder(*args)
    return _BUILT[key]


def _launch(progkey, builder, bargs, wmap, xs_list, extra_list=None):
    nc, cst_np = _get_prog(progkey, builder, bargs)
    in_maps = []
    for b in range(8):
        m = dict(wmap)
        m["x"] = xs_list[b]
        if extra_list is not None:
            m.update(extra_list[b])
        for nm, arr in cst_np.items():
            m["c_" + nm] = arr
        in_maps.append(m)
    res = run_bass_kernel_spmd(nc, in_maps, core_ids=list(range(8)))
    return [np.asarray(res.results[b]["y"]) for b in range(8)]


def kernel_multi(**inputs):
    _patch_fast_compile()
    B = 8
    x0 = np.asarray(inputs["L0_attn_x"])
    wm = {"w_" + nm: arr for nm, arr in mamba_host_prep(inputs).items()}
    w0 = {"w_" + nm: arr for nm, arr in l0_host_prep(inputs).items()}
    wa1 = {"w_" + nm: arr for nm, arr in attn_host_prep(inputs, "L1").items()}
    wf1 = {"w_" + nm: arr for nm, arr in l1ffn_host_prep(inputs).items()}
    wa2 = {"w_" + nm: arr for nm, arr in attn_host_prep(inputs, "L2").items()}
    wf2 = {"w_" + nm: arr for nm, arr in l23ffn_host_prep(inputs, "L2").items()}
    wa3 = {"w_" + nm: arr for nm, arr in attn_host_prep(inputs, "L3").items()}
    wf3 = {"w_" + nm: arr for nm, arr in l23ffn_host_prep(inputs, "L3").items()}
    xs = [np.ascontiguousarray(x0[b]) for b in range(B)]
    xs = _launch("mamba", build_mamba_program, (T,), wm, xs)
    a0 = xs
    xs = _launch("l0a0", build_l0attn_program, (0,), w0, [_pad_stream(x) for x in xs])
    xs = _launch("l0f0", build_l0ffn_program, (0, False), w0, [_pad_stream(x) for x in xs])
    xs = _launch("l0a1", build_l0attn_program, (1,), w0, [_pad_stream(x) for x in xs])
    xs = _launch("l0f1", build_l0ffn_program, (1, True), w0,
                 [_pad_stream(x) for x in xs], [{"aorig": a0[b]} for b in range(B)])
    xs = _launch("a1", build_attn_program, ("L1",), wa1, xs)
    xs = _launch("f1", build_l1ffn_program, (), wf1, xs)
    xs = _launch("a2", build_attn_program, ("L2",), wa2, xs)
    xs = _launch("f2", build_l23ffn_program, ("L2",), wf2, xs)
    xs = _launch("a3", build_attn_program, ("L3",), wa3, xs)
    xs = _launch("f3", build_l23ffn_program, ("L3", True), wf3, xs)
    return np.stack(xs, axis=0)


MAMBA_SHP = {"Win": [1024, 2048], "Wdt": [1024, 1024], "Wxp": [1024, 32],
             "Wout": [1024, 1024], "A_log": [1024, 16], "wconv": [128, 32],
             "bconv": [128, 8], "bdt": [128, 8], "gamma": [128, 8],
             "beta": [128, 8], "bout": [1, 1024]}

FUSED_SEMS = ("m_ld", "g_ld", "m_wl", "m_t", "m_v", "m_s", "m_e", "st",
              "g_st", "m_sl", "m_qk", "m_av", "g_x", "g_w")
FUSED_UNROLL = {0, 1, 2, 5, 6, 7, 8, 9, 10}


def build_fused_program():
    from contextlib import ExitStack
    nc = bass.Bass("TRN2", target_bir_lowering=False, detect_race_conditions=False)
    x_in = nc.dram_tensor("x", [T, 1024], F32, kind="ExternalInput").ap()
    y_out = nc.dram_tensor("y", [T, 1024], F32, kind="ExternalOutput").ap()
    S = {}
    for nm in ("s1", "s2", "s3", "s4"):
        S[nm] = nc.dram_tensor(nm, [T + 1, 1024], BF16, kind="Internal").ap()
    for nm in ("s5", "s6", "s7", "s8", "s9"):
        S[nm] = nc.dram_tensor(nm, [T, 1024], BF16, kind="Internal").ap()
    def wgrp(ns, spec):
        return {nm: nc.dram_tensor(f"w_{ns}_{nm}", shp, dt, kind="ExternalInput").ap()
                for nm, shp, dt in spec}
    io_m = {nm: nc.dram_tensor("w_m_" + nm, MAMBA_SHP[nm], F32, kind="ExternalInput").ap()
            for nm in MAMBA_W_NAMES}
    io_l0 = wgrp("l0", L0_W)
    io_a1 = wgrp("a1", ATTN_W["L1"])
    io_f1 = wgrp("f1", L1FFN_W)
    io_a2 = wgrp("a2", ATTN_W["L2"])
    io_f2 = wgrp("f2", L23FFN_W["L2"])
    io_a3 = wgrp("a3", ATTN_W["L3"])
    io_f3 = wgrp("f3", L23FFN_W["L3"])

    ctx = ExitStack()
    with ctx:
        E = ctx.enter_context
        cst_np, cst_aps, cst_sb = const_tensors(nc, E)
        sems = {nm: E(nc.semaphore(nm)) for nm in FUSED_SEMS}
        blk = E(nc.Block())
        k = K(nc, sems, {})

        def barrier(k):
            for e in K.ENGINES:
                for sm in FUSED_SEMS:
                    k.wait(e, sm, k.cnt[sm])

        stages = []

        def mk_mamba(E2):
            x_tm = E2(nc.sbuf_tensor("m_x_tm", [128, (T // 128) * 1024], BF16))
            mam = Mamba(nc, E2, "m_", T)

            def em(k, cst):
                k.op("gpsimd", lambda g: g.dma_start(
                    out=x_tm[:].rearrange("p (t d) -> p t d", t=T // 128),
                    in_=x_in.rearrange("(t p) d -> p t d", p=128)), inc="g_ld", inc_n=16)
                mam.emit(k, io_m, cst, x_tm[:])
                k.op("sync", lambda sy: sy.dma_start(
                    out=S["s1"][1:T + 1, :].rearrange("(t p) d -> p t d", p=128),
                    in_=x_tm[:].rearrange("p (t d) -> p t d", t=T // 128)),
                    wait=[("m_s", k.cnt["m_s"]), ("m_v", k.cnt["m_v"])],
                    inc="st", inc_n=16)
                k.wait("sync", "st", k.cnt["st"])
            return em
        stages.append(mk_mamba)

        def mk_l0a(layer, xi, xo):
            def f(E2, layer=layer, xi=xi, xo=xo):
                obj = L0Attn(nc, E2, f"la{layer}_", layer)
                io = dict(io_l0)
                io["x_pad"] = S[xi]
                io["x_out"] = S[xo][1:T + 1, :]
                return lambda k, cst: obj.emit(k, io, cst)
            return f

        def mk_l0f(layer, xi, xo, postln):
            def f(E2, layer=layer, xi=xi, xo=xo, postln=postln):
                obj = L0Ffn(nc, E2, f"lf{layer}_", layer, postln)
                io = dict(io_l0)
                io["x_pad"] = S[xi]
                io["x_out"] = (S[xo][1:T + 1, :] if xo in ("s2", "s3", "s4") else S[xo])
                if postln:
                    io["aorig"] = S["s1"][1:T + 1, :]
                return lambda k, cst: obj.emit(k, io, cst)
            return f
        stages.append(mk_l0a(0, "s1", "s2"))
        stages.append(mk_l0f(0, "s2", "s3", False))
        stages.append(mk_l0a(1, "s3", "s4"))
        stages.append(mk_l0f(1, "s4", "s5", True))

        def mk_attn(mode, pfx, iow, xi, xo):
            def f(E2, mode=mode, pfx=pfx, iow=iow, xi=xi, xo=xo):
                obj = Attn(nc, E2, pfx, mode)
                io = dict(iow)
                io["x_in"] = S[xi]
                io["x_out"] = S[xo]
                return lambda k, cst: obj.emit(k, io, cst)
            return f

        def mk_l1f(E2):
            obj = L1FFN(nc, E2, "f1_")
            io = dict(io_f1)
            io["x_in"] = S["s6"]
            io["x_out"] = S["s7"]
            return lambda k, cst: obj.emit(k, io, cst)

        def mk_l23(mode, pfx, iow, xi, xo, final):
            def f(E2, mode=mode, pfx=pfx, iow=iow, xi=xi, xo=xo, final=final):
                obj = L23FFN(nc, E2, pfx, mode, final_f32=final)
                io = dict(iow)
                io["x_in"] = S[xi]
                io["x_out"] = (y_out if final else S[xo])
                return lambda k, cst: obj.emit(k, io, cst)
            return f
        stages.append(mk_attn("L1", "a1_", io_a1, "s5", "s6"))
        stages.append(mk_l1f)
        stages.append(mk_attn("L2", "a2_", io_a2, "s7", "s8"))
        stages.append(mk_l23("L2", "f2_", io_f2, "s8", "s9", False))
        stages.append(mk_attn("L3", "a3_", io_a3, "s9", "s5"))
        stages.append(mk_l23("L3", "f3_", io_f3, "s5", None, True))

        emitters = []
        for st_fn in stages:
            with ExitStack() as c2:
                emitters.append(st_fn(c2.enter_context))

        def emit_all(k):
            for nm in cst_sb:
                k.op("sync", lambda sy, nm=nm: sy.dma_start(out=cst_sb[nm][:], in_=cst_aps[nm]),
                     inc="m_ld", inc_n=16)
            for nm in ("s1", "s2", "s3", "s4"):
                for half in range(2):
                    k.op("sync", lambda sy, nm=nm, half=half: sy.dma_start(
                        out=S[nm][0:1, half * 512:(half + 1) * 512],
                        in_=cst_sb["zeros1x512"][0:1, :]),
                        wait=[("m_ld", k.cnt["m_ld"])], inc="m_ld", inc_n=16)
            cst = {nm: cst_sb[nm][:] for nm in cst_sb}
            cst["sel32_hbm"] = cst_aps["sel32"]
            for i, em in enumerate(emitters):
                k.unroll = i in FUSED_UNROLL
                em(k, cst)
                k.unroll = False
                barrier(k)

        def section(name):
            def f(eng):
                k.run(name, eng, [emit_all])
            return f
        blk.sync(section("sync"))
        blk.gpsimd(section("gpsimd"))
        blk.tensor(section("tensor"))
        blk.vector(section("vector"))
        blk.scalar(section("scalar"))
    return nc, cst_np


def kernel(**inputs):
    _patch_fast_compile()
    x0 = np.asarray(inputs["L0_attn_x"])
    preps = {
        "m": mamba_host_prep(inputs),
        "l0": l0_host_prep(inputs),
        "a1": attn_host_prep(inputs, "L1"),
        "f1": l1ffn_host_prep(inputs),
        "a2": attn_host_prep(inputs, "L2"),
        "f2": l23ffn_host_prep(inputs, "L2"),
        "a3": attn_host_prep(inputs, "L3"),
        "f3": l23ffn_host_prep(inputs, "L3"),
    }
    nc, cst_np = _get_prog("fused", build_fused_program, ())
    base = {}
    for ns, d in preps.items():
        for nm, arr in d.items():
            base[f"w_{ns}_{nm}"] = arr
    for nm, arr in cst_np.items():
        base["c_" + nm] = arr
    in_maps = []
    for b in range(8):
        m = dict(base)
        m["x"] = np.ascontiguousarray(x0[b])
        in_maps.append(m)
    res = run_bass_kernel_spmd(nc, in_maps, core_ids=list(range(8)))
    return np.stack([np.asarray(res.results[b]["y"]) for b in range(8)], axis=0)
